# version 14
# baseline (speedup 1.0000x reference)
import math
from contextlib import ExitStack

import numpy as np
import concourse.bass as bass
import concourse.mybir as mybir
from concourse.bass_utils import run_bass_kernel_spmd

F32 = mybir.dt.float32
BF16 = mybir.dt.bfloat16
AF = mybir.ActivationFunctionType
ALU = mybir.AluOpType
AX = mybir.AxisListType

S, D = 4096, 2048
NH, NKV, HD = 16, 4, 128
KVF = NKV * HD
EXP, DEXP = 8, 2048
H1 = 5632
NC_ = D // 128
NOC1 = H1 // 128
SCALE0 = float(HD) ** -0.5
GELU_C2 = 2.0 * math.sqrt(2.0 / math.pi)
GELU_A = 0.044715
LN_EPS = 1e-5
Q_EPS = 1e-8

DEBUG_DUMPS = ()
PHASES = ("pt", "p2", "p3a", "p3b", "p3c", "p4", "p5")


def _consts():
    ident = np.eye(128, dtype=np.float32)
    r = np.zeros((128, 128), np.float32)
    for d in range(64):
        r[d, d + 64] = -1.0
        r[64 + d, d] = 1.0
    rT = np.ascontiguousarray(r.T)
    selT = np.zeros((128, 128), np.float32)
    for p in range(128):
        selT[64 + (p % 64), p] = 1.0
    sel8 = np.zeros((8, 8 * 128), np.float32)
    for e in range(8):
        sel8[e, e * 128:(e + 1) * 128] = 1.0
    j = np.arange(64, dtype=np.float64)
    inv_freq = 10000.0 ** (-2.0 * j / 128.0)
    t = np.arange(S, dtype=np.float64)
    freqs = np.outer(t, inv_freq)
    emb = np.concatenate([freqs, freqs], 1)
    cos_f = np.ascontiguousarray(np.cos(emb).T.astype(np.float32))
    sin_f = np.ascontiguousarray(np.sin(emb).T.astype(np.float32))
    return ident, rT, selT, sel8, cos_f, sin_f


class Prog:

    ENGS = ("sync", "gpsimd", "vector", "scalar", "tensor")
    SKIP_OK = ("tensor", "sync", "gpsimd")

    def __init__(self, nc, block, tick, sems):
        self.nc, self.block, self.tick = nc, block, tick
        self.sems = sems
        self.cnt = {n: 0 for n in sems}
        self.k = 0
        self.pending = []

    def step(self, eng, fn, after=None):
        idx = self.k
        self.k += 1
        prev_eng = self.pending[-1][0] if self.pending else None
        if after is None:
            wait = None if (prev_eng == eng and eng in self.SKIP_OK) else idx
        else:
            wait = after
        waits = [] if wait is None else [("TICK", wait)]
        self.pending.append((eng, fn, waits, "TICK"))
        return idx

    def raw(self, eng, fn, waits=(), inc=None):
        if inc is not None:
            self.cnt[inc] += 1
        self.pending.append((eng, fn, list(waits), inc))
        return self.cnt[inc] if inc is not None else None

    def barrier(self, names, eng="sync"):
        waits = [("TICK", self.k)] + [(n, self.cnt[n]) for n in names]
        idx = self.k
        self.k += 1
        self.pending.append((eng, lambda e: e.nop(), waits, "TICK"))
        return idx

    def flush(self):
        pend, self.pending = self.pending, []
        if not pend:
            return
        for name in self.ENGS:
            sel = [(fn, ws, inc) for (e, fn, ws, inc) in pend if e == name]
            if not sel:
                continue

            def body(eng, sel=sel):
                for fn, ws, inc in sel:
                    for sn, cv in ws:
                        sem = self.tick if sn == "TICK" else self.sems[sn]
                        eng.wait_ge(sem, 16 * cv)
                    inst = fn(eng)
                    if inc == "TICK":
                        inst.then_inc(self.tick, 16)
                    elif inc is not None:
                        inst.then_inc(self.sems[inc], 16)

            getattr(self.block, name)(body)


def build_program(phases=PHASES, dumps=DEBUG_DUMPS):
    nc = bass.Bass("TRN2", target_bir_lowering=False, debug=False)
    ident_np, rT_np, selT_np, sel8_np, cos_np, sin_np = _consts()

    xt_d = nc.dram_tensor("xt", [D, S], F32, kind="ExternalInput").ap()
    wqT_d = nc.dram_tensor("wqT", [16, 128, 16 * 128], F32, kind="ExternalInput").ap()
    wkT_d = nc.dram_tensor("wkT", [4, 128, 16 * 128], F32, kind="ExternalInput").ap()
    wvT_d = nc.dram_tensor("wvT", [4, 128, 16 * 128], F32, kind="ExternalInput").ap()
    woT_d = nc.dram_tensor("woT", [16, 128, 16 * 128], F32, kind="ExternalInput").ap()
    rw_d = nc.dram_tensor("rw", [128, 16 * EXP], F32, kind="ExternalInput").ap()
    rb_d = nc.dram_tensor("rb", [EXP], F32, kind="ExternalInput").ap()
    ew_d = nc.dram_tensor("ew", [EXP, 16, 128, 16 * 128], F32, kind="ExternalInput").ap()
    eb_d = nc.dram_tensor("eb", [EXP, DEXP], F32, kind="ExternalInput").ap()
    ow_d = nc.dram_tensor("ow", [16, 128, 16 * 128], F32, kind="ExternalInput").ap()
    ob_d = nc.dram_tensor("ob", [D], F32, kind="ExternalInput").ap()
    wkv_d = nc.dram_tensor("wkv", [4, 128, 16 * 128], F32, kind="ExternalInput").ap()
    bkv_d = nc.dram_tensor("bkv", [KVF], F32, kind="ExternalInput").ap()
    nc.dram_tensor("wq1", [D, D], F32, kind="ExternalInput")
    nc.dram_tensor("bq1", [D], F32, kind="ExternalInput")
    w1T_d = nc.dram_tensor("w1T", [NOC1, 128, 16 * 128], F32, kind="ExternalInput").ap()
    w2T_d = nc.dram_tensor("w2T", [NOC1, 128, 16 * 128], F32, kind="ExternalInput").ap()
    w3T_d = nc.dram_tensor("w3T", [16, 128, NOC1 * 128], F32, kind="ExternalInput").ap()
    out_d = nc.dram_tensor("out", [D, S], F32, kind="ExternalOutput").ap()

    ident_d = nc.inline_tensor(ident_np, "c_ident").ap()
    rT_d = nc.inline_tensor(rT_np, "c_rT").ap()
    selT_d = nc.inline_tensor(selT_np, "c_selT").ap()
    sel8_d = nc.inline_tensor(sel8_np, "c_sel8").ap()
    cos_d = nc.inline_tensor(cos_np, "c_cos").ap()
    sin_d = nc.inline_tensor(sin_np, "c_sin").ap()

    q1b_d = nc.dram_tensor("q1b", [NOC1, 128, 16 * 128], BF16).ap()
    q2b_d = nc.dram_tensor("q2b", [NOC1, 128, 16 * 128], BF16).ap()
    q3b_d = nc.dram_tensor("q3b", [16, 128, NOC1 * 128], BF16).ap()
    xnh_d = nc.dram_tensor("xnh", [D, S], BF16).ap()
    ewh_d = nc.dram_tensor("ewh", [EXP, 16, 128, 16 * 128], BF16).ap()
    owh_d = nc.dram_tensor("owh", [16, 128, 16 * 128], BF16).ap()
    kth_d = nc.dram_tensor("kth", [4, 128, 32, 128], BF16).ap()
    vta_d = nc.dram_tensor("vta", [4, 128, 32, 136], BF16).ap()
    qfh_d = nc.dram_tensor("qfh", [D, S], BF16).ap()
    aof_d = nc.dram_tensor("aof", [D, S], BF16).ap()
    sf_d = nc.dram_tensor("sf", [D, S], F32).ap()
    y1f_d = nc.dram_tensor("y1f", [D, S], F32).ap()
    gb_d = nc.dram_tensor("gbscr", [8, 512], F32).ap()
    dump_specs = {"s": [D, S], "y1": [D, S], "stats": [1, 64]}
    dump_aps = {n: nc.dram_tensor("dbg_" + n, shp, F32, kind="ExternalOutput").ap()
                for n, shp in dump_specs.items() if n in dumps}

    with ExitStack() as top:
        sbp = lambda n, s, dt=F32: top.enter_context(nc.sbuf_tensor(n, s, dt))
        s_ident = sbp("c_ident_s", [128, 128])
        s_identb = sbp("c_identb_s", [128, 128], BF16)
        s_rT = sbp("c_rT_s", [128, 128])
        s_selT = sbp("c_selT_s", [128, 128])
        s_sel8 = sbp("c_sel8_s", [8, 8 * 128])
        s_ones = sbp("c_ones_s", [128, 128])
        s_onesb = sbp("c_onesb_s", [128, 128], BF16)
        s_stats = sbp("stats_s", [128, 64])
        s_nbias = sbp("nbias_s", [128, 16])
        s_rgam = sbp("rgam_s", [128, 4])
        tick = top.enter_context(nc.semaphore("tick"))
        sems = {n: top.enter_context(nc.semaphore(n))
                for n in ("sMM", "sE2", "sGB", "sD0", "sD1", "sLD")}
        block = top.enter_context(nc.Block())
        pr = Prog(nc, block, tick, sems)
        ST = pr.step

        ST("sync", lambda e: e.dma_start(out=s_ident[:], in_=ident_d))
        ST("sync", lambda e: e.dma_start(out=s_rT[:], in_=rT_d))
        ST("sync", lambda e: e.dma_start(out=s_selT[:], in_=selT_d))
        ST("sync", lambda e: e.dma_start(out=s_sel8[:], in_=sel8_d))
        ST("vector", lambda e: e.memset(s_ones[:], 1.0))
        ST("vector", lambda e: e.memset(s_onesb[:], 1.0))
        ST("vector", lambda e: e.memset(s_stats[:], 0.0))
        ST("vector", lambda e: e.tensor_copy(s_identb[:], s_ident[:]))

        def bcast_row(p_bc, row_ap, width):
            ST("tensor", lambda e: e.matmul(p_bc[:, 0:width], s_ones[0:1, :],
                                            row_ap, start=True, stop=True))

        def emit_layernorm(sx, sxn, p_row, p_bc, s_sq, s_st, tw):
            mu = s_st[0:1, 0:tw]
            msq = s_st[0:1, tw:2 * tw]
            t1 = s_st[0:1, 2 * tw:3 * tw]
            t2 = s_st[0:1, 3 * tw:4 * tw]
            for c in range(NC_):
                ST("tensor", lambda e, c=c: e.matmul(
                    p_row[0:1, 0:tw], s_ones[:, 0:1], sx[c][:, 0:tw],
                    start=(c == 0), stop=(c == NC_ - 1)))
            ST("vector", lambda e: e.tensor_scalar(mu, p_row[0:1, 0:tw], 1.0 / D, None, ALU.mult))
            for c in range(NC_):
                ST("scalar", lambda e, c=c: e.activation(s_sq[:, 0:tw], sx[c][:, 0:tw], AF.Square))
                ST("tensor", lambda e, c=c: e.matmul(
                    p_row[0:1, 0:tw], s_ones[:, 0:1], s_sq[:, 0:tw],
                    start=(c == 0), stop=(c == NC_ - 1)))
            ST("vector", lambda e: e.tensor_scalar(msq, p_row[0:1, 0:tw], 1.0 / D, None, ALU.mult))
            ST("vector", lambda e: e.tensor_tensor(t1, mu, mu, ALU.mult))
            ST("vector", lambda e: e.tensor_tensor(t2, msq, t1, ALU.subtract))
            ST("vector", lambda e: e.tensor_scalar(t2, t2, LN_EPS, None, ALU.add))
            ST("scalar", lambda e: e.activation(t1, t2, AF.Sqrt))
            ST("vector", lambda e: e.reciprocal(t2, t1))
            bcast_row(p_bc, mu, tw)
            ST("scalar", lambda e: e.activation(s_sq[:, 0:tw], p_bc[:, 0:tw], AF.Identity))
            bcast_row(p_bc, t2, tw)
            for c in range(NC_):
                ST("vector", lambda e, c=c: e.tensor_tensor(
                    sxn[c][:, 0:tw], sx[c][:, 0:tw], s_sq[:, 0:tw], ALU.subtract))
                ST("vector", lambda e, c=c: e.tensor_tensor(
                    sxn[c][:, 0:tw], sxn[c][:, 0:tw], p_bc[:, 0:tw], ALU.mult))

        def phase_pt():
            with ExitStack() as es:
                sb = lambda n, s, dt=F32: es.enter_context(nc.sbuf_tensor(n, s, dt))
                ps = lambda n, s: es.enter_context(nc.psum_tensor(n, s, F32))
                s_w = sb("pt_w", [128, NOC1 * 128])
                s_acc = sb("pt_acc", [128, 4])
                s_sc = sb("pt_sc", [4, 4])
                p_sc = ps("pt_sc_p", [128, 4])
                for m, w_ap in enumerate((w1T_d, w2T_d, w3T_d)):
                    nb = w_ap.shape[0]
                    for rb in range(nb):
                        ST("sync", lambda e, w_ap=w_ap, rb=rb: e.dma_start(
                            out=s_w[:, 0:w_ap.shape[2]], in_=w_ap[rb]))
                        ST("vector", lambda e, w_ap=w_ap: e.reduce_sum(
                            s_acc[:, 1:2], s_w[:, 0:w_ap.shape[2]], axis=AX.X,
                            apply_absolute_value=True))
                        if rb == 0:
                            ST("vector", lambda e: e.tensor_copy(s_acc[:, 0:1], s_acc[:, 1:2]))
                        else:
                            ST("vector", lambda e: e.tensor_add(
                                s_acc[:, 0:1], s_acc[:, 0:1], s_acc[:, 1:2]))
                    ST("tensor", lambda e: e.matmul(p_sc[0:1, 0:1], s_ones[:, 0:1],
                                                    s_acc[:, 0:1], start=True, stop=True))
                    ST("vector", lambda e: e.tensor_scalar(
                        s_sc[0:1, 0:1], p_sc[0:1, 0:1], 1.0 / (H1 * D), None, ALU.mult))
                    ST("vector", lambda e: e.tensor_scalar(
                        s_sc[0:1, 0:1], s_sc[0:1, 0:1], Q_EPS, None, ALU.add))
                    ST("vector", lambda e: e.reciprocal(s_sc[0:1, 1:2], s_sc[0:1, 0:1]))
                    bcast_row(p_sc, s_sc[0:1, 1:2], 1)
                    ST("scalar", lambda e, m=m: e.activation(
                        s_rgam[:, m:m + 1], p_sc[:, 0:1], AF.Identity))
                pr.flush()
            with ExitStack() as es:
                sb = lambda n, s, dt=F32: es.enter_context(nc.sbuf_tensor(n, s, dt))
                s_w = sb("ptq_w", [128, NOC1 * 128])
                s_a = sb("ptq_a", [128, NOC1 * 128])
                s_l = sb("ptq_l", [128, NOC1 * 128], BF16)
                s_g = sb("ptq_g", [128, NOC1 * 128], BF16)
                for m, (w_ap, q_ap) in enumerate(((w1T_d, q1b_d), (w2T_d, q2b_d),
                                                  (w3T_d, q3b_d))):
                    nb, _, wid = w_ap.shape
                    for rb in range(nb):
                        ST("sync", lambda e, w_ap=w_ap, rb=rb, wid=wid: e.dma_start(
                            out=s_w[:, 0:wid], in_=w_ap[rb]))
                        ST("vector", lambda e, m=m, wid=wid: e.tensor_scalar(
                            s_a[:, 0:wid], s_w[:, 0:wid], s_rgam[:, m:m + 1], None, ALU.mult))
                        ST("vector", lambda e, wid=wid: e.tensor_scalar(
                            s_l[:, 0:wid], s_a[:, 0:wid], -0.5, None, ALU.is_lt))
                        ST("vector", lambda e, wid=wid: e.tensor_scalar(
                            s_g[:, 0:wid], s_a[:, 0:wid], 0.5, None, ALU.is_gt))
                        ST("vector", lambda e, wid=wid: e.tensor_tensor(
                            s_g[:, 0:wid], s_g[:, 0:wid], s_l[:, 0:wid], ALU.subtract))
                        ST("gpsimd", lambda e, q_ap=q_ap, rb=rb, wid=wid: e.dma_start(
                            out=q_ap[rb], in_=s_g[:, 0:wid]))
                splits = [(ow_d[oc], owh_d[oc]) for oc in range(16)]
                splits += [(ew_d[ex, dc], ewh_d[ex, dc])
                           for ex in range(EXP) for dc in range(16)]
                for (src, hi) in splits:
                    ST("sync", lambda e, src=src: e.dma_start(out=s_w[:, 0:2048], in_=src))
                    ST("vector", lambda e: e.tensor_copy(s_l[:, 0:2048], s_w[:, 0:2048]))
                    ST("gpsimd", lambda e, hi=hi: e.dma_start(out=hi, in_=s_l[:, 0:2048]))
                pr.flush()

        def phase_p2():
            TW = 512
            with ExitStack() as es:
                sb = lambda n, s, dt=F32: es.enter_context(nc.sbuf_tensor(n, s, dt))
                ps = lambda n, s, dt=F32: es.enter_context(nc.psum_tensor(n, s, dt))
                s_xf = [sb(f"p2_xf{c}", [128, TW]) for c in range(NC_)]
                s_xn = [sb(f"p2_xn{c}", [128, TW]) for c in range(NC_)]
                s_xnh = [sb(f"p2_xnh{c}", [128, TW], BF16) for c in range(NC_)]
                s_sq = sb("p2_sq", [128, TW])
                s_st = sb("p2_st", [1, 4 * TW + 8])
                s_wst = sb("p2_wst", [128, 2048])
                s_wkb = sb("p2_wkb", [128, 4 * 16 * 128], BF16)
                s_wvb = sb("p2_wvb", [128, 4 * 16 * 128], BF16)
                s_k = sb("p2_k", [128, TW])
                s_kr = sb("p2_kr", [128, TW])
                s_khb = sb("p2_khb", [128, TW], BF16)
                s_t1 = sb("p2_t1", [128, TW])
                s_vb16 = sb("p2_vb16", [128, TW], BF16)
                s_vt = sb("p2_vt", [128, 4 * 136], BF16)
                s_cos = sb("p2_cos", [128, S])
                s_sin = sb("p2_sin", [128, S])
                p_trb = ps("p2_trb", [128, 128], BF16)
                p_row = ps("p2_row", [1, TW])
                p_bc = ps("p2_bc", [128, TW])
                p_kv = ps("p2_kv", [128, TW])
                p_rot = ps("p2_rot", [128, TW])

                ST("sync", lambda e: e.dma_start(out=s_cos[:], in_=cos_d))
                ST("sync", lambda e: e.dma_start(out=s_sin[:], in_=sin_d))
                for g in range(4):
                    ST("sync", lambda e, g=g: e.dma_start(out=s_wst[:], in_=wkT_d[g]))
                    ST("vector", lambda e, g=g: e.tensor_copy(
                        s_wkb[:, g * 2048:(g + 1) * 2048], s_wst[:]))
                    ST("sync", lambda e, g=g: e.dma_start(out=s_wst[:], in_=wvT_d[g]))
                    ST("vector", lambda e, g=g: e.tensor_copy(
                        s_wvb[:, g * 2048:(g + 1) * 2048], s_wst[:]))
                ST("vector", lambda e: e.memset(s_vt[:], 0.0))
                for j in range(4):
                    ST("vector", lambda e, j=j: e.memset(
                        s_vt[:, j * 136 + 128:j * 136 + 129], 1.0))

                for tb in range(S // TW):
                    t0 = tb * TW
                    for c in range(NC_):
                        ST("sync", lambda e, c=c, t0=t0: e.dma_start(
                            out=s_xf[c][:], in_=xt_d[c * 128:(c + 1) * 128, t0:t0 + TW]))
                    emit_layernorm(s_xf, s_xn, p_row, p_bc, s_sq, s_st, TW)
                    for c in range(NC_):
                        ST("vector", lambda e, c=c: e.tensor_copy(s_xnh[c][:], s_xn[c][:]))
                        ST("gpsimd", lambda e, c=c, t0=t0: e.dma_start(
                            out=xnh_d[c * 128:(c + 1) * 128, t0:t0 + TW], in_=s_xnh[c][:]))
                    for g in range(4):
                        for ic in range(NC_):
                            ST("tensor", lambda e, ic=ic, g=g: e.matmul(
                                p_kv[:], s_wkb[:, g * 2048 + ic * 128:g * 2048 + (ic + 1) * 128],
                                s_xnh[ic][:], start=(ic == 0), stop=(ic == NC_ - 1)))
                        ST("vector", lambda e: e.tensor_copy(s_k[:], p_kv[:]))
                        ST("tensor", lambda e: e.matmul(p_rot[:], s_rT[:], s_k[:],
                                                        start=True, stop=True))
                        ST("vector", lambda e, t0=t0: e.tensor_tensor(
                            s_kr[:], s_k[:], s_cos[:, t0:t0 + TW], ALU.mult))
                        ST("vector", lambda e, t0=t0: e.tensor_tensor(
                            s_t1[:], p_rot[:], s_sin[:, t0:t0 + TW], ALU.mult))
                        ST("vector", lambda e: e.tensor_add(s_kr[:], s_kr[:], s_t1[:]))
                        ST("vector", lambda e: e.tensor_copy(s_khb[:], s_kr[:]))
                        ST("gpsimd", lambda e, g=g, tb=tb: e.dma_start(
                            out=kth_d[g, :, tb * 4:(tb + 1) * 4, :],
                            in_=s_khb[:].rearrange("p (j f) -> p j f", f=128)))
                        ST("vector", lambda e: e.tensor_tensor(s_t1[:], s_kr[:], s_kr[:], ALU.mult))
                        ST("tensor", lambda e: e.matmul(p_row[0:1, :], s_ones[:, 0:1],
                                                        s_t1[:], start=True, stop=True))
                        ST("vector", lambda e: e.reduce_max(
                            s_st[0:1, 4 * TW:4 * TW + 1], p_row[0:1, :], axis=AX.X))
                        if tb == 0:
                            ST("vector", lambda e, g=g: e.tensor_copy(
                                s_stats[0:1, g:g + 1], s_st[0:1, 4 * TW:4 * TW + 1]))
                        else:
                            ST("vector", lambda e, g=g: e.tensor_tensor(
                                s_stats[0:1, g:g + 1], s_stats[0:1, g:g + 1],
                                s_st[0:1, 4 * TW:4 * TW + 1], ALU.max))
                        for ic in range(NC_):
                            ST("tensor", lambda e, ic=ic, g=g: e.matmul(
                                p_kv[:], s_wvb[:, g * 2048 + ic * 128:g * 2048 + (ic + 1) * 128],
                                s_xnh[ic][:], start=(ic == 0), stop=(ic == NC_ - 1)))
                        ST("vector", lambda e: e.tensor_copy(s_vb16[:], p_kv[:]))
                        for j in range(TW // 128):
                            ST("tensor", lambda e, j=j: e.transpose(
                                p_trb[:], s_vb16[:, j * 128:(j + 1) * 128], s_identb[:]))
                            ST("vector", lambda e, j=j: e.tensor_copy(
                                s_vt[:, j * 136:j * 136 + 128], p_trb[:]))
                        ST("gpsimd", lambda e, g=g, tb=tb: e.dma_start(
                            out=vta_d[g, :, tb * 4:(tb + 1) * 4, :],
                            in_=s_vt[:].rearrange("p (j f) -> p j f", f=136)))
                pr.flush()

        def phase_p3a():
            TW = 512
            with ExitStack() as es:
                sb = lambda n, s, dt=F32: es.enter_context(nc.sbuf_tensor(n, s, dt))
                ps = lambda n, s: es.enter_context(nc.psum_tensor(n, s, F32))
                s_xnh = [sb(f"p3a_xnh{c}", [128, TW], BF16) for c in range(NC_)]
                s_wst = sb("p3a_wst", [128, 16 * 128])
                s_wqb = sb("p3a_wqb", [128, 16 * 16 * 128], BF16)
                s_q = sb("p3a_q", [128, TW])
                s_qr = sb("p3a_qr", [128, TW])
                s_qh = sb("p3a_qh", [128, TW], BF16)
                s_t1 = sb("p3a_t1", [128, TW])
                s_m = sb("p3a_m", [1, 2])
                s_cos = sb("p3a_cos", [128, S])
                s_sin = sb("p3a_sin", [128, S])
                p_q = ps("p3a_q_p", [128, TW])
                p_rot = ps("p3a_rot", [128, TW])
                p_row = ps("p3a_row", [1, TW])
                ST("sync", lambda e: e.dma_start(out=s_cos[:], in_=cos_d))
                ST("sync", lambda e: e.dma_start(out=s_sin[:], in_=sin_d))
                for h in range(NH):
                    ST("sync", lambda e, h=h: e.dma_start(out=s_wst[:], in_=wqT_d[h]))
                    ST("vector", lambda e, h=h: e.tensor_copy(
                        s_wqb[:, h * 2048:(h + 1) * 2048], s_wst[:]))
                for tb in range(S // TW):
                    t0 = tb * TW
                    for c in range(NC_):
                        ST("sync", lambda e, c=c, t0=t0: e.dma_start(
                            out=s_xnh[c][:], in_=xnh_d[c * 128:(c + 1) * 128, t0:t0 + TW]))
                    for h in range(NH):
                        for ic in range(NC_):
                            ST("tensor", lambda e, ic=ic, h=h: e.matmul(
                                p_q[:], s_wqb[:, (h * 16 + ic) * 128:(h * 16 + ic + 1) * 128],
                                s_xnh[ic][:], start=(ic == 0), stop=(ic == NC_ - 1)))
                        ST("vector", lambda e: e.tensor_copy(s_q[:], p_q[:]))
                        ST("tensor", lambda e: e.matmul(p_rot[:], s_rT[:], s_q[:],
                                                        start=True, stop=True))
                        ST("vector", lambda e, t0=t0: e.tensor_tensor(
                            s_qr[:], s_q[:], s_cos[:, t0:t0 + TW], ALU.mult))
                        ST("vector", lambda e, t0=t0: e.tensor_tensor(
                            s_t1[:], p_rot[:], s_sin[:, t0:t0 + TW], ALU.mult))
                        ST("vector", lambda e: e.tensor_add(s_qr[:], s_qr[:], s_t1[:]))
                        ST("vector", lambda e: e.tensor_copy(s_qh[:], s_qr[:]))
                        ST("gpsimd", lambda e, h=h, t0=t0: e.dma_start(
                            out=qfh_d[h * 128:(h + 1) * 128, t0:t0 + TW], in_=s_qh[:]))
                        ST("vector", lambda e: e.tensor_tensor(s_t1[:], s_qr[:], s_qr[:], ALU.mult))
                        ST("tensor", lambda e: e.matmul(p_row[0:1, :], s_ones[:, 0:1],
                                                        s_t1[:], start=True, stop=True))
                        ST("vector", lambda e: e.reduce_max(s_m[0:1, 0:1], p_row[0:1, :], axis=AX.X))
                        if tb == 0:
                            ST("vector", lambda e, h=h: e.tensor_copy(
                                s_stats[0:1, 4 + h:5 + h], s_m[0:1, 0:1]))
                        else:
                            ST("vector", lambda e, h=h: e.tensor_tensor(
                                s_stats[0:1, 4 + h:5 + h], s_stats[0:1, 4 + h:5 + h],
                                s_m[0:1, 0:1], ALU.max))
                pr.flush()

        def phase_p3b():
            QW = 512
            with ExitStack() as es:
                sb = lambda n, s, dt=F32: es.enter_context(nc.sbuf_tensor(n, s, dt))
                ps = lambda n, s, dt=F32: es.enter_context(nc.psum_tensor(n, s, dt))
                s_kb = [sb(f"p3b_kb{b}", [128, 32 * 128], BF16) for b in range(2)]
                s_vb = [sb(f"p3b_vb{b}", [128, 32 * 136], BF16) for b in range(2)]
                s_qh = [sb(f"p3b_qh{b}", [128, QW], BF16) for b in range(2)]
                s_e = [sb(f"p3b_e{b}", [128, QW], BF16) for b in range(2)]
                s_r = [sb(f"p3b_r{b}", [1, QW]) for b in range(2)]
                s_rb = [sb(f"p3b_rb{b}", [128, QW]) for b in range(2)]
                s_aob = [sb(f"p3b_aob{b}", [128, QW], BF16) for b in range(2)]
                s_sc = sb("p3b_sc", [1, 16])
                p_s = [ps(f"p3b_s{b}", [128, QW]) for b in range(2)]
                p_o = [ps(f"p3b_o{b}", [128, QW]) for b in range(2)]
                p_row = [ps(f"p3b_row{b}", [1, QW]) for b in range(2)]
                p_bc = ps("p3b_bc", [128, QW])
                p_t = ps("p3b_t", [128, 16])

                ST("scalar", lambda e: e.activation(s_sc[0:1, 0:4], s_stats[0:1, 0:4], AF.Sqrt))
                ST("scalar", lambda e: e.activation(s_stats[0:1, 20:36],
                                                    s_stats[0:1, 4:20], AF.Sqrt))
                for h in range(NH):
                    ST("vector", lambda e, h=h: e.tensor_tensor(
                        s_stats[0:1, 40 + h:41 + h], s_stats[0:1, 20 + h:21 + h],
                        s_sc[0:1, h // 4:h // 4 + 1], ALU.mult))
                ST("vector", lambda e: e.tensor_scalar(
                    s_stats[0:1, 40:56], s_stats[0:1, 40:56], -SCALE0, None, ALU.mult))
                ST("tensor", lambda e: e.matmul(p_t[:], s_ones[0:1, :],
                                                s_stats[0:1, 40:56], start=True, stop=True))
                ST("scalar", lambda e: e.activation(s_nbias[:], p_t[:], AF.Identity))
                pr.flush()

                entry = pr.k
                msc, mA, mexp = {}, {}, {}
                mkv, mq, mbc, mcp, mrec, mmul, mst = {}, {}, {}, {}, {}, {}, {}

                def kv_load(gi):
                    g = gi % 4
                    buf = gi % 2
                    w = [("TICK", entry)] if gi < 2 else [("sMM", mA[(gi - 2) * 4 * 32 + 4 * 32 - 1])]
                    pr.raw("sync", lambda e, g=g, buf=buf: e.dma_start(
                        out=s_kb[buf][:], in_=kth_d[g].rearrange("p j f -> p (j f)")),
                        waits=w, inc="sD1")
                    mkv[gi] = pr.raw("sync", lambda e, g=g, buf=buf: e.dma_start(
                        out=s_vb[buf][:], in_=vta_d[g].rearrange("p j f -> p (j f)")),
                        waits=w, inc="sD1")

                def q_load(t):
                    qb, h = t // NH, t % NH
                    buf = t % 2
                    w = ([("TICK", entry)] if t < 2
                         else [("sMM", msc[(t - 2) * 32 + 31])])
                    mq[t] = pr.raw("sync", lambda e, h=h, qb=qb, buf=buf: e.dma_start(
                        out=s_qh[buf][:],
                        in_=qfh_d[h * 128:(h + 1) * 128, qb * QW:(qb + 1) * QW]),
                        waits=w, inc="sD0")

                NT = (S // QW) * NH
                kv_load(0); kv_load(1)
                q_load(0); q_load(1)

                def pe_score(t, j):
                    J = t * 32 + j
                    gi = t // 4
                    w = []
                    if J >= 2:
                        w.append(("sE2", mexp[J - 2]))
                    if j == 0:
                        w += [("sD0", mq[t]), ("sD1", mkv[gi])]
                    msc[J] = pr.raw("tensor", lambda e, j=j, t=t, gi=gi: e.matmul(
                        p_s[J % 2][:], s_kb[gi % 2][:, j * 128:(j + 1) * 128],
                        s_qh[t % 2][:], start=True, stop=True), waits=w, inc="sMM")

                def pe_denom_av(t, j):
                    J = t * 32 + j
                    gi = t // 4
                    w = [("sE2", mexp[J])]
                    if j == 0 and t >= 2:
                        w.append(("sGB", mmul[t - 2]))
                    pr.raw("tensor", lambda e, j=j, t=t: e.matmul(
                        p_row[t % 2][0:1, :], s_onesb[:, 0:1], s_e[J % 2][:],
                        start=(j == 0), stop=(j == 31)), waits=w)
                    mA[J] = pr.raw("tensor", lambda e, j=j, t=t, gi=gi: e.matmul(
                        p_o[t % 2][:], s_vb[gi % 2][:, j * 136:j * 136 + 128],
                        s_e[J % 2][:], start=(j == 0), stop=(j == 31)), inc="sMM")

                def pe_bcast(t):
                    w = [("sGB", mrec[t])]
                    if t >= 1:
                        w.append(("sE2", mcp[t - 1]))
                    mbc[t] = pr.raw("tensor", lambda e, t=t: e.matmul(
                        p_bc[:], s_ones[0:1, :], s_r[t % 2][0:1, :],
                        start=True, stop=True), waits=w, inc="sMM")

                def sc_exp(t, j):
                    J = t * 32 + j
                    h = t % NH
                    mexp[J] = pr.raw("scalar", lambda e, J=J, h=h: e.activation(
                        s_e[J % 2][:], p_s[J % 2][:], AF.Exp,
                        bias=s_nbias[:, h:h + 1], scale=SCALE0),
                        waits=[("sMM", msc[J])], inc="sE2")

                def sc_copy(t):
                    w = [("sMM", mbc[t])]
                    if t >= 2:
                        w.append(("sGB", mmul[t - 2]))
                    mcp[t] = pr.raw("scalar", lambda e, t=t: e.activation(
                        s_rb[t % 2][:], p_bc[:], AF.Identity), waits=w, inc="sE2")

                def dve_recip(t):
                    mrec[t] = pr.raw("vector", lambda e, t=t: e.reciprocal(
                        s_r[t % 2][0:1, :], p_row[t % 2][0:1, :]),
                        waits=[("sMM", mA[t * 32 + 31])], inc="sGB")

                def dve_mult(t):
                    w = [("sE2", mcp[t])]
                    if t >= 2:
                        w.append(("sLD", mst[t - 2]))
                    mmul[t] = pr.raw("vector", lambda e, t=t: e.tensor_tensor(
                        s_aob[t % 2][:], p_o[t % 2][:], s_rb[t % 2][:], ALU.mult),
                        waits=w, inc="sGB")

                def ao_store(t):
                    qb, h = t // NH, t % NH
                    mst[t] = pr.raw("gpsimd", lambda e, h=h, qb=qb, t=t: e.dma_start(
                        out=aof_d[h * 128:(h + 1) * 128, qb * QW:(qb + 1) * QW],
                        in_=s_aob[t % 2][:]),
                        waits=[("sGB", mmul[t])], inc="sLD")

                for t in range(NT):
                    pe_score(t, 0); sc_exp(t, 0)
                    pe_score(t, 1); sc_exp(t, 1)
                    if t >= 1:
                        pe_bcast(t - 1)
                        sc_copy(t - 1)
                        dve_mult(t - 1)
                        ao_store(t - 1)
                    for j in range(30):
                        pe_denom_av(t, j)
                        pe_score(t, j + 2)
                        sc_exp(t, j + 2)
                    pe_denom_av(t, 30)
                    pe_denom_av(t, 31)
                    dve_recip(t)
                    if t + 2 < NT:
                        q_load(t + 2)
                        if (t + 2) % 4 == 0 and (t + 2) // 4 >= 2:
                            kv_load((t + 2) // 4)
                pe_bcast(NT - 1)
                sc_copy(NT - 1)
                dve_mult(NT - 1)
                ao_store(NT - 1)
                pr.barrier(("sMM", "sE2", "sGB", "sD0", "sD1", "sLD"))
                pr.flush()

        def phase_p3c():
            TW = 512
            with ExitStack() as es:
                sb = lambda n, s, dt=F32: es.enter_context(nc.sbuf_tensor(n, s, dt))
                ps = lambda n, s: es.enter_context(nc.psum_tensor(n, s, F32))
                s_ao = [sb(f"p3c_ao{c}", [128, TW], BF16) for c in range(NC_)]
                s_wst = sb("p3c_wst", [128, 16 * 128])
                s_wob = sb("p3c_wob", [128, 16 * 16 * 128], BF16)
                s_s = sb("p3c_s", [128, TW])
                p_s = ps("p3c_p", [128, TW])
                for oc in range(NC_):
                    ST("sync", lambda e, oc=oc: e.dma_start(out=s_wst[:], in_=woT_d[oc]))
                    ST("vector", lambda e, oc=oc: e.tensor_copy(
                        s_wob[:, oc * 2048:(oc + 1) * 2048], s_wst[:]))
                for tb in range(S // TW):
                    t0 = tb * TW
                    for c in range(NC_):
                        ST("sync", lambda e, c=c, t0=t0: e.dma_start(
                            out=s_ao[c][:], in_=aof_d[c * 128:(c + 1) * 128, t0:t0 + TW]))
                    for oc in range(NC_):
                        for ic in range(NC_):
                            ST("tensor", lambda e, ic=ic, oc=oc: e.matmul(
                                p_s[:], s_wob[:, (oc * 16 + ic) * 128:(oc * 16 + ic + 1) * 128],
                                s_ao[ic][:], start=(ic == 0), stop=(ic == NC_ - 1)))
                        ST("vector", lambda e: e.tensor_copy(s_s[:], p_s[:]))
                        ST("gpsimd", lambda e, oc=oc, t0=t0: e.dma_start(
                            out=sf_d[oc * 128:(oc + 1) * 128, t0:t0 + TW], in_=s_s[:]))
                pr.flush()

        def phase_p4():
            TW = 512
            with ExitStack() as es:
                sb = lambda n, s, dt=F32: es.enter_context(nc.sbuf_tensor(n, s, dt))
                ps = lambda n, s: es.enter_context(nc.psum_tensor(n, s, F32))
                s_xh = [sb(f"p4_xh{c}", [128, TW], BF16) for c in range(NC_)]
                s_cb = [sb(f"p4_cb{q}", [128, 4 * TW]) for q in range(4)]
                s_cbh = [sb(f"p4_cbh{q}", [128, 4 * TW], BF16) for q in range(4)]
                s_wh = [sb(f"p4_wh{b}", [128, 2 * 16 * 128], BF16) for b in range(2)]
                s_g = [sb(f"p4_g{c}", [128, TW]) for c in range(NC_)]
                s_rw = sb("p4_rw", [128, NC_ * EXP])
                s_rwh = sb("p4_rwh", [128, NC_ * EXP], BF16)
                s_rbb = sb("p4_rbb", [128, EXP])
                s_ebT = sb("p4_ebT", [128, 16 * 8])
                s_obT = sb("p4_obT", [128, 16])
                s_eb8 = sb("p4_eb8", [8, DEXP])
                s_ob1 = sb("p4_ob1", [1, D])
                s_lg = sb("p4_lg", [128, 8])
                s_m = sb("p4_m", [128, 8])
                s_gt = sb("p4_gt", [128, 4 * 8])
                s_gf = sb("p4_gf", [8, TW])
                s_gbc = sb("p4_gbc", [128, TW])
                s_z = sb("p4_z", [128, 2 * TW])
                s_t1 = sb("p4_t1", [128, 2 * TW])
                s_t2 = sb("p4_t2", [128, 2 * TW])
                p_sm = ps("p4_sm", [128, 512])
                p_l = p_sm[:, 0:8]
                p_tr = p_sm[:, 128:256]
                p_e4 = [ps(f"p4_e4{b}", [128, 2 * TW]) for b in range(2)]
                gbn, bdn, mmn, e2n = {}, {}, {}, {}

                ST("sync", lambda e: e.dma_start(out=s_rw[:], in_=rw_d))
                ST("vector", lambda e: e.tensor_copy(s_rwh[:], s_rw[:]))
                ST("sync", lambda e: e.dma_start(out=s_lg[0:1, :], in_=rb_d.unsqueeze(0)))
                bcast_row(p_sm, s_lg[0:1, 0:8], 8)
                ST("scalar", lambda e: e.activation(s_rbb[:], p_l, AF.Identity))
                ST("sync", lambda e: e.dma_start(out=s_eb8[:], in_=eb_d))
                ST("sync", lambda e: e.dma_start(out=s_ob1[:], in_=ob_d.unsqueeze(0)))
                for dc in range(16):
                    ST("tensor", lambda e, dc=dc: e.transpose(
                        p_tr[:, 0:8], s_eb8[:, dc * 128:(dc + 1) * 128], s_ident[0:8, 0:8]))
                    ST("vector", lambda e, dc=dc: e.tensor_copy(
                        s_ebT[:, dc * 8:(dc + 1) * 8], p_tr[:, 0:8]))
                    ST("tensor", lambda e, dc=dc: e.transpose(
                        p_tr[:, 0:1], s_ob1[:, dc * 128:(dc + 1) * 128], s_ident[0:1, 0:1]))
                    ST("vector", lambda e, dc=dc: e.tensor_copy(
                        s_obT[:, dc:dc + 1], p_tr[:, 0:1]))

                for tb in range(S // TW):
                    t0 = tb * TW
                    for c in range(NC_):
                        ST("sync", lambda e, c=c, t0=t0: e.dma_start(
                            out=s_xh[c][:], in_=xnh_d[c * 128:(c + 1) * 128, t0:t0 + TW]))
                    for sub in range(4):
                        for ic in range(NC_):
                            ST("tensor", lambda e, ic=ic, sub=sub: e.matmul(
                                p_l[:], s_xh[ic][:, sub * 128:(sub + 1) * 128],
                                s_rwh[:, ic * 8:(ic + 1) * 8],
                                start=(ic == 0),
                                stop=(ic == NC_ - 1)))
                        ST("vector", lambda e: e.tensor_tensor(s_lg[:], p_l[:], s_rbb[:], ALU.add))
                        ST("vector", lambda e: e.reduce_max(s_m[:, 0:1], s_lg[:], axis=AX.X))
                        ST("vector", lambda e, sub=sub: e.tensor_scalar(
                            s_gt[:, sub * 8:(sub + 1) * 8], s_lg[:], s_m[:, 0:1],
                            None, ALU.is_equal))
                        ST("vector", lambda e, sub=sub: e.scalar_tensor_tensor(
                            s_lg[:], s_gt[:, sub * 8:(sub + 1) * 8], -1e30, s_lg[:],
                            ALU.mult, ALU.add))
                        ST("vector", lambda e: e.reduce_max(s_m[:, 1:2], s_lg[:], axis=AX.X))
                        ST("vector", lambda e: e.tensor_scalar(
                            s_m[:, 4:5], s_m[:, 1:2], -1.0, None, ALU.mult))
                        ST("scalar", lambda e: e.activation(
                            s_m[:, 2:3], s_m[:, 0:1], AF.Sigmoid, bias=s_m[:, 4:5], scale=1.0))
                        ST("vector", lambda e: e.tensor_scalar(
                            s_m[:, 5:6], s_m[:, 0:1], -1.0, None, ALU.mult))
                        ST("scalar", lambda e: e.activation(
                            s_m[:, 3:4], s_m[:, 1:2], AF.Sigmoid, bias=s_m[:, 5:6], scale=1.0))
                        ST("vector", lambda e, sub=sub: e.tensor_scalar(
                            s_gt[:, sub * 8:(sub + 1) * 8], s_gt[:, sub * 8:(sub + 1) * 8],
                            s_m[:, 2:3], None, ALU.mult))
                        ST("vector", lambda e: e.tensor_scalar(
                            s_lg[:], s_lg[:], s_m[:, 1:2], None, ALU.is_equal))
                        ST("vector", lambda e: e.tensor_scalar(
                            s_lg[:], s_lg[:], s_m[:, 3:4], None, ALU.mult))
                        ST("vector", lambda e, sub=sub: e.tensor_add(
                            s_gt[:, sub * 8:(sub + 1) * 8], s_gt[:, sub * 8:(sub + 1) * 8],
                            s_lg[:]))
                        ST("tensor", lambda e, sub=sub: e.transpose(
                            p_tr[0:8, 0:128], s_gt[:, sub * 8:(sub + 1) * 8], s_ident[:]))
                        ST("vector", lambda e, sub=sub: e.tensor_copy(
                            s_gf[0:8, sub * 128:(sub + 1) * 128], p_tr[0:8, 0:128]))
                    ST("gpsimd", lambda e: e.dma_start(out=gb_d, in_=s_gf[:]))

                    entry = pr.k
                    gbn[0] = pr.raw("sync", lambda e: e.dma_start(
                        out=s_gbc[:],
                        in_=gb_d[0:1, :].partition_broadcast(128).squeeze(1)),
                        waits=[("TICK", entry)], inc="sGB")

                    groups = ([("exp", ex, dh) for ex in range(EXP) for dh in range(8)]
                              + [("out", 0, oh) for oh in range(8)])
                    NG = len(groups)

                    def emit_band(li):
                        gg = tb * NG + li
                        kind, ex, dh = groups[li]
                        buf = gg % 2
                        sd = "sD%d" % buf
                        if kind == "exp":
                            hi_src = ewh_d[ex, dh * 2:(dh + 1) * 2]
                        else:
                            hi_src = owh_d[dh * 2:(dh + 1) * 2]
                        w = ([("sMM", mmn[gg - 2])] if (gg - 2) in mmn
                             else [("TICK", entry)])
                        bdn[gg] = pr.raw(
                            "sync", lambda e, hi_src=hi_src, buf=buf: e.dma_start(
                                out=s_wh[buf][:].rearrange("p (q x) -> p q x", q=2),
                                in_=hi_src.rearrange("q p x -> p q x")), waits=w, inc=sd)

                    def EL(eng, fn, extra=()):
                        pr.raw(eng, fn,
                               waits=[("sE2", pr.cnt["sE2"])] + list(extra), inc="sE2")

                    emit_band(0)
                    emit_band(1)
                    for li, (kind, ex, dh) in enumerate(groups):
                        gg = tb * NG + li
                        buf = gg % 2
                        sd = "sD%d" % buf
                        if kind == "exp":
                            xs = s_xh
                        else:
                            xs = s_cbh
                        w = [("TICK", entry), ("sD%d" % buf, bdn[gg])]
                        if (gg - 2) in e2n:
                            w.append(("sE2", e2n[gg - 2]))
                        if kind == "out" and dh == 0:
                            w.append(("sE2", e2n[gg - 1]))
                        first = True
                        for dcl in range(2):
                            for ic in range(NC_):
                                if kind == "exp":
                                    rhs_fn = lambda xs=xs, ic=ic: xs[ic][:]
                                else:
                                    rhs_fn = (lambda xs=xs, ic=ic:
                                              xs[ic // 4][:, (ic % 4) * TW:(ic % 4 + 1) * TW])
                                inc = None
                                if dcl == 1 and ic == NC_ - 1:
                                    inc = "sMM"
                                pr.raw("tensor", lambda e, dcl=dcl,
                                       buf=buf, rhs_fn=rhs_fn, ic=ic: e.matmul(
                                           p_e4[buf][:, dcl * TW:(dcl + 1) * TW],
                                           s_wh[buf][:, (dcl * 16 + ic) * 128:(dcl * 16 + ic + 1) * 128],
                                           rhs_fn(), start=(ic == 0),
                                           stop=(ic == NC_ - 1)),
                                       waits=w if first else (), inc=inc)
                                first = False
                        mmn[gg] = pr.cnt["sMM"]
                        if li + 2 < NG:
                            emit_band(li + 2)
                        if kind == "exp":
                            for dcl in range(2):
                                dc = dh * 2 + dcl
                                EL("scalar", lambda e, dcl=dcl, dc=dc, ex=ex, buf=buf:
                                   e.activation(
                                       s_z[:, dcl * TW:(dcl + 1) * TW],
                                       p_e4[buf][:, dcl * TW:(dcl + 1) * TW], AF.Identity,
                                       bias=s_ebT[:, dc * 8 + ex:dc * 8 + ex + 1]),
                                   extra=[("sMM", mmn[gg])])
                            EL("scalar", lambda e: e.activation(s_t1[:], s_z[:], AF.Square))
                            EL("vector", lambda e: e.tensor_scalar(
                                s_t1[:], s_t1[:], GELU_A, 1.0, ALU.mult, ALU.add))
                            EL("vector", lambda e: e.tensor_tensor(
                                s_t1[:], s_t1[:], s_z[:], ALU.mult))
                            EL("scalar", lambda e: e.activation(
                                s_t1[:], s_t1[:], AF.Sigmoid, scale=GELU_C2))
                            EL("vector", lambda e: e.tensor_tensor(
                                s_t1[:], s_t1[:], s_z[:], ALU.mult))
                            q4, r4 = (dh * 2) // 4, (dh * 2) % 4
                            for dcl in range(2):
                                tgt = (lambda e, q4=q4, r4=r4, dcl=dcl: e.tensor_tensor(
                                    s_cb[q4][:, (r4 + dcl) * TW:(r4 + dcl + 1) * TW],
                                    s_t1[:, dcl * TW:(dcl + 1) * TW], s_gbc[:], ALU.mult)
                                ) if ex == 0 else (
                                    lambda e, dcl=dcl: e.tensor_tensor(
                                        s_t2[:, dcl * TW:(dcl + 1) * TW],
                                        s_t1[:, dcl * TW:(dcl + 1) * TW], s_gbc[:], ALU.mult))
                                EL("vector", tgt,
                                   extra=[("sGB", gbn[ex])] if dcl == 0 else ())
                            if ex != 0:
                                EL("vector", lambda e, q4=q4, r4=r4: e.tensor_add(
                                    s_cb[q4][:, r4 * TW:(r4 + 2) * TW],
                                    s_cb[q4][:, r4 * TW:(r4 + 2) * TW], s_t2[:]))
                            if dh == 7:
                                if ex + 1 < EXP:
                                    gbn[ex + 1] = pr.raw(
                                        "sync", lambda e, ex=ex: e.dma_start(
                                            out=s_gbc[:],
                                            in_=gb_d[ex + 1:ex + 2, :]
                                            .partition_broadcast(128).squeeze(1)),
                                        waits=[("sE2", pr.cnt["sE2"])], inc="sGB")
                                else:
                                    for q in range(4):
                                        EL("vector", lambda e, q=q: e.tensor_copy(
                                            s_cbh[q][:], s_cb[q][:]))
                        else:
                            for ocl in range(2):
                                oc = dh * 2 + ocl
                                EL("scalar", lambda e, ocl=ocl, oc=oc, buf=buf: e.activation(
                                    s_g[oc][:], p_e4[buf][:, ocl * TW:(ocl + 1) * TW],
                                    AF.Silu, bias=s_obT[:, oc:oc + 1]),
                                   extra=[("sMM", mmn[gg])] if ocl == 0 else ())
                        e2n[gg] = pr.cnt["sE2"]
                    pr.barrier(("sMM", "sE2", "sGB", "sD0", "sD1"))
                    for c in range(NC_):
                        ST("sync", lambda e, c=c, t0=t0: e.dma_start(
                            out=s_t1[:, 0:TW], in_=xt_d[c * 128:(c + 1) * 128, t0:t0 + TW]))
                        ST("sync", lambda e, c=c, t0=t0: e.dma_start(
                            out=s_t2[:, 0:TW], in_=sf_d[c * 128:(c + 1) * 128, t0:t0 + TW]))
                        ST("vector", lambda e, c=c: e.tensor_tensor(
                            s_z[:, 0:TW], s_t2[:, 0:TW], s_g[c][:], ALU.mult))
                        ST("vector", lambda e: e.tensor_add(
                            s_z[:, 0:TW], s_t1[:, 0:TW], s_z[:, 0:TW]))
                        ST("gpsimd", lambda e, c=c, t0=t0: e.dma_start(
                            out=y1f_d[c * 128:(c + 1) * 128, t0:t0 + TW], in_=s_z[:, 0:TW]))
                pr.flush()

        def phase_p5():
            TW = 512
            with ExitStack() as es:
                sb = lambda n, s, dt=F32: es.enter_context(nc.sbuf_tensor(n, s, dt))
                ps = lambda n, s: es.enter_context(nc.psum_tensor(n, s, F32))
                s_xn = [sb(f"p5_xn{c}", [128, TW]) for c in range(NC_)]
                s_xh = [sb(f"p5_xh{c}", [128, TW], BF16) for c in range(NC_)]
                s_hh = [sb(f"p5_hh{o}", [128, TW], BF16) for o in range(NOC1)]
                s_kv = [sb(f"p5_kv{g}", [128, TW]) for g in range(4)]
                s_selv = [sb(f"p5_sv{g}", [128, TW]) for g in range(4)]
                s_wst = sb("p5_wst", [128, 16 * 128])
                s_wkvb = sb("p5_wkvb", [128, 4 * 16 * 128], BF16)
                s_q1 = [sb(f"p5_q1{b}", [128, 16 * 128], BF16) for b in range(2)]
                s_q2 = [sb(f"p5_q2{b}", [128, 16 * 128], BF16) for b in range(2)]
                s_q3 = [sb(f"p5_q3{b}", [128, NOC1 * 128], BF16) for b in range(2)]
                s_bkvT = sb("p5_bkvT", [128, 4])
                s_bkv1 = sb("p5_bkv1", [1, KVF])
                s_t1 = [sb(f"p5_t1{b}", [128, TW]) for b in range(2)]
                s_t3 = [sb(f"p5_t3{b}", [128, TW]) for b in range(2)]
                s_sq = sb("p5_sq", [128, TW])
                s_st = sb("p5_st", [1, 4 * TW + 8])
                p_a = [ps(f"p5_a{b}", [128, TW]) for b in range(2)]
                p_b2 = [ps(f"p5_b2{b}", [128, TW]) for b in range(2)]
                p_row = ps("p5_row", [1, TW])
                p_bc = ps("p5_bc", [128, TW])
                p_tr = ps("p5_tr", [128, 128])

                ST("sync", lambda e: e.dma_start(out=s_bkv1[:], in_=bkv_d.unsqueeze(0)))
                for g in range(4):
                    ST("tensor", lambda e, g=g: e.transpose(
                        p_tr[:, 0:1], s_bkv1[:, g * 128:(g + 1) * 128], s_ident[0:1, 0:1]))
                    ST("vector", lambda e, g=g: e.tensor_copy(s_bkvT[:, g:g + 1], p_tr[:, 0:1]))
                for g in range(4):
                    ST("sync", lambda e, g=g: e.dma_start(out=s_wst[:], in_=wkv_d[g]))
                    ST("vector", lambda e, g=g: e.tensor_copy(
                        s_wkvb[:, g * 2048:(g + 1) * 2048], s_wst[:]))

                for tb in range(S // TW):
                    t0 = tb * TW
                    for c in range(NC_):
                        ST("sync", lambda e, c=c, t0=t0: e.dma_start(
                            out=s_xn[c][:], in_=y1f_d[c * 128:(c + 1) * 128, t0:t0 + TW]))
                    emit_layernorm(s_xn, s_xn, p_row, p_bc, s_sq, s_st, TW)
                    for c in range(NC_):
                        ST("vector", lambda e, c=c: e.tensor_copy(s_xh[c][:], s_xn[c][:]))
                    for g in range(4):
                        for ic in range(NC_):
                            ST("tensor", lambda e, ic=ic, g=g: e.matmul(
                                p_a[0][:], s_wkvb[:, g * 2048 + ic * 128:g * 2048 + (ic + 1) * 128],
                                s_xh[ic][:], start=(ic == 0), stop=(ic == NC_ - 1)))
                        ST("scalar", lambda e, g=g: e.activation(
                            s_kv[g][:], p_a[0][:], AF.Identity, bias=s_bkvT[:, g:g + 1]))
                        ST("tensor", lambda e, g=g: e.matmul(
                            p_b2[0][:], s_selT[:], s_kv[g][:], start=True, stop=True))
                        ST("vector", lambda e, g=g: e.tensor_copy(s_selv[g][:], p_b2[0][:]))
                    for c in range(NC_):
                        ST("sync", lambda e, c=c, t0=t0: e.dma_start(
                            out=s_xn[c][:], in_=y1f_d[c * 128:(c + 1) * 128, t0:t0 + TW]))
                    pr.flush()

                    entry = pr.k
                    mld1, mld2, mld3a, mld3b = {}, {}, {}, {}
                    mmA, mmB, mm3, msig, msil, mmul5, madd, mst3 = ({}, {}, {},
                                                                    {}, {}, {}, {}, {})

                    def ld12(oc):
                        b = oc % 2
                        w1 = ([("TICK", entry)] if oc < 2 else [("sMM", mmA[oc - 2])])
                        w2 = ([("TICK", entry)] if oc < 2 else [("sMM", mmB[oc - 2])])
                        mld1[oc] = pr.raw("sync", lambda e, oc=oc, b=b: e.dma_start(
                            out=s_q1[b][:], in_=q1b_d[oc]), waits=w1, inc="sD0")
                        mld2[oc] = pr.raw("gpsimd", lambda e, oc=oc, b=b: e.dma_start(
                            out=s_q2[b][:], in_=q2b_d[oc]), waits=w2, inc="sD1")

                    def ld3(fc):
                        b = fc % 2
                        HC = (NOC1 // 2) * 128
                        w = ([("TICK", entry)] if fc < 2 else [("sMM", mm3[fc - 2])])
                        mld3a[fc] = pr.raw("sync", lambda e, fc=fc, b=b: e.dma_start(
                            out=s_q3[b][:, 0:HC], in_=q3b_d[fc, :, 0:HC]),
                            waits=w, inc="sD0")
                        mld3b[fc] = pr.raw("gpsimd", lambda e, fc=fc, b=b: e.dma_start(
                            out=s_q3[b][:, HC:2 * HC], in_=q3b_d[fc, :, HC:2 * HC]),
                            waits=w, inc="sD1")

                    def pe12(oc):
                        b = oc % 2
                        w = [("sD0", mld1[oc])]
                        if oc >= 2:
                            w += [("sE2", msig[oc - 2]), ("sGB", mmul5[oc - 2])]
                        if oc == 0:
                            w.append(("TICK", entry))
                        for ic in range(NC_):
                            mmA[oc] = pr.raw("tensor", lambda e, ic=ic, b=b: e.matmul(
                                p_a[b][:], s_q1[b][:, ic * 128:(ic + 1) * 128], s_xh[ic][:],
                                start=(ic == 0), stop=(ic == NC_ - 1)),
                                waits=w if ic == 0 else (),
                                inc="sMM" if ic == NC_ - 1 else None)
                        w2 = [("sD1", mld2[oc])]
                        for ic in range(NC_):
                            mmB[oc] = pr.raw("tensor", lambda e, ic=ic, b=b: e.matmul(
                                p_b2[b][:], s_q2[b][:, ic * 128:(ic + 1) * 128], s_xh[ic][:],
                                start=(ic == 0), stop=(ic == NC_ - 1)),
                                waits=w2 if ic == 0 else (),
                                inc="sMM" if ic == NC_ - 1 else None)

                    def tail12(oc):
                        b = oc % 2
                        w = [("sMM", mmA[oc])]
                        if oc >= 2:
                            w.append(("sGB", mmul5[oc - 2]))
                        msig[oc] = pr.raw("scalar", lambda e, b=b: e.activation(
                            s_t1[b][:], p_a[b][:], AF.Sigmoid), waits=w, inc="sE2")
                        mmul5[oc] = pr.raw("vector", lambda e, oc=oc, b=b: e.tensor_tensor(
                            s_hh[oc][:], s_t1[b][:], p_b2[b][:], ALU.mult),
                            waits=[("sE2", msig[oc]), ("sMM", mmB[oc])], inc="sGB")

                    def pe3(fc):
                        b = fc % 2
                        w = [("sD0", mld3a[fc]), ("sD1", mld3b[fc])]
                        if fc < 2:
                            w.append(("sGB", mmul5[NOC1 - 1]))
                        else:
                            w.append(("sE2", msil[fc - 2]))
                        for ic in range(NOC1):
                            mm3[fc] = pr.raw("tensor", lambda e, ic=ic, b=b: e.matmul(
                                p_a[b][:], s_q3[b][:, ic * 128:(ic + 1) * 128], s_hh[ic][:],
                                start=(ic == 0), stop=(ic == NOC1 - 1)),
                                waits=w if ic == 0 else (),
                                inc="sMM" if ic == NOC1 - 1 else None)

                    def tail3(fc):
                        b = fc % 2
                        w = [("sMM", mm3[fc])]
                        if fc >= 2:
                            w.append(("sGB", madd[fc - 2]))
                        msil[fc] = pr.raw("scalar", lambda e, b=b: e.activation(
                            s_t1[b][:], p_a[b][:], AF.Silu), waits=w, inc="sE2")
                        pr.raw("vector", lambda e, fc=fc, b=b: e.tensor_tensor(
                            s_t1[b][:], s_t1[b][:], s_selv[fc // 4][:], ALU.mult),
                            waits=[("sE2", msil[fc])], inc="sGB")
                        w3 = []
                        if fc >= 2:
                            w3.append(("sLD", mst3[fc - 2]))
                        madd[fc] = pr.raw("vector", lambda e, fc=fc, b=b: e.tensor_add(
                            s_t3[b][:], s_xn[fc][:], s_t1[b][:]), waits=w3, inc="sGB")
                        mst3[fc] = pr.raw("gpsimd", lambda e, fc=fc, b=b, t0=t0: e.dma_start(
                            out=out_d[fc * 128:(fc + 1) * 128, t0:t0 + TW], in_=s_t3[b][:]),
                            waits=[("sGB", madd[fc])], inc="sLD")

                    ld12(0); ld12(1)
                    for oc in range(NOC1):
                        pe12(oc)
                        if oc + 2 < NOC1:
                            ld12(oc + 2)
                        elif oc + 2 < NOC1 + 2:
                            ld3(oc + 2 - NOC1)
                        tail12(oc)
                    for fc in range(NC_):
                        pe3(fc)
                        if fc + 2 < NC_:
                            ld3(fc + 2)
                        tail3(fc)
                    pr.barrier(("sMM", "sE2", "sGB", "sD0", "sD1", "sLD"))
                pr.flush()

        phase_fns = {"pt": phase_pt, "p2": phase_p2, "p3a": phase_p3a,
                     "p3b": phase_p3b, "p3c": phase_p3c, "p4": phase_p4,
                     "p5": phase_p5}
        for ph in phases:
            with nc.named_scope(ph):
                phase_fns[ph]()

        dump_srcs = {"s": sf_d, "y1": y1f_d}
        for name, ap in dump_aps.items():
            if name == "stats":
                ST("gpsimd", lambda e, ap=ap: e.dma_start(out=ap, in_=s_stats[0:1, :]))
            else:
                ST("gpsimd", lambda e, ap=ap, name=name: e.dma_start(
                    out=ap, in_=dump_srcs[name]))
        pr.flush()

        pr.barrier(list(sems))
        pr.flush()
        final = pr.k
        block.sync(lambda e: e.wait_ge(tick, 16 * final))
    return nc


def _tile_wT(w, noc, nic):
    return np.ascontiguousarray(
        w.reshape(noc, 128, nic, 128).transpose(0, 3, 2, 1).reshape(noc, 128, nic * 128))


def _tile_w(w, nic, noc):
    return np.ascontiguousarray(
        w.reshape(nic, 128, noc, 128).transpose(2, 1, 0, 3).reshape(noc, 128, nic * 128))


def kernel(L0_attn_x, L0_attn_Wq, L0_attn_Wk, L0_attn_Wv, L0_attn_Wo,
           L0_ffn_router_w, L0_ffn_router_b, L0_ffn_expert_w, L0_ffn_expert_b,
           L0_ffn_out_w, L0_ffn_out_b, L1_attn_Wq, L1_attn_bq, L1_attn_Wkv,
           L1_attn_bkv, L1_ffn_w1, L1_ffn_w2, L1_ffn_w3):
    nc = build_program()
    f32 = lambda a: np.asarray(a, dtype=np.float32)
    ew = f32(L0_ffn_expert_w)
    ew_t = np.stack([_tile_w(ew[e], 16, 16) for e in range(EXP)])
    rw = f32(L0_ffn_router_w)
    rw_t = np.ascontiguousarray(rw.reshape(16, 128, EXP).transpose(1, 0, 2).reshape(128, 16 * EXP))
    shared = {
        "wqT": _tile_wT(f32(L0_attn_Wq), 16, 16),
        "wkT": _tile_wT(f32(L0_attn_Wk), 4, 16),
        "wvT": _tile_wT(f32(L0_attn_Wv), 4, 16),
        "woT": _tile_wT(f32(L0_attn_Wo), 16, 16),
        "rw": rw_t, "rb": np.ascontiguousarray(f32(L0_ffn_router_b)),
        "ew": ew_t, "eb": np.ascontiguousarray(f32(L0_ffn_expert_b)),
        "ow": _tile_w(f32(L0_ffn_out_w), 16, 16),
        "ob": np.ascontiguousarray(f32(L0_ffn_out_b)),
        "wkv": _tile_w(f32(L1_attn_Wkv), 16, 4),
        "bkv": np.ascontiguousarray(f32(L1_attn_bkv)),
        "wq1": np.ascontiguousarray(f32(L1_attn_Wq)),
        "bq1": np.ascontiguousarray(f32(L1_attn_bq)),
        "w1T": _tile_wT(f32(L1_ffn_w1), NOC1, 16),
        "w2T": _tile_wT(f32(L1_ffn_w2), NOC1, 16),
        "w3T": _tile_wT(f32(L1_ffn_w3), 16, NOC1),
    }
    xs = f32(L0_attn_x)
    in_maps = [dict(shared, xt=np.ascontiguousarray(xs[i].T)) for i in range(8)]
    res = run_bass_kernel_spmd(nc, in_maps, core_ids=list(range(8)))
    return np.stack([np.ascontiguousarray(np.asarray(r["out"], dtype=np.float32).T)
                     for r in res.results])


# version 19
# speedup vs baseline: 1.0878x; 1.0878x over previous
import math
from contextlib import ExitStack

import numpy as np
import concourse.bass as bass
import concourse.mybir as mybir
from concourse.bass_utils import run_bass_kernel_spmd

F32 = mybir.dt.float32
BF16 = mybir.dt.bfloat16
AF = mybir.ActivationFunctionType
ALU = mybir.AluOpType
AX = mybir.AxisListType

S, D = 4096, 2048
NH, NKV, HD = 16, 4, 128
KVF = NKV * HD
EXP, DEXP = 8, 2048
H1 = 5632
NC_ = D // 128
NOC1 = H1 // 128
SCALE0 = float(HD) ** -0.5
GELU_C2 = 2.0 * math.sqrt(2.0 / math.pi)
GELU_A = 0.044715
LN_EPS = 1e-5
Q_EPS = 1e-8

DEBUG_DUMPS = ()
PHASES = ("p2", "p3a", "p3b", "p3c", "p4", "p5")


def _consts():
    ident = np.eye(128, dtype=np.float32)
    r = np.zeros((128, 128), np.float32)
    for d in range(64):
        r[d, d + 64] = -1.0
        r[64 + d, d] = 1.0
    rT = np.ascontiguousarray(r.T)
    selT = np.zeros((128, 128), np.float32)
    for p in range(128):
        selT[64 + (p % 64), p] = 1.0
    sel8 = np.zeros((8, 8 * 128), np.float32)
    for e in range(8):
        sel8[e, e * 128:(e + 1) * 128] = 1.0
    j = np.arange(64, dtype=np.float64)
    inv_freq = 10000.0 ** (-2.0 * j / 128.0)
    t = np.arange(S, dtype=np.float64)
    freqs = np.outer(t, inv_freq)
    emb = np.concatenate([freqs, freqs], 1)
    cos_f = np.ascontiguousarray(np.cos(emb).T.astype(np.float32))
    sin_f = np.ascontiguousarray(np.sin(emb).T.astype(np.float32))
    return ident, rT, selT, sel8, cos_f, sin_f


class Prog:

    ENGS = ("sync", "gpsimd", "vector", "scalar", "tensor")
    SKIP_OK = ("tensor", "sync", "gpsimd")

    def __init__(self, nc, block, tick, sems):
        self.nc, self.block, self.tick = nc, block, tick
        self.sems = sems
        self.cnt = {n: 0 for n in sems}
        self.k = 0
        self._teng = None
        self.pending = []

    def step(self, eng, fn, after=None):
        idx = self.k
        self.k += 1
        prev_eng = self._teng
        self._teng = eng
        if after is None:
            wait = None if (prev_eng == eng and eng in self.SKIP_OK) else idx
        else:
            wait = after
        waits = [] if wait is None else [("TICK", wait)]
        self.pending.append((eng, fn, waits, "TICK"))
        return idx

    def raw(self, eng, fn, waits=(), inc=None):
        if inc is not None:
            self.cnt[inc] += 1
        self.pending.append((eng, fn, list(waits), inc))
        return self.cnt[inc] if inc is not None else None

    def barrier(self, names, eng="sync"):
        waits = [("TICK", self.k)] + [(n, self.cnt[n]) for n in names]
        idx = self.k
        self.k += 1
        self._teng = eng
        self.pending.append((eng, lambda e: e.nop(), waits, "TICK"))
        return idx

    def flush(self):
        pend, self.pending = self.pending, []
        self._teng = None
        if not pend:
            return
        for name in self.ENGS:
            sel = [(fn, ws, inc) for (e, fn, ws, inc) in pend if e == name]
            if not sel:
                continue

            def body(eng, sel=sel):
                for fn, ws, inc in sel:
                    for sn, cv in ws:
                        sem = self.tick if sn == "TICK" else self.sems[sn]
                        eng.wait_ge(sem, 16 * cv)
                    inst = fn(eng)
                    if inc == "TICK":
                        inst.then_inc(self.tick, 16)
                    elif inc is not None:
                        inst.then_inc(self.sems[inc], 16)

            getattr(self.block, name)(body)


def build_program(phases=PHASES, dumps=DEBUG_DUMPS):
    nc = bass.Bass("TRN2", target_bir_lowering=False, debug=False)
    ident_np, rT_np, selT_np, sel8_np, cos_np, sin_np = _consts()

    xt_d = nc.dram_tensor("xt", [D, S], F32, kind="ExternalInput").ap()
    wqT_d = nc.dram_tensor("wqT", [16, 128, 16 * 128], F32, kind="ExternalInput").ap()
    wkT_d = nc.dram_tensor("wkT", [4, 128, 16 * 128], F32, kind="ExternalInput").ap()
    wvT_d = nc.dram_tensor("wvT", [4, 128, 16 * 128], F32, kind="ExternalInput").ap()
    woT_d = nc.dram_tensor("woT", [16, 128, 16 * 128], F32, kind="ExternalInput").ap()
    rw_d = nc.dram_tensor("rw", [128, 16 * EXP], F32, kind="ExternalInput").ap()
    rb_d = nc.dram_tensor("rb", [EXP], F32, kind="ExternalInput").ap()
    ew_d = nc.dram_tensor("ew", [EXP, 16, 128, 16 * 128], F32, kind="ExternalInput").ap()
    eb_d = nc.dram_tensor("eb", [EXP, DEXP], F32, kind="ExternalInput").ap()
    ow_d = nc.dram_tensor("ow", [16, 128, 16 * 128], F32, kind="ExternalInput").ap()
    ob_d = nc.dram_tensor("ob", [D], F32, kind="ExternalInput").ap()
    wkv_d = nc.dram_tensor("wkv", [4, 128, 16 * 128], F32, kind="ExternalInput").ap()
    bkv_d = nc.dram_tensor("bkv", [KVF], F32, kind="ExternalInput").ap()
    nc.dram_tensor("wq1", [D, D], F32, kind="ExternalInput")
    nc.dram_tensor("bq1", [D], F32, kind="ExternalInput")
    w1T_d = nc.dram_tensor("w1T", [NOC1, 128, 16 * 128], F32, kind="ExternalInput").ap()
    w2T_d = nc.dram_tensor("w2T", [NOC1, 128, 16 * 128], F32, kind="ExternalInput").ap()
    w3T_d = nc.dram_tensor("w3T", [16, 128, NOC1 * 128], F32, kind="ExternalInput").ap()
    out_d = nc.dram_tensor("out", [D, S], F32, kind="ExternalOutput").ap()

    ident_d = nc.inline_tensor(ident_np, "c_ident").ap()
    rT_d = nc.inline_tensor(rT_np, "c_rT").ap()
    selT_d = nc.inline_tensor(selT_np, "c_selT").ap()
    sel8_d = nc.inline_tensor(sel8_np, "c_sel8").ap()
    cos_d = nc.inline_tensor(cos_np, "c_cos").ap()
    sin_d = nc.inline_tensor(sin_np, "c_sin").ap()

    q1b_d = nc.dram_tensor("q1b", [NOC1, 128, 16 * 128], BF16).ap()
    q2b_d = nc.dram_tensor("q2b", [NOC1, 128, 16 * 128], BF16).ap()
    q3b_d = nc.dram_tensor("q3b", [16, 128, NOC1 * 128], BF16).ap()
    xnh_d = nc.dram_tensor("xnh", [D, S], BF16).ap()
    ewh_d = nc.dram_tensor("ewh", [EXP, 16, 128, 16 * 128], BF16).ap()
    owh_d = nc.dram_tensor("owh", [16, 128, 16 * 128], BF16).ap()
    kth_d = nc.dram_tensor("kth", [4, 128, 32, 128], BF16).ap()
    vta_d = nc.dram_tensor("vta", [4, 128, 32, 136], BF16).ap()
    qfh_d = nc.dram_tensor("qfh", [D, S], BF16).ap()
    aof_d = nc.dram_tensor("aof", [D, S], BF16).ap()
    rbq_d = nc.dram_tensor("rbq", [2, 512], F32).ap()
    sf_d = nc.dram_tensor("sf", [D, S], F32).ap()
    y1f_d = nc.dram_tensor("y1f", [D, S], F32).ap()
    gb_d = nc.dram_tensor("gbscr", [8, 512], F32).ap()
    dump_specs = {"s": [D, S], "y1": [D, S], "stats": [1, 64]}
    dump_aps = {n: nc.dram_tensor("dbg_" + n, shp, F32, kind="ExternalOutput").ap()
                for n, shp in dump_specs.items() if n in dumps}

    with ExitStack() as top:
        sbp = lambda n, s, dt=F32: top.enter_context(nc.sbuf_tensor(n, s, dt))
        s_ident = sbp("c_ident_s", [128, 128])
        s_identb = sbp("c_identb_s", [128, 128], BF16)
        s_rT = sbp("c_rT_s", [128, 128])
        s_selT = sbp("c_selT_s", [128, 128])
        s_sel8 = sbp("c_sel8_s", [8, 8 * 128])
        s_ones = sbp("c_ones_s", [128, 128])
        s_onesb = sbp("c_onesb_s", [128, 128], BF16)
        s_stats = sbp("stats_s", [128, 64])
        s_nbias = sbp("nbias_s", [128, 16])
        s_rgam = sbp("rgam_s", [128, 4])
        tick = top.enter_context(nc.semaphore("tick"))
        sems = {n: top.enter_context(nc.semaphore(n))
                for n in ("sMM", "sE2", "sGB", "sD0", "sD1", "sLD",
                          "sPA", "sPB", "sPC")}
        block = top.enter_context(nc.Block())
        pr = Prog(nc, block, tick, sems)
        ST = pr.step

        p_pt = top.enter_context(nc.psum_tensor("pt_p", [128, 16], F32))
        es_pt = ExitStack()
        sbq = lambda n, s, dt=F32: es_pt.enter_context(nc.sbuf_tensor(n, s, dt))
        s_ptw = [sbq(f"ptw{b}", [128, 2048]) for b in range(2)]
        s_ptq = sbq("ptq", [128, 2048])
        s_ptl = sbq("ptl", [128, 2048], BF16)
        s_ptg = [sbq(f"ptg{b}", [128, 2048], BF16) for b in range(2)]
        s_pta = sbq("pta", [128, 8])
        BG = []
        _pt = {"ji": 0, "mB": {}, "lastg": [None, None], "rgam_tick": None,
               "emitted": 0}

        def bg_pump(n):
            for _ in range(min(n, len(BG))):
                BG.pop(0)()
                _pt["emitted"] += 1

        def pt_job(kind, src, dst, m, wid, qsel=None):
            ji = _pt["ji"]; _pt["ji"] += 1
            b = ji % 2
            ldeng = qsel or ("sync" if b == 0 else "gpsimd")
            steng = qsel or ("gpsimd" if b == 0 else "sync")
            wl = [] if ji < 2 else [("sPB", _pt["mB"][ji - 2])]
            la = pr.raw(ldeng, lambda e, src=src, b=b, wid=wid: e.dma_start(
                out=s_ptw[b][:, 0:wid], in_=src), waits=wl, inc="sPA")
            if kind == "g":
                pr.raw("vector", lambda e, b=b, wid=wid: e.reduce_sum(
                    s_pta[:, 4 + b:5 + b], s_ptw[b][:, 0:wid], axis=AX.X,
                    apply_absolute_value=True),
                    waits=[("sPB", pr.cnt["sPB"]), ("sPA", la)], inc="sPB")
                _pt["mB"][ji] = pr.raw("vector", lambda e, b=b, m=m: e.tensor_add(
                    s_pta[:, m:m + 1], s_pta[:, m:m + 1], s_pta[:, 4 + b:5 + b]),
                    waits=[("sPB", pr.cnt["sPB"])], inc="sPB")
                return
            wg = [("sPB", pr.cnt["sPB"]), ("sPA", la)]
            if _pt["lastg"][b] is not None:
                wg.append(("sPC", _pt["lastg"][b]))
            if kind == "q":
                wg.append(("TICK", _pt["rgam_tick"]))
                pr.raw("vector", lambda e, b=b, m=m, wid=wid: e.tensor_scalar(
                    s_ptq[:, 0:wid], s_ptw[b][:, 0:wid], s_rgam[:, m:m + 1],
                    None, ALU.mult), waits=wg, inc="sPB")
                pr.raw("vector", lambda e, wid=wid: e.tensor_scalar(
                    s_ptl[:, 0:wid], s_ptq[:, 0:wid], -0.5, None, ALU.is_lt),
                    waits=[("sPB", pr.cnt["sPB"])], inc="sPB")
                pr.raw("vector", lambda e, b=b, wid=wid: e.tensor_scalar(
                    s_ptg[b][:, 0:wid], s_ptq[:, 0:wid], 0.5, None, ALU.is_gt),
                    waits=[("sPB", pr.cnt["sPB"])], inc="sPB")
                _pt["mB"][ji] = pr.raw("vector", lambda e, b=b, wid=wid: e.tensor_tensor(
                    s_ptg[b][:, 0:wid], s_ptg[b][:, 0:wid], s_ptl[:, 0:wid],
                    ALU.subtract), waits=[("sPB", pr.cnt["sPB"])], inc="sPB")
            else:
                _pt["mB"][ji] = pr.raw("vector", lambda e, b=b, wid=wid: e.tensor_copy(
                    s_ptg[b][:, 0:wid], s_ptw[b][:, 0:wid]), waits=wg, inc="sPB")
            _pt["lastg"][b] = pr.raw(steng, lambda e, dst=dst, b=b, wid=wid: e.dma_start(
                out=dst, in_=s_ptg[b][:, 0:wid]),
                waits=[("sPB", _pt["mB"][ji])], inc="sPC")

        def _w_slices(w_ap):
            nb, _, W = w_ap.shape
            for rb in range(nb):
                for off in range(0, W, 2048):
                    yield rb, off, min(2048, W - off)

        def queue_gamma_jobs():
            for m, w_ap in enumerate((w1T_d, w2T_d, w3T_d)):
                for rb, off, wid in _w_slices(w_ap):
                    BG.append(lambda w_ap=w_ap, rb=rb, off=off, wid=wid, m=m:
                              pt_job("g", w_ap[rb, :, off:off + wid], None, m, wid))

        def queue_cast_jobs():
            casts = [(ow_d[oc], owh_d[oc]) for oc in range(16)]
            casts += [(ew_d[ex, dc], ewh_d[ex, dc])
                      for ex in range(EXP) for dc in range(16)]
            for src, dst in casts:
                BG.append(lambda src=src, dst=dst: pt_job("c", src, dst, 0, 2048))

        def queue_quant_jobs():
            for m, (w_ap, q_ap) in enumerate(((w1T_d, q1b_d), (w2T_d, q2b_d),
                                              (w3T_d, q3b_d))):
                for rb, off, wid in _w_slices(w_ap):
                    BG.append(lambda w_ap=w_ap, q_ap=q_ap, rb=rb, off=off,
                              wid=wid, m=m: pt_job(
                                  "q", w_ap[rb, :, off:off + wid],
                                  q_ap[rb, :, off:off + wid], m, wid,
                                  qsel="gpsimd"))

        N_GAMMA = 136

        def emit_rgam():
            bg_pump(max(0, N_GAMMA - _pt["emitted"]))
            pr.barrier(("sPA", "sPB"))
            for m in range(3):
                ST("tensor", lambda e, m=m: e.matmul(
                    p_pt[0:1, 0:1], s_ones[:, 0:1], s_pta[:, m:m + 1],
                    start=True, stop=True))
                ST("vector", lambda e: e.tensor_scalar(
                    s_pta[0:1, 6:7], p_pt[0:1, 0:1], 1.0 / (H1 * D), None, ALU.mult))
                ST("vector", lambda e: e.tensor_scalar(
                    s_pta[0:1, 6:7], s_pta[0:1, 6:7], Q_EPS, None, ALU.add))
                ST("vector", lambda e: e.reciprocal(s_pta[0:1, 7:8], s_pta[0:1, 6:7]))
                ST("tensor", lambda e: e.matmul(
                    p_pt[:, 0:1], s_ones[0:1, :], s_pta[0:1, 7:8],
                    start=True, stop=True))
                ST("scalar", lambda e, m=m: e.activation(
                    s_rgam[:, m:m + 1], p_pt[:, 0:1], AF.Identity))
            pr.flush()
            _pt["rgam_tick"] = pr.k
            queue_quant_jobs()

        ST("sync", lambda e: e.dma_start(out=s_ident[:], in_=ident_d))
        ST("sync", lambda e: e.dma_start(out=s_rT[:], in_=rT_d))
        ST("sync", lambda e: e.dma_start(out=s_selT[:], in_=selT_d))
        ST("sync", lambda e: e.dma_start(out=s_sel8[:], in_=sel8_d))
        ST("vector", lambda e: e.memset(s_ones[:], 1.0))
        ST("vector", lambda e: e.memset(s_onesb[:], 1.0))
        ST("vector", lambda e: e.memset(s_stats[:], 0.0))
        ST("vector", lambda e: e.tensor_copy(s_identb[:], s_ident[:]))
        ST("vector", lambda e: e.memset(s_pta[:], 0.0))
        queue_gamma_jobs()
        queue_cast_jobs()

        def bcast_row(p_bc, row_ap, width):
            ST("tensor", lambda e: e.matmul(p_bc[:, 0:width], s_ones[0:1, :],
                                            row_ap, start=True, stop=True))

        def emit_layernorm(sx, sxn, p_row, p_bc, s_sq, s_st, tw):
            mu = s_st[0:1, 0:tw]
            msq = s_st[0:1, tw:2 * tw]
            t1 = s_st[0:1, 2 * tw:3 * tw]
            t2 = s_st[0:1, 3 * tw:4 * tw]
            for c in range(NC_):
                ST("tensor", lambda e, c=c: e.matmul(
                    p_row[0:1, 0:tw], s_ones[:, 0:1], sx[c][:, 0:tw],
                    start=(c == 0), stop=(c == NC_ - 1)))
            ST("vector", lambda e: e.tensor_scalar(mu, p_row[0:1, 0:tw], 1.0 / D, None, ALU.mult))
            for c in range(NC_):
                ST("scalar", lambda e, c=c: e.activation(s_sq[:, 0:tw], sx[c][:, 0:tw], AF.Square))
                ST("tensor", lambda e, c=c: e.matmul(
                    p_row[0:1, 0:tw], s_ones[:, 0:1], s_sq[:, 0:tw],
                    start=(c == 0), stop=(c == NC_ - 1)))
            ST("vector", lambda e: e.tensor_scalar(msq, p_row[0:1, 0:tw], 1.0 / D, None, ALU.mult))
            ST("vector", lambda e: e.tensor_tensor(t1, mu, mu, ALU.mult))
            ST("vector", lambda e: e.tensor_tensor(t2, msq, t1, ALU.subtract))
            ST("vector", lambda e: e.tensor_scalar(t2, t2, LN_EPS, None, ALU.add))
            ST("scalar", lambda e: e.activation(t1, t2, AF.Sqrt))
            ST("vector", lambda e: e.reciprocal(t2, t1))
            bcast_row(p_bc, mu, tw)
            ST("scalar", lambda e: e.activation(s_sq[:, 0:tw], p_bc[:, 0:tw], AF.Identity))
            bcast_row(p_bc, t2, tw)
            for c in range(NC_):
                ST("vector", lambda e, c=c: e.tensor_tensor(
                    sxn[c][:, 0:tw], sx[c][:, 0:tw], s_sq[:, 0:tw], ALU.subtract))
                ST("vector", lambda e, c=c: e.tensor_tensor(
                    sxn[c][:, 0:tw], sxn[c][:, 0:tw], p_bc[:, 0:tw], ALU.mult))

        def phase_p2():
            TW = 512
            with ExitStack() as es:
                sb = lambda n, s, dt=F32: es.enter_context(nc.sbuf_tensor(n, s, dt))
                ps = lambda n, s, dt=F32: es.enter_context(nc.psum_tensor(n, s, dt))
                s_xf = [sb(f"p2_xf{c}", [128, TW]) for c in range(NC_)]
                s_xn = s_xf
                s_xnh = [sb(f"p2_xnh{c}", [128, TW], BF16) for c in range(NC_)]
                s_sq = sb("p2_sq", [128, TW])
                s_st = sb("p2_st", [1, 4 * TW + 8])
                s_wst = sb("p2_wst", [128, 2048])
                s_wkb = sb("p2_wkb", [128, 4 * 16 * 128], BF16)
                s_wvb = sb("p2_wvb", [128, 4 * 16 * 128], BF16)
                s_k = sb("p2_k", [128, TW])
                s_kr = sb("p2_kr", [128, TW])
                s_khb = sb("p2_khb", [128, TW], BF16)
                s_t1 = sb("p2_t1", [128, TW])
                s_vb16 = sb("p2_vb16", [128, TW], BF16)
                s_vt = sb("p2_vt", [128, 4 * 136], BF16)
                s_cos = sb("p2_cos", [128, S])
                s_sin = sb("p2_sin", [128, S])
                p_trb = ps("p2_trb", [128, 128], BF16)
                p_row = ps("p2_row", [1, TW])
                p_bc = ps("p2_bc", [128, TW])
                p_kv = ps("p2_kv", [128, TW])
                p_rot = ps("p2_rot", [128, TW])

                ST("sync", lambda e: e.dma_start(out=s_cos[:], in_=cos_d))
                ST("sync", lambda e: e.dma_start(out=s_sin[:], in_=sin_d))
                for g in range(4):
                    ST("sync", lambda e, g=g: e.dma_start(out=s_wst[:], in_=wkT_d[g]))
                    ST("vector", lambda e, g=g: e.tensor_copy(
                        s_wkb[:, g * 2048:(g + 1) * 2048], s_wst[:]))
                    ST("sync", lambda e, g=g: e.dma_start(out=s_wst[:], in_=wvT_d[g]))
                    ST("vector", lambda e, g=g: e.tensor_copy(
                        s_wvb[:, g * 2048:(g + 1) * 2048], s_wst[:]))
                ST("vector", lambda e: e.memset(s_vt[:], 0.0))
                for j in range(4):
                    ST("vector", lambda e, j=j: e.memset(
                        s_vt[:, j * 136 + 128:j * 136 + 129], 1.0))

                for tb in range(S // TW):
                    t0 = tb * TW
                    for c in range(NC_):
                        ST("sync", lambda e, c=c, t0=t0: e.dma_start(
                            out=s_xf[c][:], in_=xt_d[c * 128:(c + 1) * 128, t0:t0 + TW]))
                    emit_layernorm(s_xf, s_xn, p_row, p_bc, s_sq, s_st, TW)
                    for c in range(NC_):
                        ST("vector", lambda e, c=c: e.tensor_copy(s_xnh[c][:], s_xn[c][:]))
                        ST("gpsimd", lambda e, c=c, t0=t0: e.dma_start(
                            out=xnh_d[c * 128:(c + 1) * 128, t0:t0 + TW], in_=s_xnh[c][:]))
                    for g in range(4):
                        for ic in range(NC_):
                            ST("tensor", lambda e, ic=ic, g=g: e.matmul(
                                p_kv[:], s_wkb[:, g * 2048 + ic * 128:g * 2048 + (ic + 1) * 128],
                                s_xnh[ic][:], start=(ic == 0), stop=(ic == NC_ - 1)))
                        ST("vector", lambda e: e.tensor_copy(s_k[:], p_kv[:]))
                        ST("tensor", lambda e: e.matmul(p_rot[:], s_rT[:], s_k[:],
                                                        start=True, stop=True))
                        ST("vector", lambda e, t0=t0: e.tensor_tensor(
                            s_kr[:], s_k[:], s_cos[:, t0:t0 + TW], ALU.mult))
                        ST("vector", lambda e, t0=t0: e.tensor_tensor(
                            s_t1[:], p_rot[:], s_sin[:, t0:t0 + TW], ALU.mult))
                        ST("vector", lambda e: e.tensor_add(s_kr[:], s_kr[:], s_t1[:]))
                        ST("vector", lambda e: e.tensor_copy(s_khb[:], s_kr[:]))
                        ST("gpsimd", lambda e, g=g, tb=tb: e.dma_start(
                            out=kth_d[g, :, tb * 4:(tb + 1) * 4, :],
                            in_=s_khb[:].rearrange("p (j f) -> p j f", f=128)))
                        ST("vector", lambda e: e.tensor_tensor(s_t1[:], s_kr[:], s_kr[:], ALU.mult))
                        ST("tensor", lambda e: e.matmul(p_row[0:1, :], s_ones[:, 0:1],
                                                        s_t1[:], start=True, stop=True))
                        ST("vector", lambda e: e.reduce_max(
                            s_st[0:1, 4 * TW:4 * TW + 1], p_row[0:1, :], axis=AX.X))
                        if tb == 0:
                            ST("vector", lambda e, g=g: e.tensor_copy(
                                s_stats[0:1, g:g + 1], s_st[0:1, 4 * TW:4 * TW + 1]))
                        else:
                            ST("vector", lambda e, g=g: e.tensor_tensor(
                                s_stats[0:1, g:g + 1], s_stats[0:1, g:g + 1],
                                s_st[0:1, 4 * TW:4 * TW + 1], ALU.max))
                        for ic in range(NC_):
                            ST("tensor", lambda e, ic=ic, g=g: e.matmul(
                                p_kv[:], s_wvb[:, g * 2048 + ic * 128:g * 2048 + (ic + 1) * 128],
                                s_xnh[ic][:], start=(ic == 0), stop=(ic == NC_ - 1)))
                        ST("vector", lambda e: e.tensor_copy(s_vb16[:], p_kv[:]))
                        for j in range(TW // 128):
                            ST("tensor", lambda e, j=j: e.transpose(
                                p_trb[:], s_vb16[:, j * 128:(j + 1) * 128], s_identb[:]))
                            ST("vector", lambda e, j=j: e.tensor_copy(
                                s_vt[:, j * 136:j * 136 + 128], p_trb[:]))
                        ST("gpsimd", lambda e, g=g, tb=tb: e.dma_start(
                            out=vta_d[g, :, tb * 4:(tb + 1) * 4, :],
                            in_=s_vt[:].rearrange("p (j f) -> p j f", f=136)))
                    bg_pump(20)
                pr.flush()

        def phase_p3a():
            TW = 512
            with ExitStack() as es:
                sb = lambda n, s, dt=F32: es.enter_context(nc.sbuf_tensor(n, s, dt))
                ps = lambda n, s: es.enter_context(nc.psum_tensor(n, s, F32))
                s_xnh = [sb(f"p3a_xnh{c}", [128, TW], BF16) for c in range(NC_)]
                s_wst = sb("p3a_wst", [128, 16 * 128])
                s_wqb = sb("p3a_wqb", [128, 16 * 16 * 128], BF16)
                s_q = sb("p3a_q", [128, TW])
                s_qr = sb("p3a_qr", [128, TW])
                s_qh = sb("p3a_qh", [128, TW], BF16)
                s_t1 = sb("p3a_t1", [128, TW])
                s_m = sb("p3a_m", [1, 2])
                s_cos = sb("p3a_cos", [128, S])
                s_sin = sb("p3a_sin", [128, S])
                p_q = ps("p3a_q_p", [128, TW])
                p_rot = ps("p3a_rot", [128, TW])
                p_row = ps("p3a_row", [1, TW])
                ST("sync", lambda e: e.dma_start(out=s_cos[:], in_=cos_d))
                ST("sync", lambda e: e.dma_start(out=s_sin[:], in_=sin_d))
                for h in range(NH):
                    ST("sync", lambda e, h=h: e.dma_start(out=s_wst[:], in_=wqT_d[h]))
                    ST("vector", lambda e, h=h: e.tensor_copy(
                        s_wqb[:, h * 2048:(h + 1) * 2048], s_wst[:]))
                for tb in range(S // TW):
                    t0 = tb * TW
                    for c in range(NC_):
                        ST("sync", lambda e, c=c, t0=t0: e.dma_start(
                            out=s_xnh[c][:], in_=xnh_d[c * 128:(c + 1) * 128, t0:t0 + TW]))
                    for h in range(NH):
                        for ic in range(NC_):
                            ST("tensor", lambda e, ic=ic, h=h: e.matmul(
                                p_q[:], s_wqb[:, (h * 16 + ic) * 128:(h * 16 + ic + 1) * 128],
                                s_xnh[ic][:], start=(ic == 0), stop=(ic == NC_ - 1)))
                        ST("vector", lambda e: e.tensor_copy(s_q[:], p_q[:]))
                        ST("tensor", lambda e: e.matmul(p_rot[:], s_rT[:], s_q[:],
                                                        start=True, stop=True))
                        ST("vector", lambda e, t0=t0: e.tensor_tensor(
                            s_qr[:], s_q[:], s_cos[:, t0:t0 + TW], ALU.mult))
                        ST("vector", lambda e, t0=t0: e.tensor_tensor(
                            s_t1[:], p_rot[:], s_sin[:, t0:t0 + TW], ALU.mult))
                        ST("vector", lambda e: e.tensor_add(s_qr[:], s_qr[:], s_t1[:]))
                        ST("vector", lambda e: e.tensor_copy(s_qh[:], s_qr[:]))
                        ST("gpsimd", lambda e, h=h, t0=t0: e.dma_start(
                            out=qfh_d[h * 128:(h + 1) * 128, t0:t0 + TW], in_=s_qh[:]))
                        ST("vector", lambda e: e.tensor_tensor(s_t1[:], s_qr[:], s_qr[:], ALU.mult))
                        ST("tensor", lambda e: e.matmul(p_row[0:1, :], s_ones[:, 0:1],
                                                        s_t1[:], start=True, stop=True))
                        ST("vector", lambda e: e.reduce_max(s_m[0:1, 0:1], p_row[0:1, :], axis=AX.X))
                        if tb == 0:
                            ST("vector", lambda e, h=h: e.tensor_copy(
                                s_stats[0:1, 4 + h:5 + h], s_m[0:1, 0:1]))
                        else:
                            ST("vector", lambda e, h=h: e.tensor_tensor(
                                s_stats[0:1, 4 + h:5 + h], s_stats[0:1, 4 + h:5 + h],
                                s_m[0:1, 0:1], ALU.max))
                        bg_pump(1)
                pr.flush()

        def phase_p3b():
            QW = 512
            with ExitStack() as es:
                sb = lambda n, s, dt=F32: es.enter_context(nc.sbuf_tensor(n, s, dt))
                ps = lambda n, s, dt=F32: es.enter_context(nc.psum_tensor(n, s, dt))
                s_kb = [sb(f"p3b_kb{b}", [128, 32 * 128], BF16) for b in range(2)]
                s_vb = [sb(f"p3b_vb{b}", [128, 32 * 136], BF16) for b in range(2)]
                s_qh = [sb(f"p3b_qh{b}", [128, QW], BF16) for b in range(4)]
                s_e = [sb(f"p3b_e{b}", [128, QW], BF16) for b in range(2)]
                s_r = [sb(f"p3b_r{b}", [1, QW]) for b in range(2)]
                s_rb = [sb(f"p3b_rb{b}", [128, QW]) for b in range(2)]
                s_aob = [sb(f"p3b_aob{b}", [128, QW], BF16) for b in range(2)]
                s_sc = sb("p3b_sc", [1, 16])
                p_s = [ps(f"p3b_s{b}", [128, QW]) for b in range(3)]
                p_o = ps("p3b_o", [128, QW])
                p_row = ps("p3b_row", [1, QW])

                ST("scalar", lambda e: e.activation(s_sc[0:1, 0:4], s_stats[0:1, 0:4], AF.Sqrt))
                ST("scalar", lambda e: e.activation(s_stats[0:1, 20:36],
                                                    s_stats[0:1, 4:20], AF.Sqrt))
                for h in range(NH):
                    ST("vector", lambda e, h=h: e.tensor_tensor(
                        s_stats[0:1, 40 + h:41 + h], s_stats[0:1, 20 + h:21 + h],
                        s_sc[0:1, h // 4:h // 4 + 1], ALU.mult))
                ST("vector", lambda e: e.tensor_scalar(
                    s_stats[0:1, 40:56], s_stats[0:1, 40:56], -SCALE0, None, ALU.mult))
                ST("tensor", lambda e: e.matmul(p_o[:, 0:16], s_ones[0:1, :],
                                                s_stats[0:1, 40:56], start=True, stop=True))
                ST("scalar", lambda e: e.activation(s_nbias[:], p_o[:, 0:16], AF.Identity))
                pr.flush()

                entry = pr.k
                msc, mA, mexp = {}, {}, {}
                mkv, mq, mstr, mldr, mrec, mmul, mst = ({}, {}, {}, {}, {}, {}, {})

                def kv_load(gi):
                    g = gi % 4
                    buf = gi % 2
                    w = [("TICK", entry)] if gi < 2 else [("sMM", mA[(gi - 2) * 4 * 32 + 4 * 32 - 1])]
                    pr.raw("sync", lambda e, g=g, buf=buf: e.dma_start(
                        out=s_kb[buf][:], in_=kth_d[g].rearrange("p j f -> p (j f)")),
                        waits=w, inc="sD1")
                    mkv[gi] = pr.raw("sync", lambda e, g=g, buf=buf: e.dma_start(
                        out=s_vb[buf][:], in_=vta_d[g].rearrange("p j f -> p (j f)")),
                        waits=w, inc="sD1")

                def q_load(t):
                    qb, h = t // NH, t % NH
                    buf = t % 4
                    w = ([("TICK", entry)] if t < 4
                         else [("sMM", msc[(t - 4) * 32 + 31])])
                    mq[t] = pr.raw("sync", lambda e, h=h, qb=qb, buf=buf: e.dma_start(
                        out=s_qh[buf][:],
                        in_=qfh_d[h * 128:(h + 1) * 128, qb * QW:(qb + 1) * QW]),
                        waits=w, inc="sD0")

                NT = (S // QW) * NH
                kv_load(0); kv_load(1)
                q_load(0); q_load(1); q_load(2); q_load(3)

                def pe_score(t, j):
                    J = t * 32 + j
                    gi = t // 4
                    w = []
                    if J >= 2 and J % 2 == 0:
                        w.append(("sE2", mexp[J - 2]))
                    if j == 0:
                        w += [("sD0", mq[t]), ("sD1", mkv[gi])]
                    msc[J] = pr.raw("tensor", lambda e, j=j, t=t, gi=gi: e.matmul(
                        p_s[J % 3][:], s_kb[gi % 2][:, j * 128:(j + 1) * 128],
                        s_qh[t % 4][:], start=True, stop=True), waits=w, inc="sMM")

                def pe_denom_av(t, j):
                    J = t * 32 + j
                    gi = t // 4
                    w = [("sE2", mexp[J])]
                    if j == 0 and t >= 1:
                        w.append(("sGB", mmul[t - 1]))
                    pr.raw("tensor", lambda e, j=j, t=t: e.matmul(
                        p_row[0:1, :], s_onesb[:, 0:1], s_e[J % 2][:],
                        start=(j == 0), stop=(j == 31)), waits=w)
                    mA[J] = pr.raw("tensor", lambda e, j=j, t=t, gi=gi: e.matmul(
                        p_o[:], s_vb[gi % 2][:, j * 136:j * 136 + 128],
                        s_e[J % 2][:], start=(j == 0), stop=(j == 31)), inc="sMM")

                def sc_exp(t, j):
                    J = t * 32 + j
                    h = t % NH
                    mexp[J] = pr.raw("scalar", lambda e, J=J, h=h: e.activation(
                        s_e[J % 2][:], p_s[J % 3][:], AF.Exp,
                        bias=s_nbias[:, h:h + 1], scale=SCALE0),
                        waits=[("sMM", msc[J])], inc="sE2")

                def dve_recip(t):
                    w = [("sMM", mA[t * 32 + 31])]
                    if t >= 2:
                        w.append(("sLD", mstr[t - 2]))
                    mrec[t] = pr.raw("vector", lambda e, t=t: e.reciprocal(
                        s_r[t % 2][0:1, :], p_row[0:1, :]), waits=w, inc="sGB")

                def r_store(t):
                    w = [("sGB", mrec[t])]
                    if t >= 2:
                        w.append(("sD0", mldr[t - 2]))
                    mstr[t] = pr.raw("gpsimd", lambda e, t=t: e.dma_start(
                        out=rbq_d[t % 2:t % 2 + 1, :], in_=s_r[t % 2][0:1, :]),
                        waits=w, inc="sLD")

                def r_bcast(t):
                    w = [("sLD", mstr[t])]
                    if t >= 2:
                        w.append(("sGB", mmul[t - 2]))
                    mldr[t] = pr.raw("sync", lambda e, t=t: e.dma_start(
                        out=s_rb[t % 2][:],
                        in_=rbq_d[t % 2:t % 2 + 1, :].partition_broadcast(128).squeeze(1)),
                        waits=w, inc="sD0")

                def dve_mult(t):
                    w = [("sD0", mldr[t])]
                    if t >= 2:
                        w.append(("sLD", mst[t - 2]))
                    mmul[t] = pr.raw("vector", lambda e, t=t: e.tensor_tensor(
                        s_aob[t % 2][:], p_o[:], s_rb[t % 2][:], ALU.mult),
                        waits=w, inc="sGB")

                def ao_store(t):
                    qb, h = t // NH, t % NH
                    mst[t] = pr.raw("gpsimd", lambda e, h=h, qb=qb, t=t: e.dma_start(
                        out=aof_d[h * 128:(h + 1) * 128, qb * QW:(qb + 1) * QW],
                        in_=s_aob[t % 2][:]),
                        waits=[("sGB", mmul[t])], inc="sLD")

                for t in range(NT):
                    pe_score(t, 0); sc_exp(t, 0)
                    pe_score(t, 1); sc_exp(t, 1)
                    if t >= 1:
                        dve_mult(t - 1)
                        ao_store(t - 1)
                    for j in range(30):
                        pe_denom_av(t, j)
                        pe_score(t, j + 2)
                        sc_exp(t, j + 2)
                    pe_denom_av(t, 30)
                    pe_denom_av(t, 31)
                    dve_recip(t)
                    r_store(t)
                    r_bcast(t)
                    if t + 4 < NT:
                        q_load(t + 4)
                    if t + 2 < NT and (t + 2) % 4 == 0 and (t + 2) // 4 >= 2:
                        kv_load((t + 2) // 4)
                    bg_pump(2)
                dve_mult(NT - 1)
                ao_store(NT - 1)
                pr.barrier(("sMM", "sE2", "sGB", "sD0", "sD1", "sLD"))
                pr.flush()

        def phase_p3c():
            TW = 512
            with ExitStack() as es:
                sb = lambda n, s, dt=F32: es.enter_context(nc.sbuf_tensor(n, s, dt))
                ps = lambda n, s: es.enter_context(nc.psum_tensor(n, s, F32))
                s_ao = [sb(f"p3c_ao{c}", [128, TW], BF16) for c in range(NC_)]
                s_wst = sb("p3c_wst", [128, 16 * 128])
                s_wob = sb("p3c_wob", [128, 16 * 16 * 128], BF16)
                s_s = sb("p3c_s", [128, TW])
                p_s = ps("p3c_p", [128, TW])
                for oc in range(NC_):
                    ST("sync", lambda e, oc=oc: e.dma_start(out=s_wst[:], in_=woT_d[oc]))
                    ST("vector", lambda e, oc=oc: e.tensor_copy(
                        s_wob[:, oc * 2048:(oc + 1) * 2048], s_wst[:]))
                for tb in range(S // TW):
                    t0 = tb * TW
                    for c in range(NC_):
                        ST("sync", lambda e, c=c, t0=t0: e.dma_start(
                            out=s_ao[c][:], in_=aof_d[c * 128:(c + 1) * 128, t0:t0 + TW]))
                    for oc in range(NC_):
                        for ic in range(NC_):
                            ST("tensor", lambda e, ic=ic, oc=oc: e.matmul(
                                p_s[:], s_wob[:, (oc * 16 + ic) * 128:(oc * 16 + ic + 1) * 128],
                                s_ao[ic][:], start=(ic == 0), stop=(ic == NC_ - 1)))
                        ST("vector", lambda e: e.tensor_copy(s_s[:], p_s[:]))
                        ST("gpsimd", lambda e, oc=oc, t0=t0: e.dma_start(
                            out=sf_d[oc * 128:(oc + 1) * 128, t0:t0 + TW], in_=s_s[:]))
                        bg_pump(1)
                pr.flush()

        def phase_p4():
            TW = 512
            with ExitStack() as es:
                sb = lambda n, s, dt=F32: es.enter_context(nc.sbuf_tensor(n, s, dt))
                ps = lambda n, s: es.enter_context(nc.psum_tensor(n, s, F32))
                s_xh = [sb(f"p4_xh{c}", [128, TW], BF16) for c in range(NC_)]
                s_cb = [sb(f"p4_cb{q}", [128, 4 * TW]) for q in range(4)]
                s_cbh = [sb(f"p4_cbh{q}", [128, 4 * TW], BF16) for q in range(4)]
                s_wh = [sb(f"p4_wh{b}", [128, 2 * 16 * 128], BF16) for b in range(2)]
                s_xld = [sb(f"p4_xld{b}", [128, TW]) for b in range(4)]
                s_sld = [sb(f"p4_sld{b}", [128, TW]) for b in range(4)]
                s_yt = [sb(f"p4_yt{b}", [128, TW]) for b in range(2)]
                s_rw = sb("p4_rw", [128, NC_ * EXP])
                s_rwh = sb("p4_rwh", [128, NC_ * EXP], BF16)
                s_rbb = sb("p4_rbb", [128, EXP])
                s_ebT = sb("p4_ebT", [128, 16 * 8])
                s_obT = sb("p4_obT", [128, 16])
                s_eb8 = sb("p4_eb8", [8, DEXP])
                s_ob1 = sb("p4_ob1", [1, D])
                s_lg = sb("p4_lg", [128, 8])
                s_m = sb("p4_m", [128, 8])
                s_gt = sb("p4_gt", [128, 4 * 8])
                s_gf = sb("p4_gf", [8, TW])
                s_gbc = sb("p4_gbc", [128, TW])
                s_z = sb("p4_z", [128, 2 * TW])
                s_t1 = sb("p4_t1", [128, 2 * TW])
                s_t2 = sb("p4_t2", [128, 2 * TW])
                p_sm = ps("p4_sm", [128, 512])
                p_l = p_sm[:, 0:8]
                p_tr = p_sm[:, 128:256]
                p_e4 = [ps(f"p4_e4{b}", [128, 2 * TW]) for b in range(2)]
                gbn, bdn, mmn, e2n = {}, {}, {}, {}

                ST("sync", lambda e: e.dma_start(out=s_rw[:], in_=rw_d))
                ST("vector", lambda e: e.tensor_copy(s_rwh[:], s_rw[:]))
                ST("sync", lambda e: e.dma_start(out=s_lg[0:1, :], in_=rb_d.unsqueeze(0)))
                bcast_row(p_sm, s_lg[0:1, 0:8], 8)
                ST("scalar", lambda e: e.activation(s_rbb[:], p_l, AF.Identity))
                ST("sync", lambda e: e.dma_start(out=s_eb8[:], in_=eb_d))
                ST("sync", lambda e: e.dma_start(out=s_ob1[:], in_=ob_d.unsqueeze(0)))
                for dc in range(16):
                    ST("tensor", lambda e, dc=dc: e.transpose(
                        p_tr[:, 0:8], s_eb8[:, dc * 128:(dc + 1) * 128], s_ident[0:8, 0:8]))
                    ST("vector", lambda e, dc=dc: e.tensor_copy(
                        s_ebT[:, dc * 8:(dc + 1) * 8], p_tr[:, 0:8]))
                    ST("tensor", lambda e, dc=dc: e.transpose(
                        p_tr[:, 0:1], s_ob1[:, dc * 128:(dc + 1) * 128], s_ident[0:1, 0:1]))
                    ST("vector", lambda e, dc=dc: e.tensor_copy(
                        s_obT[:, dc:dc + 1], p_tr[:, 0:1]))

                for tb in range(S // TW):
                    t0 = tb * TW
                    for c in range(NC_):
                        ST("sync", lambda e, c=c, t0=t0: e.dma_start(
                            out=s_xh[c][:], in_=xnh_d[c * 128:(c + 1) * 128, t0:t0 + TW]))
                    for sub in range(4):
                        for ic in range(NC_):
                            ST("tensor", lambda e, ic=ic, sub=sub: e.matmul(
                                p_l[:], s_xh[ic][:, sub * 128:(sub + 1) * 128],
                                s_rwh[:, ic * 8:(ic + 1) * 8],
                                start=(ic == 0),
                                stop=(ic == NC_ - 1)))
                        ST("vector", lambda e: e.tensor_tensor(s_lg[:], p_l[:], s_rbb[:], ALU.add))
                        ST("vector", lambda e: e.reduce_max(s_m[:, 0:1], s_lg[:], axis=AX.X))
                        ST("vector", lambda e, sub=sub: e.tensor_scalar(
                            s_gt[:, sub * 8:(sub + 1) * 8], s_lg[:], s_m[:, 0:1],
                            None, ALU.is_equal))
                        ST("vector", lambda e, sub=sub: e.scalar_tensor_tensor(
                            s_lg[:], s_gt[:, sub * 8:(sub + 1) * 8], -1e30, s_lg[:],
                            ALU.mult, ALU.add))
                        ST("vector", lambda e: e.reduce_max(s_m[:, 1:2], s_lg[:], axis=AX.X))
                        ST("vector", lambda e: e.tensor_scalar(
                            s_m[:, 4:5], s_m[:, 1:2], -1.0, None, ALU.mult))
                        ST("scalar", lambda e: e.activation(
                            s_m[:, 2:3], s_m[:, 0:1], AF.Sigmoid, bias=s_m[:, 4:5], scale=1.0))
                        ST("vector", lambda e: e.tensor_scalar(
                            s_m[:, 5:6], s_m[:, 0:1], -1.0, None, ALU.mult))
                        ST("scalar", lambda e: e.activation(
                            s_m[:, 3:4], s_m[:, 1:2], AF.Sigmoid, bias=s_m[:, 5:6], scale=1.0))
                        ST("vector", lambda e, sub=sub: e.tensor_scalar(
                            s_gt[:, sub * 8:(sub + 1) * 8], s_gt[:, sub * 8:(sub + 1) * 8],
                            s_m[:, 2:3], None, ALU.mult))
                        ST("vector", lambda e: e.tensor_scalar(
                            s_lg[:], s_lg[:], s_m[:, 1:2], None, ALU.is_equal))
                        ST("vector", lambda e: e.tensor_scalar(
                            s_lg[:], s_lg[:], s_m[:, 3:4], None, ALU.mult))
                        ST("vector", lambda e, sub=sub: e.tensor_add(
                            s_gt[:, sub * 8:(sub + 1) * 8], s_gt[:, sub * 8:(sub + 1) * 8],
                            s_lg[:]))
                        ST("tensor", lambda e, sub=sub: e.transpose(
                            p_tr[0:8, 0:128], s_gt[:, sub * 8:(sub + 1) * 8], s_ident[:]))
                        ST("vector", lambda e, sub=sub: e.tensor_copy(
                            s_gf[0:8, sub * 128:(sub + 1) * 128], p_tr[0:8, 0:128]))
                    ST("gpsimd", lambda e: e.dma_start(out=gb_d, in_=s_gf[:]))

                    entry = pr.k
                    gbn[0] = pr.raw("sync", lambda e: e.dma_start(
                        out=s_gbc[:],
                        in_=gb_d[0:1, :].partition_broadcast(128).squeeze(1)),
                        waits=[("TICK", entry)], inc="sGB")

                    groups = ([("exp", ex, dh) for ex in range(EXP) for dh in range(8)]
                              + [("out", 0, oh) for oh in range(8)])
                    NG = len(groups)
                    mxs, msty = {}, {}

                    def emit_band(li):
                        gg = tb * NG + li
                        kind, ex, dh = groups[li]
                        buf = gg % 2
                        sd = "sD%d" % buf
                        if kind == "exp":
                            hi_src = ewh_d[ex, dh * 2:(dh + 1) * 2]
                        else:
                            hi_src = owh_d[dh * 2:(dh + 1) * 2]
                        w = ([("sMM", mmn[gg - 2])] if (gg - 2) in mmn
                             else [("TICK", entry)])
                        pr.raw("sync", lambda e, hi_src=hi_src, buf=buf: e.dma_start(
                            out=s_wh[buf][:, 0:2048], in_=hi_src[0]), waits=w, inc=sd)
                        bdn[gg] = pr.raw(
                            "gpsimd", lambda e, hi_src=hi_src, buf=buf: e.dma_start(
                                out=s_wh[buf][:, 2048:4096], in_=hi_src[1]),
                            waits=w, inc=sd)

                    def emit_xs(oc):
                        b = oc % 4
                        w = [("TICK", entry)] if oc < 4 else [("sLD", msty[oc - 4])]
                        pr.raw("sync", lambda e, oc=oc, b=b, t0=t0: e.dma_start(
                            out=s_xld[b][:], in_=xt_d[oc * 128:(oc + 1) * 128, t0:t0 + TW]),
                            waits=w, inc="sLD")
                        mxs[oc] = pr.raw("sync", lambda e, oc=oc, b=b, t0=t0: e.dma_start(
                            out=s_sld[b][:], in_=sf_d[oc * 128:(oc + 1) * 128, t0:t0 + TW]),
                            waits=w, inc="sLD")

                    def EL(eng, fn, extra=()):
                        pr.raw(eng, fn,
                               waits=[("sE2", pr.cnt["sE2"])] + list(extra), inc="sE2")

                    emit_band(0)
                    emit_band(1)
                    for li, (kind, ex, dh) in enumerate(groups):
                        gg = tb * NG + li
                        buf = gg % 2
                        sd = "sD%d" % buf
                        if kind == "exp":
                            xs = s_xh
                        else:
                            xs = s_cbh
                        w = [("TICK", entry), ("sD%d" % buf, bdn[gg])]
                        if (gg - 2) in e2n:
                            w.append(("sE2", e2n[gg - 2]))
                        if kind == "out" and dh == 0:
                            w.append(("sE2", e2n[gg - 1]))
                        first = True
                        for dcl in range(2):
                            for ic in range(NC_):
                                if kind == "exp":
                                    rhs_fn = lambda xs=xs, ic=ic: xs[ic][:]
                                else:
                                    rhs_fn = (lambda xs=xs, ic=ic:
                                              xs[ic // 4][:, (ic % 4) * TW:(ic % 4 + 1) * TW])
                                inc = None
                                if dcl == 1 and ic == NC_ - 1:
                                    inc = "sMM"
                                pr.raw("tensor", lambda e, dcl=dcl,
                                       buf=buf, rhs_fn=rhs_fn, ic=ic: e.matmul(
                                           p_e4[buf][:, dcl * TW:(dcl + 1) * TW],
                                           s_wh[buf][:, (dcl * 16 + ic) * 128:(dcl * 16 + ic + 1) * 128],
                                           rhs_fn(), start=(ic == 0),
                                           stop=(ic == NC_ - 1)),
                                       waits=w if first else (), inc=inc)
                                first = False
                        mmn[gg] = pr.cnt["sMM"]
                        if li + 2 < NG:
                            emit_band(li + 2)
                        if kind == "exp":
                            for dcl in range(2):
                                dc = dh * 2 + dcl
                                EL("scalar", lambda e, dcl=dcl, dc=dc, ex=ex, buf=buf:
                                   e.activation(
                                       s_z[:, dcl * TW:(dcl + 1) * TW],
                                       p_e4[buf][:, dcl * TW:(dcl + 1) * TW], AF.Identity,
                                       bias=s_ebT[:, dc * 8 + ex:dc * 8 + ex + 1]),
                                   extra=[("sMM", mmn[gg])])
                            EL("scalar", lambda e: e.activation(s_t1[:], s_z[:], AF.Square))
                            EL("vector", lambda e: e.tensor_scalar(
                                s_t1[:], s_t1[:], GELU_A, 1.0, ALU.mult, ALU.add))
                            EL("vector", lambda e: e.tensor_tensor(
                                s_t1[:], s_t1[:], s_z[:], ALU.mult))
                            EL("scalar", lambda e: e.activation(
                                s_t1[:], s_t1[:], AF.Sigmoid, scale=GELU_C2))
                            EL("vector", lambda e: e.tensor_tensor(
                                s_t1[:], s_t1[:], s_z[:], ALU.mult))
                            q4, r4 = (dh * 2) // 4, (dh * 2) % 4
                            for dcl in range(2):
                                tgt = (lambda e, q4=q4, r4=r4, dcl=dcl: e.tensor_tensor(
                                    s_cb[q4][:, (r4 + dcl) * TW:(r4 + dcl + 1) * TW],
                                    s_t1[:, dcl * TW:(dcl + 1) * TW], s_gbc[:], ALU.mult)
                                ) if ex == 0 else (
                                    lambda e, dcl=dcl: e.tensor_tensor(
                                        s_t2[:, dcl * TW:(dcl + 1) * TW],
                                        s_t1[:, dcl * TW:(dcl + 1) * TW], s_gbc[:], ALU.mult))
                                EL("vector", tgt,
                                   extra=[("sGB", gbn[ex])] if dcl == 0 else ())
                            if ex != 0:
                                EL("vector", lambda e, q4=q4, r4=r4: e.tensor_add(
                                    s_cb[q4][:, r4 * TW:(r4 + 2) * TW],
                                    s_cb[q4][:, r4 * TW:(r4 + 2) * TW], s_t2[:]))
                            if dh == 7:
                                if ex + 1 < EXP:
                                    gbn[ex + 1] = pr.raw(
                                        "sync", lambda e, ex=ex: e.dma_start(
                                            out=s_gbc[:],
                                            in_=gb_d[ex + 1:ex + 2, :]
                                            .partition_broadcast(128).squeeze(1)),
                                        waits=[("sE2", pr.cnt["sE2"])], inc="sGB")
                                else:
                                    for q in range(4):
                                        EL("vector", lambda e, q=q: e.tensor_copy(
                                            s_cbh[q][:], s_cb[q][:]))
                        else:
                            if dh == 0:
                                for oc4 in range(4):
                                    emit_xs(oc4)
                            for ocl in range(2):
                                oc = dh * 2 + ocl
                                EL("scalar", lambda e, ocl=ocl, oc=oc, buf=buf: e.activation(
                                    s_z[:, ocl * TW:(ocl + 1) * TW],
                                    p_e4[buf][:, ocl * TW:(ocl + 1) * TW],
                                    AF.Silu, bias=s_obT[:, oc:oc + 1]),
                                   extra=[("sMM", mmn[gg])] if ocl == 0 else ())
                                ex2 = [("sLD", mxs[oc])]
                                if oc >= 2:
                                    ex2.append(("sLD", msty[oc - 2]))
                                EL("vector", lambda e, ocl=ocl, oc=oc: e.tensor_tensor(
                                    s_yt[oc % 2][:], s_z[:, ocl * TW:(ocl + 1) * TW],
                                    s_sld[oc % 4][:], ALU.mult), extra=ex2)
                                EL("vector", lambda e, oc=oc: e.tensor_add(
                                    s_yt[oc % 2][:], s_yt[oc % 2][:], s_xld[oc % 4][:]))
                                msty[oc] = pr.raw(
                                    "gpsimd", lambda e, oc=oc, t0=t0: e.dma_start(
                                        out=y1f_d[oc * 128:(oc + 1) * 128, t0:t0 + TW],
                                        in_=s_yt[oc % 2][:]),
                                    waits=[("sE2", pr.cnt["sE2"])], inc="sLD")
                                if oc + 4 < NC_:
                                    emit_xs(oc + 4)
                        e2n[gg] = pr.cnt["sE2"]
                    pr.barrier(("sMM", "sE2", "sGB", "sD0", "sD1", "sLD"))
                pr.flush()

        def phase_p5():
            TW = 512
            with ExitStack() as es:
                sb = lambda n, s, dt=F32: es.enter_context(nc.sbuf_tensor(n, s, dt))
                ps = lambda n, s: es.enter_context(nc.psum_tensor(n, s, F32))
                s_xn = [sb(f"p5_xn{c}", [128, TW]) for c in range(NC_)]
                s_xh = [sb(f"p5_xh{c}", [128, TW], BF16) for c in range(NC_)]
                s_hh = [sb(f"p5_hh{o}", [128, TW], BF16) for o in range(NOC1)]
                s_kv = [sb(f"p5_kv{g}", [128, TW]) for g in range(4)]
                s_selv = [sb(f"p5_sv{g}", [128, TW]) for g in range(4)]
                s_wst = sb("p5_wst", [128, 16 * 128])
                s_wkvb = sb("p5_wkvb", [128, 4 * 16 * 128], BF16)
                s_q1 = [sb(f"p5_q1{b}", [128, 16 * 128], BF16) for b in range(2)]
                s_q2 = [sb(f"p5_q2{b}", [128, 16 * 128], BF16) for b in range(2)]
                s_q3 = [sb(f"p5_q3{b}", [128, NOC1 * 128], BF16) for b in range(2)]
                s_bkvT = sb("p5_bkvT", [128, 4])
                s_bkv1 = sb("p5_bkv1", [1, KVF])
                s_t1 = [sb(f"p5_t1{b}", [128, TW]) for b in range(2)]
                s_t3 = [sb(f"p5_t3{b}", [128, TW]) for b in range(2)]
                s_sq = sb("p5_sq", [128, TW])
                s_st = sb("p5_st", [1, 4 * TW + 8])
                p_a = [ps(f"p5_a{b}", [128, TW]) for b in range(2)]
                p_b2 = [ps(f"p5_b2{b}", [128, TW]) for b in range(2)]
                p_row = ps("p5_row", [1, TW])
                p_bc = ps("p5_bc", [128, TW])
                p_tr = ps("p5_tr", [128, 128])

                ST("sync", lambda e: e.dma_start(out=s_bkv1[:], in_=bkv_d.unsqueeze(0)))
                for g in range(4):
                    ST("tensor", lambda e, g=g: e.transpose(
                        p_tr[:, 0:1], s_bkv1[:, g * 128:(g + 1) * 128], s_ident[0:1, 0:1]))
                    ST("vector", lambda e, g=g: e.tensor_copy(s_bkvT[:, g:g + 1], p_tr[:, 0:1]))
                for g in range(4):
                    ST("sync", lambda e, g=g: e.dma_start(out=s_wst[:], in_=wkv_d[g]))
                    ST("vector", lambda e, g=g: e.tensor_copy(
                        s_wkvb[:, g * 2048:(g + 1) * 2048], s_wst[:]))

                for tb in range(S // TW):
                    t0 = tb * TW
                    for c in range(NC_):
                        ST("sync", lambda e, c=c, t0=t0: e.dma_start(
                            out=s_xn[c][:], in_=y1f_d[c * 128:(c + 1) * 128, t0:t0 + TW]))
                    emit_layernorm(s_xn, s_xn, p_row, p_bc, s_sq, s_st, TW)
                    for c in range(NC_):
                        ST("vector", lambda e, c=c: e.tensor_copy(s_xh[c][:], s_xn[c][:]))
                    for g in range(4):
                        for ic in range(NC_):
                            ST("tensor", lambda e, ic=ic, g=g: e.matmul(
                                p_a[0][:], s_wkvb[:, g * 2048 + ic * 128:g * 2048 + (ic + 1) * 128],
                                s_xh[ic][:], start=(ic == 0), stop=(ic == NC_ - 1)))
                        ST("scalar", lambda e, g=g: e.activation(
                            s_kv[g][:], p_a[0][:], AF.Identity, bias=s_bkvT[:, g:g + 1]))
                        ST("tensor", lambda e, g=g: e.matmul(
                            p_b2[0][:], s_selT[:], s_kv[g][:], start=True, stop=True))
                        ST("vector", lambda e, g=g: e.tensor_copy(s_selv[g][:], p_b2[0][:]))
                    for c in range(NC_):
                        ST("sync", lambda e, c=c, t0=t0: e.dma_start(
                            out=s_xn[c][:], in_=y1f_d[c * 128:(c + 1) * 128, t0:t0 + TW]))
                    pr.flush()

                    entry = pr.k
                    mld1, mld2, mld3a, mld3b = {}, {}, {}, {}
                    mmA, mmB, mm3, msig, msil, mmul5, madd, mst3 = ({}, {}, {},
                                                                    {}, {}, {}, {}, {})

                    def ld12(oc):
                        b = oc % 2
                        w1 = ([("TICK", entry)] if oc < 2 else [("sMM", mmA[oc - 2])])
                        w2 = ([("TICK", entry)] if oc < 2 else [("sMM", mmB[oc - 2])])
                        mld1[oc] = pr.raw("sync", lambda e, oc=oc, b=b: e.dma_start(
                            out=s_q1[b][:], in_=q1b_d[oc]), waits=w1, inc="sD0")
                        mld2[oc] = pr.raw("gpsimd", lambda e, oc=oc, b=b: e.dma_start(
                            out=s_q2[b][:], in_=q2b_d[oc]), waits=w2, inc="sD1")

                    def ld3(fc):
                        b = fc % 2
                        HC = (NOC1 // 2) * 128
                        w = ([("TICK", entry)] if fc < 2 else [("sMM", mm3[fc - 2])])
                        mld3a[fc] = pr.raw("sync", lambda e, fc=fc, b=b: e.dma_start(
                            out=s_q3[b][:, 0:HC], in_=q3b_d[fc, :, 0:HC]),
                            waits=w, inc="sD0")
                        mld3b[fc] = pr.raw("gpsimd", lambda e, fc=fc, b=b: e.dma_start(
                            out=s_q3[b][:, HC:2 * HC], in_=q3b_d[fc, :, HC:2 * HC]),
                            waits=w, inc="sD1")

                    def pe12(oc):
                        b = oc % 2
                        w = [("sD0", mld1[oc])]
                        if oc >= 2:
                            w += [("sE2", msig[oc - 2]), ("sGB", mmul5[oc - 2])]
                        if oc == 0:
                            w.append(("TICK", entry))
                        for ic in range(NC_):
                            mmA[oc] = pr.raw("tensor", lambda e, ic=ic, b=b: e.matmul(
                                p_a[b][:], s_q1[b][:, ic * 128:(ic + 1) * 128], s_xh[ic][:],
                                start=(ic == 0), stop=(ic == NC_ - 1)),
                                waits=w if ic == 0 else (),
                                inc="sMM" if ic == NC_ - 1 else None)
                        w2 = [("sD1", mld2[oc])]
                        for ic in range(NC_):
                            mmB[oc] = pr.raw("tensor", lambda e, ic=ic, b=b: e.matmul(
                                p_b2[b][:], s_q2[b][:, ic * 128:(ic + 1) * 128], s_xh[ic][:],
                                start=(ic == 0), stop=(ic == NC_ - 1)),
                                waits=w2 if ic == 0 else (),
                                inc="sMM" if ic == NC_ - 1 else None)

                    def tail12(oc):
                        b = oc % 2
                        w = [("sMM", mmA[oc])]
                        if oc >= 2:
                            w.append(("sGB", mmul5[oc - 2]))
                        msig[oc] = pr.raw("scalar", lambda e, b=b: e.activation(
                            s_t1[b][:], p_a[b][:], AF.Sigmoid), waits=w, inc="sE2")
                        mmul5[oc] = pr.raw("vector", lambda e, oc=oc, b=b: e.tensor_tensor(
                            s_hh[oc][:], s_t1[b][:], p_b2[b][:], ALU.mult),
                            waits=[("sE2", msig[oc]), ("sMM", mmB[oc])], inc="sGB")

                    def pe3(fc):
                        b = fc % 2
                        w = [("sD0", mld3a[fc]), ("sD1", mld3b[fc])]
                        if fc < 2:
                            w.append(("sGB", mmul5[NOC1 - 1]))
                        else:
                            w.append(("sE2", msil[fc - 2]))
                        for ic in range(NOC1):
                            mm3[fc] = pr.raw("tensor", lambda e, ic=ic, b=b: e.matmul(
                                p_a[b][:], s_q3[b][:, ic * 128:(ic + 1) * 128], s_hh[ic][:],
                                start=(ic == 0), stop=(ic == NOC1 - 1)),
                                waits=w if ic == 0 else (),
                                inc="sMM" if ic == NOC1 - 1 else None)

                    def tail3(fc):
                        b = fc % 2
                        w = [("sMM", mm3[fc])]
                        if fc >= 2:
                            w.append(("sGB", madd[fc - 2]))
                        msil[fc] = pr.raw("scalar", lambda e, b=b: e.activation(
                            s_t1[b][:], p_a[b][:], AF.Silu), waits=w, inc="sE2")
                        pr.raw("vector", lambda e, fc=fc, b=b: e.tensor_tensor(
                            s_t1[b][:], s_t1[b][:], s_selv[fc // 4][:], ALU.mult),
                            waits=[("sE2", msil[fc])], inc="sGB")
                        w3 = []
                        if fc >= 2:
                            w3.append(("sLD", mst3[fc - 2]))
                        madd[fc] = pr.raw("vector", lambda e, fc=fc, b=b: e.tensor_add(
                            s_t3[b][:], s_xn[fc][:], s_t1[b][:]), waits=w3, inc="sGB")
                        mst3[fc] = pr.raw("gpsimd", lambda e, fc=fc, b=b, t0=t0: e.dma_start(
                            out=out_d[fc * 128:(fc + 1) * 128, t0:t0 + TW], in_=s_t3[b][:]),
                            waits=[("sGB", madd[fc])], inc="sLD")

                    ld12(0); ld12(1)
                    for oc in range(NOC1):
                        pe12(oc)
                        if oc + 2 < NOC1:
                            ld12(oc + 2)
                        elif oc + 2 < NOC1 + 2:
                            ld3(oc + 2 - NOC1)
                        tail12(oc)
                    for fc in range(NC_):
                        pe3(fc)
                        if fc + 2 < NC_:
                            ld3(fc + 2)
                        tail3(fc)
                    pr.barrier(("sMM", "sE2", "sGB", "sD0", "sD1", "sLD"))
                pr.flush()

        with nc.named_scope("p2"):
            phase_p2()
        with nc.named_scope("p3a"):
            phase_p3a()
        with nc.named_scope("rg"):
            emit_rgam()
        with nc.named_scope("p3b"):
            phase_p3b()
        with nc.named_scope("p3c"):
            phase_p3c()
        with nc.named_scope("ptd"):
            bg_pump(len(BG))
            pr.barrier(("sPA", "sPB", "sPC"))
            pr.flush()
        es_pt.close()
        with nc.named_scope("p4"):
            phase_p4()
        with nc.named_scope("p5"):
            phase_p5()

        dump_srcs = {"s": sf_d, "y1": y1f_d}
        for name, ap in dump_aps.items():
            if name == "stats":
                ST("gpsimd", lambda e, ap=ap: e.dma_start(out=ap, in_=s_stats[0:1, :]))
            else:
                ST("gpsimd", lambda e, ap=ap, name=name: e.dma_start(
                    out=ap, in_=dump_srcs[name]))
        pr.flush()

        pr.barrier(list(sems))
        pr.flush()
        final = pr.k
        block.sync(lambda e: e.wait_ge(tick, 16 * final))
    return nc


def _tile_wT(w, noc, nic):
    return np.ascontiguousarray(
        w.reshape(noc, 128, nic, 128).transpose(0, 3, 2, 1).reshape(noc, 128, nic * 128))


def _tile_w(w, nic, noc):
    return np.ascontiguousarray(
        w.reshape(nic, 128, noc, 128).transpose(2, 1, 0, 3).reshape(noc, 128, nic * 128))


def kernel(L0_attn_x, L0_attn_Wq, L0_attn_Wk, L0_attn_Wv, L0_attn_Wo,
           L0_ffn_router_w, L0_ffn_router_b, L0_ffn_expert_w, L0_ffn_expert_b,
           L0_ffn_out_w, L0_ffn_out_b, L1_attn_Wq, L1_attn_bq, L1_attn_Wkv,
           L1_attn_bkv, L1_ffn_w1, L1_ffn_w2, L1_ffn_w3):
    nc = build_program()
    f32 = lambda a: np.asarray(a, dtype=np.float32)
    ew = f32(L0_ffn_expert_w)
    ew_t = np.stack([_tile_w(ew[e], 16, 16) for e in range(EXP)])
    rw = f32(L0_ffn_router_w)
    rw_t = np.ascontiguousarray(rw.reshape(16, 128, EXP).transpose(1, 0, 2).reshape(128, 16 * EXP))
    shared = {
        "wqT": _tile_wT(f32(L0_attn_Wq), 16, 16),
        "wkT": _tile_wT(f32(L0_attn_Wk), 4, 16),
        "wvT": _tile_wT(f32(L0_attn_Wv), 4, 16),
        "woT": _tile_wT(f32(L0_attn_Wo), 16, 16),
        "rw": rw_t, "rb": np.ascontiguousarray(f32(L0_ffn_router_b)),
        "ew": ew_t, "eb": np.ascontiguousarray(f32(L0_ffn_expert_b)),
        "ow": _tile_w(f32(L0_ffn_out_w), 16, 16),
        "ob": np.ascontiguousarray(f32(L0_ffn_out_b)),
        "wkv": _tile_w(f32(L1_attn_Wkv), 16, 4),
        "bkv": np.ascontiguousarray(f32(L1_attn_bkv)),
        "wq1": np.ascontiguousarray(f32(L1_attn_Wq)),
        "bq1": np.ascontiguousarray(f32(L1_attn_bq)),
        "w1T": _tile_wT(f32(L1_ffn_w1), NOC1, 16),
        "w2T": _tile_wT(f32(L1_ffn_w2), NOC1, 16),
        "w3T": _tile_wT(f32(L1_ffn_w3), 16, NOC1),
    }
    xs = f32(L0_attn_x)
    in_maps = [dict(shared, xt=np.ascontiguousarray(xs[i].T)) for i in range(8)]
    res = run_bass_kernel_spmd(nc, in_maps, core_ids=list(range(8)))
    return np.stack([np.ascontiguousarray(np.asarray(r["out"], dtype=np.float32).T)
                     for r in res.results])


# version 20
# speedup vs baseline: 1.1063x; 1.0170x over previous
import math
from contextlib import ExitStack

import numpy as np
import concourse.bass as bass
import concourse.mybir as mybir
from concourse.bass_utils import run_bass_kernel_spmd

F32 = mybir.dt.float32
BF16 = mybir.dt.bfloat16
AF = mybir.ActivationFunctionType
ALU = mybir.AluOpType
AX = mybir.AxisListType

S, D = 4096, 2048
NH, NKV, HD = 16, 4, 128
KVF = NKV * HD
EXP, DEXP = 8, 2048
H1 = 5632
NC_ = D // 128
NOC1 = H1 // 128
SCALE0 = float(HD) ** -0.5
GELU_C2 = 2.0 * math.sqrt(2.0 / math.pi)
GELU_A = 0.044715
LN_EPS = 1e-5
Q_EPS = 1e-8

DEBUG_DUMPS = ()
PHASES = ("p2", "p3a", "p3b", "p3c", "p4", "p5")


def _consts():
    ident = np.eye(128, dtype=np.float32)
    r = np.zeros((128, 128), np.float32)
    for d in range(64):
        r[d, d + 64] = -1.0
        r[64 + d, d] = 1.0
    rT = np.ascontiguousarray(r.T)
    selT = np.zeros((128, 128), np.float32)
    for p in range(128):
        selT[64 + (p % 64), p] = 1.0
    sel8 = np.zeros((8, 8 * 128), np.float32)
    for e in range(8):
        sel8[e, e * 128:(e + 1) * 128] = 1.0
    j = np.arange(64, dtype=np.float64)
    inv_freq = 10000.0 ** (-2.0 * j / 128.0)
    t = np.arange(S, dtype=np.float64)
    freqs = np.outer(t, inv_freq)
    emb = np.concatenate([freqs, freqs], 1)
    cos_f = np.ascontiguousarray(np.cos(emb).T.astype(np.float32))
    sin_f = np.ascontiguousarray(np.sin(emb).T.astype(np.float32))
    return ident, rT, selT, sel8, cos_f, sin_f


class Prog:

    ENGS = ("sync", "gpsimd", "vector", "scalar", "tensor")
    SKIP_OK = ("tensor", "sync", "gpsimd")

    def __init__(self, nc, block, tick, sems):
        self.nc, self.block, self.tick = nc, block, tick
        self.sems = sems
        self.cnt = {n: 0 for n in sems}
        self.k = 0
        self._teng = None
        self.pending = []

    def step(self, eng, fn, after=None):
        idx = self.k
        self.k += 1
        prev_eng = self._teng
        self._teng = eng
        if after is None:
            wait = None if (prev_eng == eng and eng in self.SKIP_OK) else idx
        else:
            wait = after
        waits = [] if wait is None else [("TICK", wait)]
        self.pending.append((eng, fn, waits, "TICK"))
        return idx

    def raw(self, eng, fn, waits=(), inc=None):
        if inc is not None:
            self.cnt[inc] += 1
        self.pending.append((eng, fn, list(waits), inc))
        return self.cnt[inc] if inc is not None else None

    def barrier(self, names, eng="sync"):
        waits = [("TICK", self.k)] + [(n, self.cnt[n]) for n in names]
        idx = self.k
        self.k += 1
        self._teng = eng
        self.pending.append((eng, lambda e: e.nop(), waits, "TICK"))
        return idx

    def flush(self):
        pend, self.pending = self.pending, []
        self._teng = None
        if not pend:
            return
        for name in self.ENGS:
            sel = [(fn, ws, inc) for (e, fn, ws, inc) in pend if e == name]
            if not sel:
                continue

            def body(eng, sel=sel):
                for fn, ws, inc in sel:
                    for sn, cv in ws:
                        sem = self.tick if sn == "TICK" else self.sems[sn]
                        eng.wait_ge(sem, 16 * cv)
                    inst = fn(eng)
                    if inc == "TICK":
                        inst.then_inc(self.tick, 16)
                    elif inc is not None:
                        inst.then_inc(self.sems[inc], 16)

            getattr(self.block, name)(body)


def build_program(phases=PHASES, dumps=DEBUG_DUMPS):
    nc = bass.Bass("TRN2", target_bir_lowering=False, debug=False)
    ident_np, rT_np, selT_np, sel8_np, cos_np, sin_np = _consts()

    xt_d = nc.dram_tensor("xt", [D, S], F32, kind="ExternalInput").ap()
    wqT_d = nc.dram_tensor("wqT", [16, 128, 16 * 128], F32, kind="ExternalInput").ap()
    wkT_d = nc.dram_tensor("wkT", [4, 128, 16 * 128], F32, kind="ExternalInput").ap()
    wvT_d = nc.dram_tensor("wvT", [4, 128, 16 * 128], F32, kind="ExternalInput").ap()
    woT_d = nc.dram_tensor("woT", [16, 128, 16 * 128], F32, kind="ExternalInput").ap()
    rw_d = nc.dram_tensor("rw", [128, 16 * EXP], F32, kind="ExternalInput").ap()
    rb_d = nc.dram_tensor("rb", [EXP], F32, kind="ExternalInput").ap()
    ew_d = nc.dram_tensor("ew", [EXP, 16, 128, 16 * 128], F32, kind="ExternalInput").ap()
    eb_d = nc.dram_tensor("eb", [EXP, DEXP], F32, kind="ExternalInput").ap()
    ow_d = nc.dram_tensor("ow", [16, 128, 16 * 128], F32, kind="ExternalInput").ap()
    ob_d = nc.dram_tensor("ob", [D], F32, kind="ExternalInput").ap()
    wkv_d = nc.dram_tensor("wkv", [4, 128, 16 * 128], F32, kind="ExternalInput").ap()
    bkv_d = nc.dram_tensor("bkv", [KVF], F32, kind="ExternalInput").ap()
    nc.dram_tensor("wq1", [D, D], F32, kind="ExternalInput")
    nc.dram_tensor("bq1", [D], F32, kind="ExternalInput")
    w1T_d = nc.dram_tensor("w1T", [NOC1, 128, 16 * 128], F32, kind="ExternalInput").ap()
    w2T_d = nc.dram_tensor("w2T", [NOC1, 128, 16 * 128], F32, kind="ExternalInput").ap()
    w3T_d = nc.dram_tensor("w3T", [16, 128, NOC1 * 128], F32, kind="ExternalInput").ap()
    out_d = nc.dram_tensor("out", [D, S], F32, kind="ExternalOutput").ap()

    ident_d = nc.inline_tensor(ident_np, "c_ident").ap()
    rT_d = nc.inline_tensor(rT_np, "c_rT").ap()
    selT_d = nc.inline_tensor(selT_np, "c_selT").ap()
    sel8_d = nc.inline_tensor(sel8_np, "c_sel8").ap()
    cos_d = nc.inline_tensor(cos_np, "c_cos").ap()
    sin_d = nc.inline_tensor(sin_np, "c_sin").ap()

    q1b_d = nc.dram_tensor("q1b", [NOC1, 128, 16 * 128], BF16).ap()
    q2b_d = nc.dram_tensor("q2b", [NOC1, 128, 16 * 128], BF16).ap()
    q3b_d = nc.dram_tensor("q3b", [16, 128, NOC1 * 128], BF16).ap()
    xnh_d = nc.dram_tensor("xnh", [D, S], BF16).ap()
    ewh_d = nc.dram_tensor("ewh", [EXP, 16, 128, 16 * 128], BF16).ap()
    owh_d = nc.dram_tensor("owh", [16, 128, 16 * 128], BF16).ap()
    kth_d = nc.dram_tensor("kth", [4, 128, 32, 128], BF16).ap()
    vta_d = nc.dram_tensor("vta", [4, 128, 32, 136], BF16).ap()
    qfh_d = nc.dram_tensor("qfh", [D, S], BF16).ap()
    aof_d = nc.dram_tensor("aof", [D, S], BF16).ap()
    rbq_d = nc.dram_tensor("rbq", [2, 512], F32).ap()
    sf_d = nc.dram_tensor("sf", [D, S], F32).ap()
    y1f_d = nc.dram_tensor("y1f", [D, S], F32).ap()
    gb_d = nc.dram_tensor("gbscr", [8, 1024], F32).ap()
    dump_specs = {"s": [D, S], "y1": [D, S], "stats": [1, 64]}
    dump_aps = {n: nc.dram_tensor("dbg_" + n, shp, F32, kind="ExternalOutput").ap()
                for n, shp in dump_specs.items() if n in dumps}

    with ExitStack() as top:
        sbp = lambda n, s, dt=F32: top.enter_context(nc.sbuf_tensor(n, s, dt))
        s_ident = sbp("c_ident_s", [128, 128])
        s_identb = sbp("c_identb_s", [128, 128], BF16)
        s_rT = sbp("c_rT_s", [128, 128])
        s_selT = sbp("c_selT_s", [128, 128])
        s_sel8 = sbp("c_sel8_s", [8, 8 * 128])
        s_ones = sbp("c_ones_s", [128, 128])
        s_onesb = sbp("c_onesb_s", [128, 128], BF16)
        s_stats = sbp("stats_s", [128, 64])
        s_nbias = sbp("nbias_s", [128, 16])
        s_rgam = sbp("rgam_s", [128, 4])
        tick = top.enter_context(nc.semaphore("tick"))
        sems = {n: top.enter_context(nc.semaphore(n))
                for n in ("sMM", "sE2", "sGB", "sD0", "sD1", "sLD",
                          "sPA", "sPB", "sPC")}
        block = top.enter_context(nc.Block())
        pr = Prog(nc, block, tick, sems)
        ST = pr.step

        p_pt = top.enter_context(nc.psum_tensor("pt_p", [128, 16], F32))
        es_pt = ExitStack()
        sbq = lambda n, s, dt=F32: es_pt.enter_context(nc.sbuf_tensor(n, s, dt))
        s_ptw = [sbq(f"ptw{b}", [128, 2048]) for b in range(2)]
        s_ptq = sbq("ptq", [128, 2048])
        s_ptl = sbq("ptl", [128, 2048], BF16)
        s_ptg = [sbq(f"ptg{b}", [128, 2048], BF16) for b in range(2)]
        s_pta = sbq("pta", [128, 8])
        BG = []
        _pt = {"ji": 0, "mB": {}, "lastg": [None, None], "rgam_tick": None,
               "emitted": 0}

        def bg_pump(n):
            for _ in range(min(n, len(BG))):
                BG.pop(0)()
                _pt["emitted"] += 1

        def pt_job(kind, src, dst, m, wid, qsel=None):
            ji = _pt["ji"]; _pt["ji"] += 1
            b = ji % 2
            ldeng = qsel or ("sync" if b == 0 else "gpsimd")
            steng = qsel or ("gpsimd" if b == 0 else "sync")
            wl = [] if ji < 2 else [("sPB", _pt["mB"][ji - 2])]
            la = pr.raw(ldeng, lambda e, src=src, b=b, wid=wid: e.dma_start(
                out=s_ptw[b][:, 0:wid], in_=src), waits=wl, inc="sPA")
            if kind == "g":
                pr.raw("vector", lambda e, b=b, wid=wid: e.reduce_sum(
                    s_pta[:, 4 + b:5 + b], s_ptw[b][:, 0:wid], axis=AX.X,
                    apply_absolute_value=True),
                    waits=[("sPB", pr.cnt["sPB"]), ("sPA", la)], inc="sPB")
                _pt["mB"][ji] = pr.raw("vector", lambda e, b=b, m=m: e.tensor_add(
                    s_pta[:, m:m + 1], s_pta[:, m:m + 1], s_pta[:, 4 + b:5 + b]),
                    waits=[("sPB", pr.cnt["sPB"])], inc="sPB")
                return
            wg = [("sPB", pr.cnt["sPB"]), ("sPA", la)]
            if _pt["lastg"][b] is not None:
                wg.append(("sPC", _pt["lastg"][b]))
            if kind == "q":
                wg.append(("TICK", _pt["rgam_tick"]))
                pr.raw("vector", lambda e, b=b, m=m, wid=wid: e.tensor_scalar(
                    s_ptq[:, 0:wid], s_ptw[b][:, 0:wid], s_rgam[:, m:m + 1],
                    None, ALU.mult), waits=wg, inc="sPB")
                pr.raw("vector", lambda e, wid=wid: e.tensor_scalar(
                    s_ptl[:, 0:wid], s_ptq[:, 0:wid], -0.5, None, ALU.is_lt),
                    waits=[("sPB", pr.cnt["sPB"])], inc="sPB")
                pr.raw("vector", lambda e, b=b, wid=wid: e.tensor_scalar(
                    s_ptg[b][:, 0:wid], s_ptq[:, 0:wid], 0.5, None, ALU.is_gt),
                    waits=[("sPB", pr.cnt["sPB"])], inc="sPB")
                _pt["mB"][ji] = pr.raw("vector", lambda e, b=b, wid=wid: e.tensor_tensor(
                    s_ptg[b][:, 0:wid], s_ptg[b][:, 0:wid], s_ptl[:, 0:wid],
                    ALU.subtract), waits=[("sPB", pr.cnt["sPB"])], inc="sPB")
            else:
                _pt["mB"][ji] = pr.raw("vector", lambda e, b=b, wid=wid: e.tensor_copy(
                    s_ptg[b][:, 0:wid], s_ptw[b][:, 0:wid]), waits=wg, inc="sPB")
            _pt["lastg"][b] = pr.raw(steng, lambda e, dst=dst, b=b, wid=wid: e.dma_start(
                out=dst, in_=s_ptg[b][:, 0:wid]),
                waits=[("sPB", _pt["mB"][ji])], inc="sPC")

        def _w_slices(w_ap):
            nb, _, W = w_ap.shape
            for rb in range(nb):
                for off in range(0, W, 2048):
                    yield rb, off, min(2048, W - off)

        def queue_gamma_jobs():
            for m, w_ap in enumerate((w1T_d, w2T_d, w3T_d)):
                for rb, off, wid in _w_slices(w_ap):
                    BG.append(lambda w_ap=w_ap, rb=rb, off=off, wid=wid, m=m:
                              pt_job("g", w_ap[rb, :, off:off + wid], None, m, wid))

        def queue_cast_jobs():
            casts = [(ow_d[oc], owh_d[oc]) for oc in range(16)]
            casts += [(ew_d[ex, dc], ewh_d[ex, dc])
                      for ex in range(EXP) for dc in range(16)]
            for src, dst in casts:
                BG.append(lambda src=src, dst=dst: pt_job("c", src, dst, 0, 2048))

        def queue_quant_jobs():
            for m, (w_ap, q_ap) in enumerate(((w1T_d, q1b_d), (w2T_d, q2b_d),
                                              (w3T_d, q3b_d))):
                for rb, off, wid in _w_slices(w_ap):
                    BG.append(lambda w_ap=w_ap, q_ap=q_ap, rb=rb, off=off,
                              wid=wid, m=m: pt_job(
                                  "q", w_ap[rb, :, off:off + wid],
                                  q_ap[rb, :, off:off + wid], m, wid,
                                  qsel="gpsimd"))

        N_GAMMA = 136

        def emit_rgam():
            bg_pump(max(0, N_GAMMA - _pt["emitted"]))
            pr.barrier(("sPA", "sPB"))
            for m in range(3):
                ST("tensor", lambda e, m=m: e.matmul(
                    p_pt[0:1, 0:1], s_ones[:, 0:1], s_pta[:, m:m + 1],
                    start=True, stop=True))
                ST("vector", lambda e: e.tensor_scalar(
                    s_pta[0:1, 6:7], p_pt[0:1, 0:1], 1.0 / (H1 * D), None, ALU.mult))
                ST("vector", lambda e: e.tensor_scalar(
                    s_pta[0:1, 6:7], s_pta[0:1, 6:7], Q_EPS, None, ALU.add))
                ST("vector", lambda e: e.reciprocal(s_pta[0:1, 7:8], s_pta[0:1, 6:7]))
                ST("tensor", lambda e: e.matmul(
                    p_pt[:, 0:1], s_ones[0:1, :], s_pta[0:1, 7:8],
                    start=True, stop=True))
                ST("scalar", lambda e, m=m: e.activation(
                    s_rgam[:, m:m + 1], p_pt[:, 0:1], AF.Identity))
            pr.flush()
            _pt["rgam_tick"] = pr.k
            queue_quant_jobs()

        ST("sync", lambda e: e.dma_start(out=s_ident[:], in_=ident_d))
        ST("sync", lambda e: e.dma_start(out=s_rT[:], in_=rT_d))
        ST("sync", lambda e: e.dma_start(out=s_selT[:], in_=selT_d))
        ST("sync", lambda e: e.dma_start(out=s_sel8[:], in_=sel8_d))
        ST("vector", lambda e: e.memset(s_ones[:], 1.0))
        ST("vector", lambda e: e.memset(s_onesb[:], 1.0))
        ST("vector", lambda e: e.memset(s_stats[:], 0.0))
        ST("vector", lambda e: e.tensor_copy(s_identb[:], s_ident[:]))
        ST("vector", lambda e: e.memset(s_pta[:], 0.0))
        queue_gamma_jobs()
        queue_cast_jobs()

        def bcast_row(p_bc, row_ap, width):
            ST("tensor", lambda e: e.matmul(p_bc[:, 0:width], s_ones[0:1, :],
                                            row_ap, start=True, stop=True))

        def emit_layernorm(sx, sxn, p_row, p_bc, s_sq, s_st, tw):
            mu = s_st[0:1, 0:tw]
            msq = s_st[0:1, tw:2 * tw]
            t1 = s_st[0:1, 2 * tw:3 * tw]
            t2 = s_st[0:1, 3 * tw:4 * tw]
            for c in range(NC_):
                ST("tensor", lambda e, c=c: e.matmul(
                    p_row[0:1, 0:tw], s_ones[:, 0:1], sx[c][:, 0:tw],
                    start=(c == 0), stop=(c == NC_ - 1)))
            ST("vector", lambda e: e.tensor_scalar(mu, p_row[0:1, 0:tw], 1.0 / D, None, ALU.mult))
            for c in range(NC_):
                ST("scalar", lambda e, c=c: e.activation(s_sq[:, 0:tw], sx[c][:, 0:tw], AF.Square))
                ST("tensor", lambda e, c=c: e.matmul(
                    p_row[0:1, 0:tw], s_ones[:, 0:1], s_sq[:, 0:tw],
                    start=(c == 0), stop=(c == NC_ - 1)))
            ST("vector", lambda e: e.tensor_scalar(msq, p_row[0:1, 0:tw], 1.0 / D, None, ALU.mult))
            ST("vector", lambda e: e.tensor_tensor(t1, mu, mu, ALU.mult))
            ST("vector", lambda e: e.tensor_tensor(t2, msq, t1, ALU.subtract))
            ST("vector", lambda e: e.tensor_scalar(t2, t2, LN_EPS, None, ALU.add))
            ST("scalar", lambda e: e.activation(t1, t2, AF.Sqrt))
            ST("vector", lambda e: e.reciprocal(t2, t1))
            bcast_row(p_bc, mu, tw)
            ST("scalar", lambda e: e.activation(s_sq[:, 0:tw], p_bc[:, 0:tw], AF.Identity))
            bcast_row(p_bc, t2, tw)
            for c in range(NC_):
                ST("vector", lambda e, c=c: e.tensor_tensor(
                    sxn[c][:, 0:tw], sx[c][:, 0:tw], s_sq[:, 0:tw], ALU.subtract))
                ST("vector", lambda e, c=c: e.tensor_tensor(
                    sxn[c][:, 0:tw], sxn[c][:, 0:tw], p_bc[:, 0:tw], ALU.mult))

        def phase_p2():
            TW = 512
            with ExitStack() as es:
                sb = lambda n, s, dt=F32: es.enter_context(nc.sbuf_tensor(n, s, dt))
                ps = lambda n, s, dt=F32: es.enter_context(nc.psum_tensor(n, s, dt))
                s_xf = [sb(f"p2_xf{c}", [128, TW]) for c in range(NC_)]
                s_xn = s_xf
                s_xnh = [sb(f"p2_xnh{c}", [128, TW], BF16) for c in range(NC_)]
                s_sq = sb("p2_sq", [128, TW])
                s_st = sb("p2_st", [1, 4 * TW + 8])
                s_wst = sb("p2_wst", [128, 2048])
                s_wkb = sb("p2_wkb", [128, 4 * 16 * 128], BF16)
                s_wvb = sb("p2_wvb", [128, 4 * 16 * 128], BF16)
                s_k = sb("p2_k", [128, TW])
                s_kr = sb("p2_kr", [128, TW])
                s_khb = sb("p2_khb", [128, TW], BF16)
                s_t1 = sb("p2_t1", [128, TW])
                s_vb16 = sb("p2_vb16", [128, TW], BF16)
                s_vt = sb("p2_vt", [128, 4 * 136], BF16)
                s_cos = sb("p2_cos", [128, S])
                s_sin = sb("p2_sin", [128, S])
                p_trb = ps("p2_trb", [128, 128], BF16)
                p_row = ps("p2_row", [1, TW])
                p_bc = ps("p2_bc", [128, TW])
                p_kv = ps("p2_kv", [128, TW])
                p_rot = ps("p2_rot", [128, TW])

                ST("sync", lambda e: e.dma_start(out=s_cos[:], in_=cos_d))
                ST("sync", lambda e: e.dma_start(out=s_sin[:], in_=sin_d))
                for g in range(4):
                    ST("sync", lambda e, g=g: e.dma_start(out=s_wst[:], in_=wkT_d[g]))
                    ST("vector", lambda e, g=g: e.tensor_copy(
                        s_wkb[:, g * 2048:(g + 1) * 2048], s_wst[:]))
                    ST("sync", lambda e, g=g: e.dma_start(out=s_wst[:], in_=wvT_d[g]))
                    ST("vector", lambda e, g=g: e.tensor_copy(
                        s_wvb[:, g * 2048:(g + 1) * 2048], s_wst[:]))
                ST("vector", lambda e: e.memset(s_vt[:], 0.0))
                for j in range(4):
                    ST("vector", lambda e, j=j: e.memset(
                        s_vt[:, j * 136 + 128:j * 136 + 129], 1.0))

                for tb in range(S // TW):
                    t0 = tb * TW
                    for c in range(NC_):
                        ST("sync", lambda e, c=c, t0=t0: e.dma_start(
                            out=s_xf[c][:], in_=xt_d[c * 128:(c + 1) * 128, t0:t0 + TW]))
                    emit_layernorm(s_xf, s_xn, p_row, p_bc, s_sq, s_st, TW)
                    for c in range(NC_):
                        ST("vector", lambda e, c=c: e.tensor_copy(s_xnh[c][:], s_xn[c][:]))
                        ST("gpsimd", lambda e, c=c, t0=t0: e.dma_start(
                            out=xnh_d[c * 128:(c + 1) * 128, t0:t0 + TW], in_=s_xnh[c][:]))
                    for g in range(4):
                        for ic in range(NC_):
                            ST("tensor", lambda e, ic=ic, g=g: e.matmul(
                                p_kv[:], s_wkb[:, g * 2048 + ic * 128:g * 2048 + (ic + 1) * 128],
                                s_xnh[ic][:], start=(ic == 0), stop=(ic == NC_ - 1)))
                        ST("vector", lambda e: e.tensor_copy(s_k[:], p_kv[:]))
                        ST("tensor", lambda e: e.matmul(p_rot[:], s_rT[:], s_k[:],
                                                        start=True, stop=True))
                        ST("vector", lambda e, t0=t0: e.tensor_tensor(
                            s_kr[:], s_k[:], s_cos[:, t0:t0 + TW], ALU.mult))
                        ST("vector", lambda e, t0=t0: e.tensor_tensor(
                            s_t1[:], p_rot[:], s_sin[:, t0:t0 + TW], ALU.mult))
                        ST("vector", lambda e: e.tensor_add(s_kr[:], s_kr[:], s_t1[:]))
                        ST("vector", lambda e: e.tensor_copy(s_khb[:], s_kr[:]))
                        ST("gpsimd", lambda e, g=g, tb=tb: e.dma_start(
                            out=kth_d[g, :, tb * 4:(tb + 1) * 4, :],
                            in_=s_khb[:].rearrange("p (j f) -> p j f", f=128)))
                        ST("vector", lambda e: e.tensor_tensor(s_t1[:], s_kr[:], s_kr[:], ALU.mult))
                        ST("tensor", lambda e: e.matmul(p_row[0:1, :], s_ones[:, 0:1],
                                                        s_t1[:], start=True, stop=True))
                        ST("vector", lambda e: e.reduce_max(
                            s_st[0:1, 4 * TW:4 * TW + 1], p_row[0:1, :], axis=AX.X))
                        if tb == 0:
                            ST("vector", lambda e, g=g: e.tensor_copy(
                                s_stats[0:1, g:g + 1], s_st[0:1, 4 * TW:4 * TW + 1]))
                        else:
                            ST("vector", lambda e, g=g: e.tensor_tensor(
                                s_stats[0:1, g:g + 1], s_stats[0:1, g:g + 1],
                                s_st[0:1, 4 * TW:4 * TW + 1], ALU.max))
                        for ic in range(NC_):
                            ST("tensor", lambda e, ic=ic, g=g: e.matmul(
                                p_kv[:], s_wvb[:, g * 2048 + ic * 128:g * 2048 + (ic + 1) * 128],
                                s_xnh[ic][:], start=(ic == 0), stop=(ic == NC_ - 1)))
                        ST("vector", lambda e: e.tensor_copy(s_vb16[:], p_kv[:]))
                        for j in range(TW // 128):
                            ST("tensor", lambda e, j=j: e.transpose(
                                p_trb[:], s_vb16[:, j * 128:(j + 1) * 128], s_identb[:]))
                            ST("vector", lambda e, j=j: e.tensor_copy(
                                s_vt[:, j * 136:j * 136 + 128], p_trb[:]))
                        ST("gpsimd", lambda e, g=g, tb=tb: e.dma_start(
                            out=vta_d[g, :, tb * 4:(tb + 1) * 4, :],
                            in_=s_vt[:].rearrange("p (j f) -> p j f", f=136)))
                    bg_pump(20)
                pr.flush()

        def phase_p3a():
            TW = 512
            with ExitStack() as es:
                sb = lambda n, s, dt=F32: es.enter_context(nc.sbuf_tensor(n, s, dt))
                ps = lambda n, s: es.enter_context(nc.psum_tensor(n, s, F32))
                s_xnh = [sb(f"p3a_xnh{c}", [128, TW], BF16) for c in range(NC_)]
                s_wst = sb("p3a_wst", [128, 16 * 128])
                s_wqb = sb("p3a_wqb", [128, 16 * 16 * 128], BF16)
                s_q = sb("p3a_q", [128, TW])
                s_qr = sb("p3a_qr", [128, TW])
                s_qh = sb("p3a_qh", [128, TW], BF16)
                s_t1 = sb("p3a_t1", [128, TW])
                s_m = sb("p3a_m", [1, 2])
                s_cos = sb("p3a_cos", [128, S])
                s_sin = sb("p3a_sin", [128, S])
                p_q = ps("p3a_q_p", [128, TW])
                p_rot = ps("p3a_rot", [128, TW])
                p_row = ps("p3a_row", [1, TW])
                ST("sync", lambda e: e.dma_start(out=s_cos[:], in_=cos_d))
                ST("sync", lambda e: e.dma_start(out=s_sin[:], in_=sin_d))
                for h in range(NH):
                    ST("sync", lambda e, h=h: e.dma_start(out=s_wst[:], in_=wqT_d[h]))
                    ST("vector", lambda e, h=h: e.tensor_copy(
                        s_wqb[:, h * 2048:(h + 1) * 2048], s_wst[:]))
                for tb in range(S // TW):
                    t0 = tb * TW
                    for c in range(NC_):
                        ST("sync", lambda e, c=c, t0=t0: e.dma_start(
                            out=s_xnh[c][:], in_=xnh_d[c * 128:(c + 1) * 128, t0:t0 + TW]))
                    for h in range(NH):
                        for ic in range(NC_):
                            ST("tensor", lambda e, ic=ic, h=h: e.matmul(
                                p_q[:], s_wqb[:, (h * 16 + ic) * 128:(h * 16 + ic + 1) * 128],
                                s_xnh[ic][:], start=(ic == 0), stop=(ic == NC_ - 1)))
                        ST("vector", lambda e: e.tensor_copy(s_q[:], p_q[:]))
                        ST("tensor", lambda e: e.matmul(p_rot[:], s_rT[:], s_q[:],
                                                        start=True, stop=True))
                        ST("vector", lambda e, t0=t0: e.tensor_tensor(
                            s_qr[:], s_q[:], s_cos[:, t0:t0 + TW], ALU.mult))
                        ST("vector", lambda e, t0=t0: e.tensor_tensor(
                            s_t1[:], p_rot[:], s_sin[:, t0:t0 + TW], ALU.mult))
                        ST("vector", lambda e: e.tensor_add(s_qr[:], s_qr[:], s_t1[:]))
                        ST("vector", lambda e: e.tensor_copy(s_qh[:], s_qr[:]))
                        ST("gpsimd", lambda e, h=h, t0=t0: e.dma_start(
                            out=qfh_d[h * 128:(h + 1) * 128, t0:t0 + TW], in_=s_qh[:]))
                        ST("vector", lambda e: e.tensor_tensor(s_t1[:], s_qr[:], s_qr[:], ALU.mult))
                        ST("tensor", lambda e: e.matmul(p_row[0:1, :], s_ones[:, 0:1],
                                                        s_t1[:], start=True, stop=True))
                        ST("vector", lambda e: e.reduce_max(s_m[0:1, 0:1], p_row[0:1, :], axis=AX.X))
                        if tb == 0:
                            ST("vector", lambda e, h=h: e.tensor_copy(
                                s_stats[0:1, 4 + h:5 + h], s_m[0:1, 0:1]))
                        else:
                            ST("vector", lambda e, h=h: e.tensor_tensor(
                                s_stats[0:1, 4 + h:5 + h], s_stats[0:1, 4 + h:5 + h],
                                s_m[0:1, 0:1], ALU.max))
                        bg_pump(1)
                pr.flush()

        def phase_p3b():
            QW = 512
            with ExitStack() as es:
                sb = lambda n, s, dt=F32: es.enter_context(nc.sbuf_tensor(n, s, dt))
                ps = lambda n, s, dt=F32: es.enter_context(nc.psum_tensor(n, s, dt))
                s_kb = [sb(f"p3b_kb{b}", [128, 32 * 128], BF16) for b in range(2)]
                s_vb = [sb(f"p3b_vb{b}", [128, 32 * 136], BF16) for b in range(2)]
                s_qh = [sb(f"p3b_qh{b}", [128, QW], BF16) for b in range(4)]
                s_e = [sb(f"p3b_e{b}", [128, QW], BF16) for b in range(2)]
                s_r = [sb(f"p3b_r{b}", [1, QW]) for b in range(2)]
                s_rb = [sb(f"p3b_rb{b}", [128, QW]) for b in range(2)]
                s_aob = [sb(f"p3b_aob{b}", [128, QW], BF16) for b in range(2)]
                s_sc = sb("p3b_sc", [1, 16])
                p_s = [ps(f"p3b_s{b}", [128, QW]) for b in range(3)]
                p_o = ps("p3b_o", [128, QW])
                p_row = ps("p3b_row", [1, QW])

                ST("scalar", lambda e: e.activation(s_sc[0:1, 0:4], s_stats[0:1, 0:4], AF.Sqrt))
                ST("scalar", lambda e: e.activation(s_stats[0:1, 20:36],
                                                    s_stats[0:1, 4:20], AF.Sqrt))
                for h in range(NH):
                    ST("vector", lambda e, h=h: e.tensor_tensor(
                        s_stats[0:1, 40 + h:41 + h], s_stats[0:1, 20 + h:21 + h],
                        s_sc[0:1, h // 4:h // 4 + 1], ALU.mult))
                ST("vector", lambda e: e.tensor_scalar(
                    s_stats[0:1, 40:56], s_stats[0:1, 40:56], -SCALE0, None, ALU.mult))
                ST("tensor", lambda e: e.matmul(p_o[:, 0:16], s_ones[0:1, :],
                                                s_stats[0:1, 40:56], start=True, stop=True))
                ST("scalar", lambda e: e.activation(s_nbias[:], p_o[:, 0:16], AF.Identity))
                pr.flush()

                entry = pr.k
                msc, mA, mexp = {}, {}, {}
                mkv, mq, mstr, mldr, mrec, mmul, mst = ({}, {}, {}, {}, {}, {}, {})

                def kv_load(gi):
                    g = gi % 4
                    buf = gi % 2
                    w = [("TICK", entry)] if gi < 2 else [("sMM", mA[(gi - 2) * 4 * 32 + 4 * 32 - 1])]
                    pr.raw("sync", lambda e, g=g, buf=buf: e.dma_start(
                        out=s_kb[buf][:], in_=kth_d[g].rearrange("p j f -> p (j f)")),
                        waits=w, inc="sD1")
                    mkv[gi] = pr.raw("sync", lambda e, g=g, buf=buf: e.dma_start(
                        out=s_vb[buf][:], in_=vta_d[g].rearrange("p j f -> p (j f)")),
                        waits=w, inc="sD1")

                def q_load(t):
                    qb, h = t // NH, t % NH
                    buf = t % 4
                    w = ([("TICK", entry)] if t < 4
                         else [("sMM", msc[(t - 4) * 32 + 31])])
                    mq[t] = pr.raw("sync", lambda e, h=h, qb=qb, buf=buf: e.dma_start(
                        out=s_qh[buf][:],
                        in_=qfh_d[h * 128:(h + 1) * 128, qb * QW:(qb + 1) * QW]),
                        waits=w, inc="sD0")

                NT = (S // QW) * NH
                kv_load(0); kv_load(1)
                q_load(0); q_load(1); q_load(2); q_load(3)

                def pe_score(t, j):
                    J = t * 32 + j
                    gi = t // 4
                    w = []
                    if J >= 2 and J % 2 == 0:
                        w.append(("sE2", mexp[J - 2]))
                    if j == 0:
                        w += [("sD0", mq[t]), ("sD1", mkv[gi])]
                    msc[J] = pr.raw("tensor", lambda e, j=j, t=t, gi=gi: e.matmul(
                        p_s[J % 3][:], s_kb[gi % 2][:, j * 128:(j + 1) * 128],
                        s_qh[t % 4][:], start=True, stop=True), waits=w, inc="sMM")

                def pe_denom_av(t, j):
                    J = t * 32 + j
                    gi = t // 4
                    w = [("sE2", mexp[J])]
                    if j == 0 and t >= 1:
                        w.append(("sGB", mmul[t - 1]))
                    pr.raw("tensor", lambda e, j=j, t=t: e.matmul(
                        p_row[0:1, :], s_onesb[:, 0:1], s_e[J % 2][:],
                        start=(j == 0), stop=(j == 31)), waits=w)
                    mA[J] = pr.raw("tensor", lambda e, j=j, t=t, gi=gi: e.matmul(
                        p_o[:], s_vb[gi % 2][:, j * 136:j * 136 + 128],
                        s_e[J % 2][:], start=(j == 0), stop=(j == 31)), inc="sMM")

                def sc_exp(t, j):
                    J = t * 32 + j
                    h = t % NH
                    mexp[J] = pr.raw("scalar", lambda e, J=J, h=h: e.activation(
                        s_e[J % 2][:], p_s[J % 3][:], AF.Exp,
                        bias=s_nbias[:, h:h + 1], scale=SCALE0),
                        waits=[("sMM", msc[J])], inc="sE2")

                def dve_recip(t):
                    w = [("sMM", mA[t * 32 + 31])]
                    if t >= 2:
                        w.append(("sLD", mstr[t - 2]))
                    mrec[t] = pr.raw("vector", lambda e, t=t: e.reciprocal(
                        s_r[t % 2][0:1, :], p_row[0:1, :]), waits=w, inc="sGB")

                def r_store(t):
                    w = [("sGB", mrec[t])]
                    if t >= 2:
                        w.append(("sD0", mldr[t - 2]))
                    mstr[t] = pr.raw("gpsimd", lambda e, t=t: e.dma_start(
                        out=rbq_d[t % 2:t % 2 + 1, :], in_=s_r[t % 2][0:1, :]),
                        waits=w, inc="sLD")

                def r_bcast(t):
                    w = [("sLD", mstr[t])]
                    if t >= 2:
                        w.append(("sGB", mmul[t - 2]))
                    mldr[t] = pr.raw("sync", lambda e, t=t: e.dma_start(
                        out=s_rb[t % 2][:],
                        in_=rbq_d[t % 2:t % 2 + 1, :].partition_broadcast(128).squeeze(1)),
                        waits=w, inc="sD0")

                def dve_mult(t):
                    w = [("sD0", mldr[t])]
                    if t >= 2:
                        w.append(("sLD", mst[t - 2]))
                    mmul[t] = pr.raw("vector", lambda e, t=t: e.tensor_tensor(
                        s_aob[t % 2][:], p_o[:], s_rb[t % 2][:], ALU.mult),
                        waits=w, inc="sGB")

                def ao_store(t):
                    qb, h = t // NH, t % NH
                    mst[t] = pr.raw("gpsimd", lambda e, h=h, qb=qb, t=t: e.dma_start(
                        out=aof_d[h * 128:(h + 1) * 128, qb * QW:(qb + 1) * QW],
                        in_=s_aob[t % 2][:]),
                        waits=[("sGB", mmul[t])], inc="sLD")

                for t in range(NT):
                    pe_score(t, 0); sc_exp(t, 0)
                    pe_score(t, 1); sc_exp(t, 1)
                    if t >= 1:
                        dve_mult(t - 1)
                        ao_store(t - 1)
                    for j in range(30):
                        pe_denom_av(t, j)
                        pe_score(t, j + 2)
                        sc_exp(t, j + 2)
                    pe_denom_av(t, 30)
                    pe_denom_av(t, 31)
                    dve_recip(t)
                    r_store(t)
                    r_bcast(t)
                    if t + 4 < NT:
                        q_load(t + 4)
                    if t + 2 < NT and (t + 2) % 4 == 0 and (t + 2) // 4 >= 2:
                        kv_load((t + 2) // 4)
                    bg_pump(2)
                dve_mult(NT - 1)
                ao_store(NT - 1)
                pr.barrier(("sMM", "sE2", "sGB", "sD0", "sD1", "sLD"))
                pr.flush()

        def phase_p3c():
            TW = 512
            with ExitStack() as es:
                sb = lambda n, s, dt=F32: es.enter_context(nc.sbuf_tensor(n, s, dt))
                ps = lambda n, s: es.enter_context(nc.psum_tensor(n, s, F32))
                s_ao = [sb(f"p3c_ao{c}", [128, TW], BF16) for c in range(NC_)]
                s_wst = sb("p3c_wst", [128, 16 * 128])
                s_wob = sb("p3c_wob", [128, 16 * 16 * 128], BF16)
                s_s = sb("p3c_s", [128, TW])
                p_s = ps("p3c_p", [128, TW])
                for oc in range(NC_):
                    ST("sync", lambda e, oc=oc: e.dma_start(out=s_wst[:], in_=woT_d[oc]))
                    ST("vector", lambda e, oc=oc: e.tensor_copy(
                        s_wob[:, oc * 2048:(oc + 1) * 2048], s_wst[:]))
                for tb in range(S // TW):
                    t0 = tb * TW
                    for c in range(NC_):
                        ST("sync", lambda e, c=c, t0=t0: e.dma_start(
                            out=s_ao[c][:], in_=aof_d[c * 128:(c + 1) * 128, t0:t0 + TW]))
                    for oc in range(NC_):
                        for ic in range(NC_):
                            ST("tensor", lambda e, ic=ic, oc=oc: e.matmul(
                                p_s[:], s_wob[:, (oc * 16 + ic) * 128:(oc * 16 + ic + 1) * 128],
                                s_ao[ic][:], start=(ic == 0), stop=(ic == NC_ - 1)))
                        ST("vector", lambda e: e.tensor_copy(s_s[:], p_s[:]))
                        ST("gpsimd", lambda e, oc=oc, t0=t0: e.dma_start(
                            out=sf_d[oc * 128:(oc + 1) * 128, t0:t0 + TW], in_=s_s[:]))
                        bg_pump(1)
                pr.flush()

        def phase_p4():
            TW = 1024
            NSUB = TW // 128
            with ExitStack() as es:
                sb = lambda n, s, dt=F32: es.enter_context(nc.sbuf_tensor(n, s, dt))
                ps = lambda n, s: es.enter_context(nc.psum_tensor(n, s, F32))
                s_xh = [sb(f"p4_xh{c}", [128, TW], BF16) for c in range(NC_)]
                s_cb = [sb(f"p4_cb{c}", [128, TW]) for c in range(NC_)]
                s_wh = [sb(f"p4_wh{b}", [128, 16 * 128], BF16) for b in range(2)]
                s_xld = [sb(f"p4_xld{b}", [128, TW]) for b in range(4)]
                s_sld = [sb(f"p4_sld{b}", [128, TW]) for b in range(4)]
                s_yt = [sb(f"p4_yt{b}", [128, TW]) for b in range(2)]
                s_rw = sb("p4_rw", [128, NC_ * EXP])
                s_rwh = sb("p4_rwh", [128, NC_ * EXP], BF16)
                s_rbb = sb("p4_rbb", [128, EXP])
                s_ebT = sb("p4_ebT", [128, 16 * 8])
                s_obT = sb("p4_obT", [128, 16])
                s_eb8 = sb("p4_eb8", [8, DEXP])
                s_ob1 = sb("p4_ob1", [1, D])
                s_lg = sb("p4_lg", [128, 8])
                s_m = sb("p4_m", [128, 8])
                s_gt = sb("p4_gt", [128, NSUB * 8])
                s_gf = sb("p4_gf", [8, TW])
                s_gbc = sb("p4_gbc", [128, TW])
                s_z = sb("p4_z", [128, TW])
                s_t1 = sb("p4_t1", [128, TW])
                s_t2 = sb("p4_t2", [128, TW])
                p_sm = ps("p4_sm", [128, 512])
                p_l = p_sm[:, 0:8]
                p_tr = p_sm[:, 128:256]
                p_e4 = [ps(f"p4_e4{b}", [128, TW]) for b in range(2)]
                gbn, bdn, mmn, e2n = {}, {}, {}, {}

                ST("sync", lambda e: e.dma_start(out=s_rw[:], in_=rw_d))
                ST("vector", lambda e: e.tensor_copy(s_rwh[:], s_rw[:]))
                ST("sync", lambda e: e.dma_start(out=s_lg[0:1, :], in_=rb_d.unsqueeze(0)))
                bcast_row(p_sm, s_lg[0:1, 0:8], 8)
                ST("scalar", lambda e: e.activation(s_rbb[:], p_l, AF.Identity))
                ST("sync", lambda e: e.dma_start(out=s_eb8[:], in_=eb_d))
                ST("sync", lambda e: e.dma_start(out=s_ob1[:], in_=ob_d.unsqueeze(0)))
                for dc in range(16):
                    ST("tensor", lambda e, dc=dc: e.transpose(
                        p_tr[:, 0:8], s_eb8[:, dc * 128:(dc + 1) * 128], s_ident[0:8, 0:8]))
                    ST("vector", lambda e, dc=dc: e.tensor_copy(
                        s_ebT[:, dc * 8:(dc + 1) * 8], p_tr[:, 0:8]))
                    ST("tensor", lambda e, dc=dc: e.transpose(
                        p_tr[:, 0:1], s_ob1[:, dc * 128:(dc + 1) * 128], s_ident[0:1, 0:1]))
                    ST("vector", lambda e, dc=dc: e.tensor_copy(
                        s_obT[:, dc:dc + 1], p_tr[:, 0:1]))

                for tb in range(S // TW):
                    t0 = tb * TW
                    for c in range(NC_):
                        ST("sync", lambda e, c=c, t0=t0: e.dma_start(
                            out=s_xh[c][:], in_=xnh_d[c * 128:(c + 1) * 128, t0:t0 + TW]))
                    for sub in range(NSUB):
                        for ic in range(NC_):
                            ST("tensor", lambda e, ic=ic, sub=sub: e.matmul(
                                p_l[:], s_xh[ic][:, sub * 128:(sub + 1) * 128],
                                s_rwh[:, ic * 8:(ic + 1) * 8],
                                start=(ic == 0),
                                stop=(ic == NC_ - 1)))
                        ST("vector", lambda e: e.tensor_tensor(s_lg[:], p_l[:], s_rbb[:], ALU.add))
                        ST("vector", lambda e: e.reduce_max(s_m[:, 0:1], s_lg[:], axis=AX.X))
                        ST("vector", lambda e, sub=sub: e.tensor_scalar(
                            s_gt[:, sub * 8:(sub + 1) * 8], s_lg[:], s_m[:, 0:1],
                            None, ALU.is_equal))
                        ST("vector", lambda e, sub=sub: e.scalar_tensor_tensor(
                            s_lg[:], s_gt[:, sub * 8:(sub + 1) * 8], -1e30, s_lg[:],
                            ALU.mult, ALU.add))
                        ST("vector", lambda e: e.reduce_max(s_m[:, 1:2], s_lg[:], axis=AX.X))
                        ST("vector", lambda e: e.tensor_scalar(
                            s_m[:, 4:5], s_m[:, 1:2], -1.0, None, ALU.mult))
                        ST("scalar", lambda e: e.activation(
                            s_m[:, 2:3], s_m[:, 0:1], AF.Sigmoid, bias=s_m[:, 4:5], scale=1.0))
                        ST("vector", lambda e: e.tensor_scalar(
                            s_m[:, 5:6], s_m[:, 0:1], -1.0, None, ALU.mult))
                        ST("scalar", lambda e: e.activation(
                            s_m[:, 3:4], s_m[:, 1:2], AF.Sigmoid, bias=s_m[:, 5:6], scale=1.0))
                        ST("vector", lambda e, sub=sub: e.tensor_scalar(
                            s_gt[:, sub * 8:(sub + 1) * 8], s_gt[:, sub * 8:(sub + 1) * 8],
                            s_m[:, 2:3], None, ALU.mult))
                        ST("vector", lambda e: e.tensor_scalar(
                            s_lg[:], s_lg[:], s_m[:, 1:2], None, ALU.is_equal))
                        ST("vector", lambda e: e.tensor_scalar(
                            s_lg[:], s_lg[:], s_m[:, 3:4], None, ALU.mult))
                        ST("vector", lambda e, sub=sub: e.tensor_add(
                            s_gt[:, sub * 8:(sub + 1) * 8], s_gt[:, sub * 8:(sub + 1) * 8],
                            s_lg[:]))
                        ST("tensor", lambda e, sub=sub: e.transpose(
                            p_tr[0:8, 0:128], s_gt[:, sub * 8:(sub + 1) * 8], s_ident[:]))
                        ST("vector", lambda e, sub=sub: e.tensor_copy(
                            s_gf[0:8, sub * 128:(sub + 1) * 128], p_tr[0:8, 0:128]))
                    ST("gpsimd", lambda e: e.dma_start(out=gb_d, in_=s_gf[:]))

                    entry = pr.k
                    gbn[0] = pr.raw("sync", lambda e: e.dma_start(
                        out=s_gbc[:],
                        in_=gb_d[0:1, :].partition_broadcast(128).squeeze(1)),
                        waits=[("TICK", entry)], inc="sGB")

                    groups = ([("exp", ex, dc) for ex in range(EXP) for dc in range(16)]
                              + [("out", 0, oc) for oc in range(16)])
                    NG = len(groups)
                    mxs, msty = {}, {}

                    def emit_band(li):
                        gg = tb * NG + li
                        kind, ex, dc = groups[li]
                        buf = gg % 2
                        src = ewh_d[ex, dc] if kind == "exp" else owh_d[dc]
                        w = ([("sMM", mmn[gg - 2])] if (gg - 2) in mmn
                             else [("TICK", entry)])
                        bdn[gg] = pr.raw(
                            "sync" if buf == 0 else "gpsimd",
                            lambda e, src=src, buf=buf: e.dma_start(
                                out=s_wh[buf][:], in_=src),
                            waits=w, inc="sD%d" % buf)

                    def emit_xs(oc):
                        b = oc % 4
                        w = [("TICK", entry)] if oc < 4 else [("sLD", msty[oc - 4])]
                        pr.raw("sync", lambda e, oc=oc, b=b, t0=t0: e.dma_start(
                            out=s_xld[b][:], in_=xt_d[oc * 128:(oc + 1) * 128, t0:t0 + TW]),
                            waits=w, inc="sLD")
                        mxs[oc] = pr.raw("sync", lambda e, oc=oc, b=b, t0=t0: e.dma_start(
                            out=s_sld[b][:], in_=sf_d[oc * 128:(oc + 1) * 128, t0:t0 + TW]),
                            waits=w, inc="sLD")

                    def EL(eng, fn, extra=()):
                        pr.raw(eng, fn,
                               waits=[("sE2", pr.cnt["sE2"])] + list(extra), inc="sE2")

                    emit_band(0)
                    emit_band(1)
                    for li, (kind, ex, dc) in enumerate(groups):
                        gg = tb * NG + li
                        buf = gg % 2
                        w = [("TICK", entry), ("sD%d" % buf, bdn[gg])]
                        if (gg - 2) in e2n:
                            w.append(("sE2", e2n[gg - 2]))
                        if kind == "out" and dc == 0:
                            w.append(("sE2", e2n[gg - 1]))
                        first = True
                        for hh in range(2):
                            for ic in range(NC_):
                                inc = "sMM" if (hh == 1 and ic == NC_ - 1) else None
                                pr.raw("tensor", lambda e, hh=hh, buf=buf, ic=ic: e.matmul(
                                    p_e4[buf][:, hh * 512:(hh + 1) * 512],
                                    s_wh[buf][:, ic * 128:(ic + 1) * 128],
                                    s_xh[ic][:, hh * 512:(hh + 1) * 512],
                                    start=(ic == 0), stop=(ic == NC_ - 1)),
                                    waits=w if first else (), inc=inc)
                                first = False
                        mmn[gg] = pr.cnt["sMM"]
                        if li + 2 < NG:
                            emit_band(li + 2)
                        if kind == "exp":
                            for hh in range(2):
                                EL("scalar", lambda e, hh=hh, dc=dc, ex=ex, buf=buf:
                                   e.activation(
                                       s_z[:, hh * 512:(hh + 1) * 512],
                                       p_e4[buf][:, hh * 512:(hh + 1) * 512], AF.Identity,
                                       bias=s_ebT[:, dc * 8 + ex:dc * 8 + ex + 1]),
                                   extra=[("sMM", mmn[gg])] if hh == 0 else ())
                            EL("scalar", lambda e: e.activation(s_t1[:], s_z[:], AF.Square))
                            EL("vector", lambda e: e.tensor_scalar(
                                s_t1[:], s_t1[:], GELU_A, 1.0, ALU.mult, ALU.add))
                            EL("vector", lambda e: e.tensor_tensor(
                                s_t1[:], s_t1[:], s_z[:], ALU.mult))
                            EL("scalar", lambda e: e.activation(
                                s_t1[:], s_t1[:], AF.Sigmoid, scale=GELU_C2))
                            EL("vector", lambda e: e.tensor_tensor(
                                s_t1[:], s_t1[:], s_z[:], ALU.mult))
                            if ex == 0:
                                EL("vector", lambda e, dc=dc: e.tensor_tensor(
                                    s_cb[dc][:], s_t1[:], s_gbc[:], ALU.mult),
                                   extra=[("sGB", gbn[ex])] if dc == 0 else ())
                            else:
                                EL("vector", lambda e: e.tensor_tensor(
                                    s_t2[:], s_t1[:], s_gbc[:], ALU.mult),
                                   extra=[("sGB", gbn[ex])] if dc == 0 else ())
                                EL("vector", lambda e, dc=dc: e.tensor_add(
                                    s_cb[dc][:], s_cb[dc][:], s_t2[:]))
                            if dc == 15:
                                if ex + 1 < EXP:
                                    gbn[ex + 1] = pr.raw(
                                        "sync", lambda e, ex=ex: e.dma_start(
                                            out=s_gbc[:],
                                            in_=gb_d[ex + 1:ex + 2, :]
                                            .partition_broadcast(128).squeeze(1)),
                                        waits=[("sE2", pr.cnt["sE2"])], inc="sGB")
                                else:
                                    for c in range(NC_):
                                        EL("vector", lambda e, c=c: e.tensor_copy(
                                            s_xh[c][:], s_cb[c][:]))
                        else:
                            oc = dc
                            if oc == 0:
                                for o4 in range(4):
                                    emit_xs(o4)
                            for hh in range(2):
                                EL("scalar", lambda e, hh=hh, oc=oc, buf=buf: e.activation(
                                    s_z[:, hh * 512:(hh + 1) * 512],
                                    p_e4[buf][:, hh * 512:(hh + 1) * 512],
                                    AF.Silu, bias=s_obT[:, oc:oc + 1]),
                                   extra=[("sMM", mmn[gg])] if hh == 0 else ())
                            ex2 = [("sLD", mxs[oc])]
                            if oc >= 2:
                                ex2.append(("sLD", msty[oc - 2]))
                            EL("vector", lambda e, oc=oc: e.tensor_tensor(
                                s_yt[oc % 2][:], s_z[:], s_sld[oc % 4][:], ALU.mult),
                               extra=ex2)
                            EL("vector", lambda e, oc=oc: e.tensor_add(
                                s_yt[oc % 2][:], s_yt[oc % 2][:], s_xld[oc % 4][:]))
                            msty[oc] = pr.raw(
                                "gpsimd", lambda e, oc=oc, t0=t0: e.dma_start(
                                    out=y1f_d[oc * 128:(oc + 1) * 128, t0:t0 + TW],
                                    in_=s_yt[oc % 2][:]),
                                waits=[("sE2", pr.cnt["sE2"])], inc="sLD")
                            if oc + 4 < NC_:
                                emit_xs(oc + 4)
                        e2n[gg] = pr.cnt["sE2"]
                    pr.barrier(("sMM", "sE2", "sGB", "sD0", "sD1", "sLD"))
                pr.flush()

        def phase_p5():
            TW = 512
            with ExitStack() as es:
                sb = lambda n, s, dt=F32: es.enter_context(nc.sbuf_tensor(n, s, dt))
                ps = lambda n, s: es.enter_context(nc.psum_tensor(n, s, F32))
                s_xn = [sb(f"p5_xn{c}", [128, TW]) for c in range(NC_)]
                s_xh = [sb(f"p5_xh{c}", [128, TW], BF16) for c in range(NC_)]
                s_hh = [sb(f"p5_hh{o}", [128, TW], BF16) for o in range(NOC1)]
                s_kv = [sb(f"p5_kv{g}", [128, TW]) for g in range(4)]
                s_selv = [sb(f"p5_sv{g}", [128, TW]) for g in range(4)]
                s_wst = sb("p5_wst", [128, 16 * 128])
                s_wkvb = sb("p5_wkvb", [128, 4 * 16 * 128], BF16)
                s_q1 = [sb(f"p5_q1{b}", [128, 16 * 128], BF16) for b in range(2)]
                s_q2 = [sb(f"p5_q2{b}", [128, 16 * 128], BF16) for b in range(2)]
                s_q3 = [sb(f"p5_q3{b}", [128, NOC1 * 128], BF16) for b in range(2)]
                s_bkvT = sb("p5_bkvT", [128, 4])
                s_bkv1 = sb("p5_bkv1", [1, KVF])
                s_t1 = [sb(f"p5_t1{b}", [128, TW]) for b in range(2)]
                s_t3 = [sb(f"p5_t3{b}", [128, TW]) for b in range(2)]
                s_sq = sb("p5_sq", [128, TW])
                s_st = sb("p5_st", [1, 4 * TW + 8])
                p_a = [ps(f"p5_a{b}", [128, TW]) for b in range(2)]
                p_b2 = [ps(f"p5_b2{b}", [128, TW]) for b in range(2)]
                p_row = ps("p5_row", [1, TW])
                p_bc = ps("p5_bc", [128, TW])
                p_tr = ps("p5_tr", [128, 128])

                ST("sync", lambda e: e.dma_start(out=s_bkv1[:], in_=bkv_d.unsqueeze(0)))
                for g in range(4):
                    ST("tensor", lambda e, g=g: e.transpose(
                        p_tr[:, 0:1], s_bkv1[:, g * 128:(g + 1) * 128], s_ident[0:1, 0:1]))
                    ST("vector", lambda e, g=g: e.tensor_copy(s_bkvT[:, g:g + 1], p_tr[:, 0:1]))
                for g in range(4):
                    ST("sync", lambda e, g=g: e.dma_start(out=s_wst[:], in_=wkv_d[g]))
                    ST("vector", lambda e, g=g: e.tensor_copy(
                        s_wkvb[:, g * 2048:(g + 1) * 2048], s_wst[:]))

                for tb in range(S // TW):
                    t0 = tb * TW
                    for c in range(NC_):
                        ST("sync", lambda e, c=c, t0=t0: e.dma_start(
                            out=s_xn[c][:], in_=y1f_d[c * 128:(c + 1) * 128, t0:t0 + TW]))
                    emit_layernorm(s_xn, s_xn, p_row, p_bc, s_sq, s_st, TW)
                    for c in range(NC_):
                        ST("vector", lambda e, c=c: e.tensor_copy(s_xh[c][:], s_xn[c][:]))
                    for g in range(4):
                        for ic in range(NC_):
                            ST("tensor", lambda e, ic=ic, g=g: e.matmul(
                                p_a[0][:], s_wkvb[:, g * 2048 + ic * 128:g * 2048 + (ic + 1) * 128],
                                s_xh[ic][:], start=(ic == 0), stop=(ic == NC_ - 1)))
                        ST("scalar", lambda e, g=g: e.activation(
                            s_kv[g][:], p_a[0][:], AF.Identity, bias=s_bkvT[:, g:g + 1]))
                        ST("tensor", lambda e, g=g: e.matmul(
                            p_b2[0][:], s_selT[:], s_kv[g][:], start=True, stop=True))
                        ST("vector", lambda e, g=g: e.tensor_copy(s_selv[g][:], p_b2[0][:]))
                    for c in range(NC_):
                        ST("sync", lambda e, c=c, t0=t0: e.dma_start(
                            out=s_xn[c][:], in_=y1f_d[c * 128:(c + 1) * 128, t0:t0 + TW]))
                    pr.flush()

                    entry = pr.k
                    mld1, mld2, mld3a, mld3b = {}, {}, {}, {}
                    mmA, mmB, mm3, msig, msil, mmul5, madd, mst3 = ({}, {}, {},
                                                                    {}, {}, {}, {}, {})

                    def ld12(oc):
                        b = oc % 2
                        w1 = ([("TICK", entry)] if oc < 2 else [("sMM", mmA[oc - 2])])
                        w2 = ([("TICK", entry)] if oc < 2 else [("sMM", mmB[oc - 2])])
                        mld1[oc] = pr.raw("sync", lambda e, oc=oc, b=b: e.dma_start(
                            out=s_q1[b][:], in_=q1b_d[oc]), waits=w1, inc="sD0")
                        mld2[oc] = pr.raw("gpsimd", lambda e, oc=oc, b=b: e.dma_start(
                            out=s_q2[b][:], in_=q2b_d[oc]), waits=w2, inc="sD1")

                    def ld3(fc):
                        b = fc % 2
                        HC = (NOC1 // 2) * 128
                        w = ([("TICK", entry)] if fc < 2 else [("sMM", mm3[fc - 2])])
                        mld3a[fc] = pr.raw("sync", lambda e, fc=fc, b=b: e.dma_start(
                            out=s_q3[b][:, 0:HC], in_=q3b_d[fc, :, 0:HC]),
                            waits=w, inc="sD0")
                        mld3b[fc] = pr.raw("gpsimd", lambda e, fc=fc, b=b: e.dma_start(
                            out=s_q3[b][:, HC:2 * HC], in_=q3b_d[fc, :, HC:2 * HC]),
                            waits=w, inc="sD1")

                    def pe12(oc):
                        b = oc % 2
                        w = [("sD0", mld1[oc])]
                        if oc >= 2:
                            w += [("sE2", msig[oc - 2]), ("sGB", mmul5[oc - 2])]
                        if oc == 0:
                            w.append(("TICK", entry))
                        for ic in range(NC_):
                            mmA[oc] = pr.raw("tensor", lambda e, ic=ic, b=b: e.matmul(
                                p_a[b][:], s_q1[b][:, ic * 128:(ic + 1) * 128], s_xh[ic][:],
                                start=(ic == 0), stop=(ic == NC_ - 1)),
                                waits=w if ic == 0 else (),
                                inc="sMM" if ic == NC_ - 1 else None)
                        w2 = [("sD1", mld2[oc])]
                        for ic in range(NC_):
                            mmB[oc] = pr.raw("tensor", lambda e, ic=ic, b=b: e.matmul(
                                p_b2[b][:], s_q2[b][:, ic * 128:(ic + 1) * 128], s_xh[ic][:],
                                start=(ic == 0), stop=(ic == NC_ - 1)),
                                waits=w2 if ic == 0 else (),
                                inc="sMM" if ic == NC_ - 1 else None)

                    def tail12(oc):
                        b = oc % 2
                        w = [("sMM", mmA[oc])]
                        if oc >= 2:
                            w.append(("sGB", mmul5[oc - 2]))
                        msig[oc] = pr.raw("scalar", lambda e, b=b: e.activation(
                            s_t1[b][:], p_a[b][:], AF.Sigmoid), waits=w, inc="sE2")
                        mmul5[oc] = pr.raw("vector", lambda e, oc=oc, b=b: e.tensor_tensor(
                            s_hh[oc][:], s_t1[b][:], p_b2[b][:], ALU.mult),
                            waits=[("sE2", msig[oc]), ("sMM", mmB[oc])], inc="sGB")

                    def pe3(fc):
                        b = fc % 2
                        w = [("sD0", mld3a[fc]), ("sD1", mld3b[fc])]
                        if fc < 2:
                            w.append(("sGB", mmul5[NOC1 - 1]))
                        else:
                            w.append(("sE2", msil[fc - 2]))
                        for ic in range(NOC1):
                            mm3[fc] = pr.raw("tensor", lambda e, ic=ic, b=b: e.matmul(
                                p_a[b][:], s_q3[b][:, ic * 128:(ic + 1) * 128], s_hh[ic][:],
                                start=(ic == 0), stop=(ic == NOC1 - 1)),
                                waits=w if ic == 0 else (),
                                inc="sMM" if ic == NOC1 - 1 else None)

                    def tail3(fc):
                        b = fc % 2
                        w = [("sMM", mm3[fc])]
                        if fc >= 2:
                            w.append(("sGB", madd[fc - 2]))
                        msil[fc] = pr.raw("scalar", lambda e, b=b: e.activation(
                            s_t1[b][:], p_a[b][:], AF.Silu), waits=w, inc="sE2")
                        pr.raw("vector", lambda e, fc=fc, b=b: e.tensor_tensor(
                            s_t1[b][:], s_t1[b][:], s_selv[fc // 4][:], ALU.mult),
                            waits=[("sE2", msil[fc])], inc="sGB")
                        w3 = []
                        if fc >= 2:
                            w3.append(("sLD", mst3[fc - 2]))
                        madd[fc] = pr.raw("vector", lambda e, fc=fc, b=b: e.tensor_add(
                            s_t3[b][:], s_xn[fc][:], s_t1[b][:]), waits=w3, inc="sGB")
                        mst3[fc] = pr.raw("gpsimd", lambda e, fc=fc, b=b, t0=t0: e.dma_start(
                            out=out_d[fc * 128:(fc + 1) * 128, t0:t0 + TW], in_=s_t3[b][:]),
                            waits=[("sGB", madd[fc])], inc="sLD")

                    ld12(0); ld12(1)
                    for oc in range(NOC1):
                        pe12(oc)
                        if oc + 2 < NOC1:
                            ld12(oc + 2)
                        elif oc + 2 < NOC1 + 2:
                            ld3(oc + 2 - NOC1)
                        tail12(oc)
                    for fc in range(NC_):
                        pe3(fc)
                        if fc + 2 < NC_:
                            ld3(fc + 2)
                        tail3(fc)
                    pr.barrier(("sMM", "sE2", "sGB", "sD0", "sD1", "sLD"))
                pr.flush()

        with nc.named_scope("p2"):
            phase_p2()
        with nc.named_scope("p3a"):
            phase_p3a()
        with nc.named_scope("rg"):
            emit_rgam()
        with nc.named_scope("p3b"):
            phase_p3b()
        with nc.named_scope("p3c"):
            phase_p3c()
        with nc.named_scope("ptd"):
            bg_pump(len(BG))
            pr.barrier(("sPA", "sPB", "sPC"))
            pr.flush()
        es_pt.close()
        with nc.named_scope("p4"):
            phase_p4()
        with nc.named_scope("p5"):
            phase_p5()

        dump_srcs = {"s": sf_d, "y1": y1f_d}
        for name, ap in dump_aps.items():
            if name == "stats":
                ST("gpsimd", lambda e, ap=ap: e.dma_start(out=ap, in_=s_stats[0:1, :]))
            else:
                ST("gpsimd", lambda e, ap=ap, name=name: e.dma_start(
                    out=ap, in_=dump_srcs[name]))
        pr.flush()

        pr.barrier(list(sems))
        pr.flush()
        final = pr.k
        block.sync(lambda e: e.wait_ge(tick, 16 * final))
    return nc


def _tile_wT(w, noc, nic):
    return np.ascontiguousarray(
        w.reshape(noc, 128, nic, 128).transpose(0, 3, 2, 1).reshape(noc, 128, nic * 128))


def _tile_w(w, nic, noc):
    return np.ascontiguousarray(
        w.reshape(nic, 128, noc, 128).transpose(2, 1, 0, 3).reshape(noc, 128, nic * 128))


def kernel(L0_attn_x, L0_attn_Wq, L0_attn_Wk, L0_attn_Wv, L0_attn_Wo,
           L0_ffn_router_w, L0_ffn_router_b, L0_ffn_expert_w, L0_ffn_expert_b,
           L0_ffn_out_w, L0_ffn_out_b, L1_attn_Wq, L1_attn_bq, L1_attn_Wkv,
           L1_attn_bkv, L1_ffn_w1, L1_ffn_w2, L1_ffn_w3):
    nc = build_program()
    f32 = lambda a: np.asarray(a, dtype=np.float32)
    ew = f32(L0_ffn_expert_w)
    ew_t = np.stack([_tile_w(ew[e], 16, 16) for e in range(EXP)])
    rw = f32(L0_ffn_router_w)
    rw_t = np.ascontiguousarray(rw.reshape(16, 128, EXP).transpose(1, 0, 2).reshape(128, 16 * EXP))
    shared = {
        "wqT": _tile_wT(f32(L0_attn_Wq), 16, 16),
        "wkT": _tile_wT(f32(L0_attn_Wk), 4, 16),
        "wvT": _tile_wT(f32(L0_attn_Wv), 4, 16),
        "woT": _tile_wT(f32(L0_attn_Wo), 16, 16),
        "rw": rw_t, "rb": np.ascontiguousarray(f32(L0_ffn_router_b)),
        "ew": ew_t, "eb": np.ascontiguousarray(f32(L0_ffn_expert_b)),
        "ow": _tile_w(f32(L0_ffn_out_w), 16, 16),
        "ob": np.ascontiguousarray(f32(L0_ffn_out_b)),
        "wkv": _tile_w(f32(L1_attn_Wkv), 16, 4),
        "bkv": np.ascontiguousarray(f32(L1_attn_bkv)),
        "wq1": np.ascontiguousarray(f32(L1_attn_Wq)),
        "bq1": np.ascontiguousarray(f32(L1_attn_bq)),
        "w1T": _tile_wT(f32(L1_ffn_w1), NOC1, 16),
        "w2T": _tile_wT(f32(L1_ffn_w2), NOC1, 16),
        "w3T": _tile_wT(f32(L1_ffn_w3), 16, NOC1),
    }
    xs = f32(L0_attn_x)
    in_maps = [dict(shared, xt=np.ascontiguousarray(xs[i].T)) for i in range(8)]
    res = run_bass_kernel_spmd(nc, in_maps, core_ids=list(range(8)))
    return np.stack([np.ascontiguousarray(np.asarray(r["out"], dtype=np.float32).T)
                     for r in res.results])


# version 21
# speedup vs baseline: 1.1551x; 1.0441x over previous
import math
from contextlib import ExitStack

import numpy as np
import concourse.bass as bass
import concourse.mybir as mybir
from concourse.bass_utils import run_bass_kernel_spmd

F32 = mybir.dt.float32
BF16 = mybir.dt.bfloat16
AF = mybir.ActivationFunctionType
ALU = mybir.AluOpType
AX = mybir.AxisListType

S, D = 4096, 2048
NH, NKV, HD = 16, 4, 128
KVF = NKV * HD
EXP, DEXP = 8, 2048
H1 = 5632
NC_ = D // 128
NOC1 = H1 // 128
SCALE0 = float(HD) ** -0.5
GELU_C2 = 2.0 * math.sqrt(2.0 / math.pi)
GELU_A = 0.044715
LN_EPS = 1e-5
Q_EPS = 1e-8

DEBUG_DUMPS = ()
PHASES = ("p2", "p3a", "p3b", "p3c", "p4", "p5")


def _consts():
    ident = np.eye(128, dtype=np.float32)
    r = np.zeros((128, 128), np.float32)
    for d in range(64):
        r[d, d + 64] = -1.0
        r[64 + d, d] = 1.0
    rT = np.ascontiguousarray(r.T)
    selT = np.zeros((128, 128), np.float32)
    for p in range(128):
        selT[64 + (p % 64), p] = 1.0
    sel8 = np.zeros((8, 8 * 128), np.float32)
    for e in range(8):
        sel8[e, e * 128:(e + 1) * 128] = 1.0
    j = np.arange(64, dtype=np.float64)
    inv_freq = 10000.0 ** (-2.0 * j / 128.0)
    t = np.arange(S, dtype=np.float64)
    freqs = np.outer(t, inv_freq)
    emb = np.concatenate([freqs, freqs], 1)
    cos_f = np.ascontiguousarray(np.cos(emb).T.astype(np.float32))
    sin_f = np.ascontiguousarray(np.sin(emb).T.astype(np.float32))
    return ident, rT, selT, sel8, cos_f, sin_f


class Prog:

    ENGS = ("sync", "gpsimd", "vector", "scalar", "tensor")
    SKIP_OK = ("tensor", "sync", "gpsimd")

    def __init__(self, nc, block, tick, sems):
        self.nc, self.block, self.tick = nc, block, tick
        self.sems = sems
        self.cnt = {n: 0 for n in sems}
        self.k = 0
        self._teng = None
        self.pending = []

    def step(self, eng, fn, after=None):
        idx = self.k
        self.k += 1
        prev_eng = self._teng
        self._teng = eng
        if after is None:
            wait = None if (prev_eng == eng and eng in self.SKIP_OK) else idx
        else:
            wait = after
        waits = [] if wait is None else [("TICK", wait)]
        self.pending.append((eng, fn, waits, "TICK"))
        return idx

    def raw(self, eng, fn, waits=(), inc=None):
        if inc is not None:
            self.cnt[inc] += 1
        self.pending.append((eng, fn, list(waits), inc))
        return self.cnt[inc] if inc is not None else None

    def barrier(self, names, eng="sync"):
        waits = [("TICK", self.k)] + [(n, self.cnt[n]) for n in names]
        idx = self.k
        self.k += 1
        self._teng = eng
        self.pending.append((eng, lambda e: e.nop(), waits, "TICK"))
        return idx

    def flush(self):
        pend, self.pending = self.pending, []
        self._teng = None
        if not pend:
            return
        for name in self.ENGS:
            sel = [(fn, ws, inc) for (e, fn, ws, inc) in pend if e == name]
            if not sel:
                continue

            def body(eng, sel=sel):
                for fn, ws, inc in sel:
                    for sn, cv in ws:
                        sem = self.tick if sn == "TICK" else self.sems[sn]
                        eng.wait_ge(sem, 16 * cv)
                    inst = fn(eng)
                    if inc == "TICK":
                        inst.then_inc(self.tick, 16)
                    elif inc is not None:
                        inst.then_inc(self.sems[inc], 16)

            getattr(self.block, name)(body)


def build_program(phases=PHASES, dumps=DEBUG_DUMPS):
    nc = bass.Bass("TRN2", target_bir_lowering=False, debug=False)
    ident_np, rT_np, selT_np, sel8_np, cos_np, sin_np = _consts()

    xt_d = nc.dram_tensor("xt", [D, S], F32, kind="ExternalInput").ap()
    wqT_d = nc.dram_tensor("wqT", [16, 128, 16 * 128], F32, kind="ExternalInput").ap()
    wkT_d = nc.dram_tensor("wkT", [4, 128, 16 * 128], F32, kind="ExternalInput").ap()
    wvT_d = nc.dram_tensor("wvT", [4, 128, 16 * 128], F32, kind="ExternalInput").ap()
    woT_d = nc.dram_tensor("woT", [16, 128, 16 * 128], F32, kind="ExternalInput").ap()
    rw_d = nc.dram_tensor("rw", [128, 16 * EXP], F32, kind="ExternalInput").ap()
    rb_d = nc.dram_tensor("rb", [EXP], F32, kind="ExternalInput").ap()
    ew_d = nc.dram_tensor("ew", [EXP, 16, 128, 16 * 128], F32, kind="ExternalInput").ap()
    eb_d = nc.dram_tensor("eb", [EXP, DEXP], F32, kind="ExternalInput").ap()
    ow_d = nc.dram_tensor("ow", [16, 128, 16 * 128], F32, kind="ExternalInput").ap()
    ob_d = nc.dram_tensor("ob", [D], F32, kind="ExternalInput").ap()
    wkv_d = nc.dram_tensor("wkv", [4, 128, 16 * 128], F32, kind="ExternalInput").ap()
    bkv_d = nc.dram_tensor("bkv", [KVF], F32, kind="ExternalInput").ap()
    nc.dram_tensor("wq1", [D, D], F32, kind="ExternalInput")
    nc.dram_tensor("bq1", [D], F32, kind="ExternalInput")
    w1T_d = nc.dram_tensor("w1T", [NOC1, 128, 16 * 128], F32, kind="ExternalInput").ap()
    w2T_d = nc.dram_tensor("w2T", [NOC1, 128, 16 * 128], F32, kind="ExternalInput").ap()
    w3T_d = nc.dram_tensor("w3T", [16, 128, NOC1 * 128], F32, kind="ExternalInput").ap()
    out_d = nc.dram_tensor("out", [D, S], F32, kind="ExternalOutput").ap()

    ident_d = nc.inline_tensor(ident_np, "c_ident").ap()
    rT_d = nc.inline_tensor(rT_np, "c_rT").ap()
    selT_d = nc.inline_tensor(selT_np, "c_selT").ap()
    sel8_d = nc.inline_tensor(sel8_np, "c_sel8").ap()
    cos_d = nc.inline_tensor(cos_np, "c_cos").ap()
    sin_d = nc.inline_tensor(sin_np, "c_sin").ap()

    q1b_d = nc.dram_tensor("q1b", [NOC1, 128, 16 * 128], BF16).ap()
    q2b_d = nc.dram_tensor("q2b", [NOC1, 128, 16 * 128], BF16).ap()
    q3b_d = nc.dram_tensor("q3b", [16, 128, NOC1 * 128], BF16).ap()
    xnh_d = nc.dram_tensor("xnh", [D, S], BF16).ap()
    ewh_d = nc.dram_tensor("ewh", [EXP, 16, 128, 16 * 128], BF16).ap()
    owh_d = nc.dram_tensor("owh", [16, 128, 16 * 128], BF16).ap()
    kth_d = nc.dram_tensor("kth", [4, 128, 32, 128], BF16).ap()
    vta_d = nc.dram_tensor("vta", [4, 128, 32, 136], BF16).ap()
    qfh_d = nc.dram_tensor("qfh", [D, S], BF16).ap()
    aof_d = nc.dram_tensor("aof", [D, S], BF16).ap()
    rbq_d = nc.dram_tensor("rbq", [2, 512], F32).ap()
    sf_d = nc.dram_tensor("sf", [D, S], F32).ap()
    y1f_d = nc.dram_tensor("y1f", [D, S], F32).ap()
    gb_d = nc.dram_tensor("gbscr", [8, 1024], F32).ap()
    dump_specs = {"s": [D, S], "y1": [D, S], "stats": [1, 64]}
    dump_aps = {n: nc.dram_tensor("dbg_" + n, shp, F32, kind="ExternalOutput").ap()
                for n, shp in dump_specs.items() if n in dumps}

    with ExitStack() as top:
        sbp = lambda n, s, dt=F32: top.enter_context(nc.sbuf_tensor(n, s, dt))
        s_ident = sbp("c_ident_s", [128, 128])
        s_identb = sbp("c_identb_s", [128, 128], BF16)
        s_rT = sbp("c_rT_s", [128, 128])
        s_selT = sbp("c_selT_s", [128, 128])
        s_sel8 = sbp("c_sel8_s", [8, 8 * 128])
        s_ones = sbp("c_ones_s", [128, 128])
        s_onesb = sbp("c_onesb_s", [128, 128], BF16)
        s_stats = sbp("stats_s", [128, 64])
        s_nbias = sbp("nbias_s", [128, 16])
        s_rgam = sbp("rgam_s", [128, 4])
        tick = top.enter_context(nc.semaphore("tick"))
        sems = {n: top.enter_context(nc.semaphore(n))
                for n in ("sMM", "sE2", "sGB", "sD0", "sD1", "sLD",
                          "sPA", "sPB", "sPC")}
        block = top.enter_context(nc.Block())
        pr = Prog(nc, block, tick, sems)
        ST = pr.step

        p_pt = top.enter_context(nc.psum_tensor("pt_p", [128, 16], F32))
        es_pt = ExitStack()
        sbq = lambda n, s, dt=F32: es_pt.enter_context(nc.sbuf_tensor(n, s, dt))
        s_ptw = [sbq(f"ptw{b}", [128, 2048]) for b in range(2)]
        s_ptq = sbq("ptq", [128, 2048])
        s_ptl = sbq("ptl", [128, 2048], BF16)
        s_ptg = [sbq(f"ptg{b}", [128, 2048], BF16) for b in range(2)]
        s_pta = sbq("pta", [128, 8])
        BG = []
        _pt = {"ji": 0, "mB": {}, "lastg": [None, None], "rgam_tick": None,
               "emitted": 0}

        def bg_pump(n):
            for _ in range(min(n, len(BG))):
                BG.pop(0)()
                _pt["emitted"] += 1

        def pt_job(kind, src, dst, m, wid, qsel=None):
            ji = _pt["ji"]; _pt["ji"] += 1
            b = ji % 2
            ldeng = qsel or ("sync" if b == 0 else "gpsimd")
            steng = qsel or ("gpsimd" if b == 0 else "sync")
            wl = [] if ji < 2 else [("sPB", _pt["mB"][ji - 2])]
            la = pr.raw(ldeng, lambda e, src=src, b=b, wid=wid: e.dma_start(
                out=s_ptw[b][:, 0:wid], in_=src), waits=wl, inc="sPA")
            if kind == "g":
                pr.raw("vector", lambda e, b=b, wid=wid: e.reduce_sum(
                    s_pta[:, 4 + b:5 + b], s_ptw[b][:, 0:wid], axis=AX.X,
                    apply_absolute_value=True),
                    waits=[("sPB", pr.cnt["sPB"]), ("sPA", la)], inc="sPB")
                _pt["mB"][ji] = pr.raw("vector", lambda e, b=b, m=m: e.tensor_add(
                    s_pta[:, m:m + 1], s_pta[:, m:m + 1], s_pta[:, 4 + b:5 + b]),
                    waits=[("sPB", pr.cnt["sPB"])], inc="sPB")
                return
            wg = [("sPB", pr.cnt["sPB"]), ("sPA", la)]
            if _pt["lastg"][b] is not None:
                wg.append(("sPC", _pt["lastg"][b]))
            if kind == "q":
                wg.append(("TICK", _pt["rgam_tick"]))
                pr.raw("vector", lambda e, b=b, m=m, wid=wid: e.tensor_scalar(
                    s_ptq[:, 0:wid], s_ptw[b][:, 0:wid], s_rgam[:, m:m + 1],
                    None, ALU.mult), waits=wg, inc="sPB")
                pr.raw("vector", lambda e, wid=wid: e.tensor_scalar(
                    s_ptl[:, 0:wid], s_ptq[:, 0:wid], -0.5, None, ALU.is_lt),
                    waits=[("sPB", pr.cnt["sPB"])], inc="sPB")
                pr.raw("vector", lambda e, b=b, wid=wid: e.tensor_scalar(
                    s_ptg[b][:, 0:wid], s_ptq[:, 0:wid], 0.5, None, ALU.is_gt),
                    waits=[("sPB", pr.cnt["sPB"])], inc="sPB")
                _pt["mB"][ji] = pr.raw("vector", lambda e, b=b, wid=wid: e.tensor_tensor(
                    s_ptg[b][:, 0:wid], s_ptg[b][:, 0:wid], s_ptl[:, 0:wid],
                    ALU.subtract), waits=[("sPB", pr.cnt["sPB"])], inc="sPB")
            else:
                _pt["mB"][ji] = pr.raw("vector", lambda e, b=b, wid=wid: e.tensor_copy(
                    s_ptg[b][:, 0:wid], s_ptw[b][:, 0:wid]), waits=wg, inc="sPB")
            _pt["lastg"][b] = pr.raw(steng, lambda e, dst=dst, b=b, wid=wid: e.dma_start(
                out=dst, in_=s_ptg[b][:, 0:wid]),
                waits=[("sPB", _pt["mB"][ji])], inc="sPC")

        def _w_slices(w_ap):
            nb, _, W = w_ap.shape
            for rb in range(nb):
                for off in range(0, W, 2048):
                    yield rb, off, min(2048, W - off)

        def queue_gamma_jobs():
            for m, w_ap in enumerate((w1T_d, w2T_d, w3T_d)):
                for rb, off, wid in _w_slices(w_ap):
                    BG.append(lambda w_ap=w_ap, rb=rb, off=off, wid=wid, m=m:
                              pt_job("g", w_ap[rb, :, off:off + wid], None, m, wid))

        def queue_cast_jobs():
            casts = [(ow_d[oc], owh_d[oc]) for oc in range(16)]
            casts += [(ew_d[ex, dc], ewh_d[ex, dc])
                      for ex in range(EXP) for dc in range(16)]
            for src, dst in casts:
                BG.append(lambda src=src, dst=dst: pt_job("c", src, dst, 0, 2048))

        def queue_quant_jobs():
            for m, (w_ap, q_ap) in enumerate(((w1T_d, q1b_d), (w2T_d, q2b_d),
                                              (w3T_d, q3b_d))):
                for rb, off, wid in _w_slices(w_ap):
                    BG.append(lambda w_ap=w_ap, q_ap=q_ap, rb=rb, off=off,
                              wid=wid, m=m: pt_job(
                                  "q", w_ap[rb, :, off:off + wid],
                                  q_ap[rb, :, off:off + wid], m, wid,
                                  qsel="gpsimd"))

        N_GAMMA = 136

        def emit_rgam():
            bg_pump(max(0, N_GAMMA - _pt["emitted"]))
            pr.barrier(("sPA", "sPB"))
            for m in range(3):
                ST("tensor", lambda e, m=m: e.matmul(
                    p_pt[0:1, 0:1], s_ones[:, 0:1], s_pta[:, m:m + 1],
                    start=True, stop=True))
                ST("vector", lambda e: e.tensor_scalar(
                    s_pta[0:1, 6:7], p_pt[0:1, 0:1], 1.0 / (H1 * D), None, ALU.mult))
                ST("vector", lambda e: e.tensor_scalar(
                    s_pta[0:1, 6:7], s_pta[0:1, 6:7], Q_EPS, None, ALU.add))
                ST("vector", lambda e: e.reciprocal(s_pta[0:1, 7:8], s_pta[0:1, 6:7]))
                ST("tensor", lambda e: e.matmul(
                    p_pt[:, 0:1], s_ones[0:1, :], s_pta[0:1, 7:8],
                    start=True, stop=True))
                ST("scalar", lambda e, m=m: e.activation(
                    s_rgam[:, m:m + 1], p_pt[:, 0:1], AF.Identity))
            pr.flush()
            _pt["rgam_tick"] = pr.k
            queue_quant_jobs()

        ST("sync", lambda e: e.dma_start(out=s_ident[:], in_=ident_d))
        ST("sync", lambda e: e.dma_start(out=s_rT[:], in_=rT_d))
        ST("sync", lambda e: e.dma_start(out=s_selT[:], in_=selT_d))
        ST("sync", lambda e: e.dma_start(out=s_sel8[:], in_=sel8_d))
        ST("vector", lambda e: e.memset(s_ones[:], 1.0))
        ST("vector", lambda e: e.memset(s_onesb[:], 1.0))
        ST("vector", lambda e: e.memset(s_stats[:], 0.0))
        ST("vector", lambda e: e.tensor_copy(s_identb[:], s_ident[:]))
        ST("vector", lambda e: e.memset(s_pta[:], 0.0))
        queue_gamma_jobs()
        queue_cast_jobs()

        def bcast_row(p_bc, row_ap, width):
            ST("tensor", lambda e: e.matmul(p_bc[:, 0:width], s_ones[0:1, :],
                                            row_ap, start=True, stop=True))

        def emit_layernorm(sx, sxn, p_row, p_bc, s_sq, s_st, tw):
            mu = s_st[0:1, 0:tw]
            msq = s_st[0:1, tw:2 * tw]
            t1 = s_st[0:1, 2 * tw:3 * tw]
            t2 = s_st[0:1, 3 * tw:4 * tw]
            for c in range(NC_):
                ST("tensor", lambda e, c=c: e.matmul(
                    p_row[0:1, 0:tw], s_ones[:, 0:1], sx[c][:, 0:tw],
                    start=(c == 0), stop=(c == NC_ - 1)))
            ST("vector", lambda e: e.tensor_scalar(mu, p_row[0:1, 0:tw], 1.0 / D, None, ALU.mult))
            for c in range(NC_):
                ST("scalar", lambda e, c=c: e.activation(s_sq[:, 0:tw], sx[c][:, 0:tw], AF.Square))
                ST("tensor", lambda e, c=c: e.matmul(
                    p_row[0:1, 0:tw], s_ones[:, 0:1], s_sq[:, 0:tw],
                    start=(c == 0), stop=(c == NC_ - 1)))
            ST("vector", lambda e: e.tensor_scalar(msq, p_row[0:1, 0:tw], 1.0 / D, None, ALU.mult))
            ST("vector", lambda e: e.tensor_tensor(t1, mu, mu, ALU.mult))
            ST("vector", lambda e: e.tensor_tensor(t2, msq, t1, ALU.subtract))
            ST("vector", lambda e: e.tensor_scalar(t2, t2, LN_EPS, None, ALU.add))
            ST("scalar", lambda e: e.activation(t1, t2, AF.Sqrt))
            ST("vector", lambda e: e.reciprocal(t2, t1))
            bcast_row(p_bc, mu, tw)
            ST("scalar", lambda e: e.activation(s_sq[:, 0:tw], p_bc[:, 0:tw], AF.Identity))
            bcast_row(p_bc, t2, tw)
            for c in range(NC_):
                ST("vector", lambda e, c=c: e.tensor_tensor(
                    sxn[c][:, 0:tw], sx[c][:, 0:tw], s_sq[:, 0:tw], ALU.subtract))
                ST("vector", lambda e, c=c: e.tensor_tensor(
                    sxn[c][:, 0:tw], sxn[c][:, 0:tw], p_bc[:, 0:tw], ALU.mult))

        def phase_p2():
            TW = 512
            with ExitStack() as es:
                sb = lambda n, s, dt=F32: es.enter_context(nc.sbuf_tensor(n, s, dt))
                ps = lambda n, s, dt=F32: es.enter_context(nc.psum_tensor(n, s, dt))
                s_xf = [sb(f"p2_xf{c}", [128, TW]) for c in range(NC_)]
                s_xn = s_xf
                s_xnh = [sb(f"p2_xnh{c}", [128, TW], BF16) for c in range(NC_)]
                s_sq = sb("p2_sq", [128, TW])
                s_st = sb("p2_st", [1, 4 * TW + 8])
                s_wst = sb("p2_wst", [128, 2048])
                s_wkb = sb("p2_wkb", [128, 4 * 16 * 128], BF16)
                s_wvb = sb("p2_wvb", [128, 4 * 16 * 128], BF16)
                s_k = sb("p2_k", [128, TW])
                s_kr = sb("p2_kr", [128, TW])
                s_khb = sb("p2_khb", [128, TW], BF16)
                s_t1 = sb("p2_t1", [128, TW])
                s_vb16 = sb("p2_vb16", [128, TW], BF16)
                s_vt = sb("p2_vt", [128, 4 * 136], BF16)
                s_cos = sb("p2_cos", [128, S])
                s_sin = sb("p2_sin", [128, S])
                p_trb = ps("p2_trb", [128, 128], BF16)
                p_row = ps("p2_row", [1, TW])
                p_bc = ps("p2_bc", [128, TW])
                p_kv = ps("p2_kv", [128, TW])
                p_rot = ps("p2_rot", [128, TW])

                ST("sync", lambda e: e.dma_start(out=s_cos[:], in_=cos_d))
                ST("sync", lambda e: e.dma_start(out=s_sin[:], in_=sin_d))
                for g in range(4):
                    ST("sync", lambda e, g=g: e.dma_start(out=s_wst[:], in_=wkT_d[g]))
                    ST("vector", lambda e, g=g: e.tensor_copy(
                        s_wkb[:, g * 2048:(g + 1) * 2048], s_wst[:]))
                    ST("sync", lambda e, g=g: e.dma_start(out=s_wst[:], in_=wvT_d[g]))
                    ST("vector", lambda e, g=g: e.tensor_copy(
                        s_wvb[:, g * 2048:(g + 1) * 2048], s_wst[:]))
                ST("vector", lambda e: e.memset(s_vt[:], 0.0))
                for j in range(4):
                    ST("vector", lambda e, j=j: e.memset(
                        s_vt[:, j * 136 + 128:j * 136 + 129], 1.0))

                for tb in range(S // TW):
                    t0 = tb * TW
                    for c in range(NC_):
                        ST("sync", lambda e, c=c, t0=t0: e.dma_start(
                            out=s_xf[c][:], in_=xt_d[c * 128:(c + 1) * 128, t0:t0 + TW]))
                    emit_layernorm(s_xf, s_xn, p_row, p_bc, s_sq, s_st, TW)
                    for c in range(NC_):
                        ST("vector", lambda e, c=c: e.tensor_copy(s_xnh[c][:], s_xn[c][:]))
                        ST("gpsimd", lambda e, c=c, t0=t0: e.dma_start(
                            out=xnh_d[c * 128:(c + 1) * 128, t0:t0 + TW], in_=s_xnh[c][:]))
                    for g in range(4):
                        for ic in range(NC_):
                            ST("tensor", lambda e, ic=ic, g=g: e.matmul(
                                p_kv[:], s_wkb[:, g * 2048 + ic * 128:g * 2048 + (ic + 1) * 128],
                                s_xnh[ic][:], start=(ic == 0), stop=(ic == NC_ - 1)))
                        ST("vector", lambda e: e.tensor_copy(s_k[:], p_kv[:]))
                        ST("tensor", lambda e: e.matmul(p_rot[:], s_rT[:], s_k[:],
                                                        start=True, stop=True))
                        ST("vector", lambda e, t0=t0: e.tensor_tensor(
                            s_kr[:], s_k[:], s_cos[:, t0:t0 + TW], ALU.mult))
                        ST("vector", lambda e, t0=t0: e.tensor_tensor(
                            s_t1[:], p_rot[:], s_sin[:, t0:t0 + TW], ALU.mult))
                        ST("vector", lambda e: e.tensor_add(s_kr[:], s_kr[:], s_t1[:]))
                        ST("vector", lambda e: e.tensor_copy(s_khb[:], s_kr[:]))
                        ST("gpsimd", lambda e, g=g, tb=tb: e.dma_start(
                            out=kth_d[g, :, tb * 4:(tb + 1) * 4, :],
                            in_=s_khb[:].rearrange("p (j f) -> p j f", f=128)))
                        ST("vector", lambda e: e.tensor_tensor(s_t1[:], s_kr[:], s_kr[:], ALU.mult))
                        ST("tensor", lambda e: e.matmul(p_row[0:1, :], s_ones[:, 0:1],
                                                        s_t1[:], start=True, stop=True))
                        ST("vector", lambda e: e.reduce_max(
                            s_st[0:1, 4 * TW:4 * TW + 1], p_row[0:1, :], axis=AX.X))
                        if tb == 0:
                            ST("vector", lambda e, g=g: e.tensor_copy(
                                s_stats[0:1, g:g + 1], s_st[0:1, 4 * TW:4 * TW + 1]))
                        else:
                            ST("vector", lambda e, g=g: e.tensor_tensor(
                                s_stats[0:1, g:g + 1], s_stats[0:1, g:g + 1],
                                s_st[0:1, 4 * TW:4 * TW + 1], ALU.max))
                        for ic in range(NC_):
                            ST("tensor", lambda e, ic=ic, g=g: e.matmul(
                                p_kv[:], s_wvb[:, g * 2048 + ic * 128:g * 2048 + (ic + 1) * 128],
                                s_xnh[ic][:], start=(ic == 0), stop=(ic == NC_ - 1)))
                        ST("vector", lambda e: e.tensor_copy(s_vb16[:], p_kv[:]))
                        for j in range(TW // 128):
                            ST("tensor", lambda e, j=j: e.transpose(
                                p_trb[:], s_vb16[:, j * 128:(j + 1) * 128], s_identb[:]))
                            ST("vector", lambda e, j=j: e.tensor_copy(
                                s_vt[:, j * 136:j * 136 + 128], p_trb[:]))
                        ST("gpsimd", lambda e, g=g, tb=tb: e.dma_start(
                            out=vta_d[g, :, tb * 4:(tb + 1) * 4, :],
                            in_=s_vt[:].rearrange("p (j f) -> p j f", f=136)))
                    bg_pump(20)
                pr.flush()

        def phase_p3a():
            TW = 512
            with ExitStack() as es:
                sb = lambda n, s, dt=F32: es.enter_context(nc.sbuf_tensor(n, s, dt))
                ps = lambda n, s: es.enter_context(nc.psum_tensor(n, s, F32))
                s_xnh = [sb(f"p3a_xnh{c}", [128, TW], BF16) for c in range(NC_)]
                s_wst = sb("p3a_wst", [128, 16 * 128])
                s_wqb = sb("p3a_wqb", [128, 16 * 16 * 128], BF16)
                s_q = sb("p3a_q", [128, TW])
                s_qr = sb("p3a_qr", [128, TW])
                s_qh = sb("p3a_qh", [128, TW], BF16)
                s_t1 = sb("p3a_t1", [128, TW])
                s_m = sb("p3a_m", [1, 2])
                s_cos = sb("p3a_cos", [128, S])
                s_sin = sb("p3a_sin", [128, S])
                p_q = ps("p3a_q_p", [128, TW])
                p_rot = ps("p3a_rot", [128, TW])
                p_row = ps("p3a_row", [1, TW])
                ST("sync", lambda e: e.dma_start(out=s_cos[:], in_=cos_d))
                ST("sync", lambda e: e.dma_start(out=s_sin[:], in_=sin_d))
                for h in range(NH):
                    ST("sync", lambda e, h=h: e.dma_start(out=s_wst[:], in_=wqT_d[h]))
                    ST("vector", lambda e, h=h: e.tensor_copy(
                        s_wqb[:, h * 2048:(h + 1) * 2048], s_wst[:]))
                for tb in range(S // TW):
                    t0 = tb * TW
                    for c in range(NC_):
                        ST("sync", lambda e, c=c, t0=t0: e.dma_start(
                            out=s_xnh[c][:], in_=xnh_d[c * 128:(c + 1) * 128, t0:t0 + TW]))
                    for h in range(NH):
                        for ic in range(NC_):
                            ST("tensor", lambda e, ic=ic, h=h: e.matmul(
                                p_q[:], s_wqb[:, (h * 16 + ic) * 128:(h * 16 + ic + 1) * 128],
                                s_xnh[ic][:], start=(ic == 0), stop=(ic == NC_ - 1)))
                        ST("vector", lambda e: e.tensor_copy(s_q[:], p_q[:]))
                        ST("tensor", lambda e: e.matmul(p_rot[:], s_rT[:], s_q[:],
                                                        start=True, stop=True))
                        ST("vector", lambda e, t0=t0: e.tensor_tensor(
                            s_qr[:], s_q[:], s_cos[:, t0:t0 + TW], ALU.mult))
                        ST("vector", lambda e, t0=t0: e.tensor_tensor(
                            s_t1[:], p_rot[:], s_sin[:, t0:t0 + TW], ALU.mult))
                        ST("vector", lambda e: e.tensor_add(s_qr[:], s_qr[:], s_t1[:]))
                        ST("vector", lambda e: e.tensor_copy(s_qh[:], s_qr[:]))
                        ST("gpsimd", lambda e, h=h, t0=t0: e.dma_start(
                            out=qfh_d[h * 128:(h + 1) * 128, t0:t0 + TW], in_=s_qh[:]))
                        ST("vector", lambda e: e.tensor_tensor(s_t1[:], s_qr[:], s_qr[:], ALU.mult))
                        ST("tensor", lambda e: e.matmul(p_row[0:1, :], s_ones[:, 0:1],
                                                        s_t1[:], start=True, stop=True))
                        ST("vector", lambda e: e.reduce_max(s_m[0:1, 0:1], p_row[0:1, :], axis=AX.X))
                        if tb == 0:
                            ST("vector", lambda e, h=h: e.tensor_copy(
                                s_stats[0:1, 4 + h:5 + h], s_m[0:1, 0:1]))
                        else:
                            ST("vector", lambda e, h=h: e.tensor_tensor(
                                s_stats[0:1, 4 + h:5 + h], s_stats[0:1, 4 + h:5 + h],
                                s_m[0:1, 0:1], ALU.max))
                        bg_pump(1)
                pr.flush()

        def phase_p3b():
            QW = 512
            with ExitStack() as es:
                sb = lambda n, s, dt=F32: es.enter_context(nc.sbuf_tensor(n, s, dt))
                ps = lambda n, s, dt=F32: es.enter_context(nc.psum_tensor(n, s, dt))
                s_kb = [sb(f"p3b_kb{b}", [128, 32 * 128], BF16) for b in range(2)]
                s_vb = [sb(f"p3b_vb{b}", [128, 32 * 136], BF16) for b in range(2)]
                s_qh = [sb(f"p3b_qh{b}", [128, QW], BF16) for b in range(4)]
                s_e = [sb(f"p3b_e{b}", [128, QW], BF16) for b in range(2)]
                s_r = [sb(f"p3b_r{b}", [1, QW]) for b in range(2)]
                s_rb = [sb(f"p3b_rb{b}", [128, QW]) for b in range(2)]
                s_aob = [sb(f"p3b_aob{b}", [128, QW], BF16) for b in range(2)]
                s_sc = sb("p3b_sc", [1, 16])
                p_s = [ps(f"p3b_s{b}", [128, QW]) for b in range(3)]
                p_o = ps("p3b_o", [128, QW])
                p_row = ps("p3b_row", [1, QW])

                ST("scalar", lambda e: e.activation(s_sc[0:1, 0:4], s_stats[0:1, 0:4], AF.Sqrt))
                ST("scalar", lambda e: e.activation(s_stats[0:1, 20:36],
                                                    s_stats[0:1, 4:20], AF.Sqrt))
                for h in range(NH):
                    ST("vector", lambda e, h=h: e.tensor_tensor(
                        s_stats[0:1, 40 + h:41 + h], s_stats[0:1, 20 + h:21 + h],
                        s_sc[0:1, h // 4:h // 4 + 1], ALU.mult))
                ST("vector", lambda e: e.tensor_scalar(
                    s_stats[0:1, 40:56], s_stats[0:1, 40:56], -SCALE0, None, ALU.mult))
                ST("tensor", lambda e: e.matmul(p_o[:, 0:16], s_ones[0:1, :],
                                                s_stats[0:1, 40:56], start=True, stop=True))
                ST("scalar", lambda e: e.activation(s_nbias[:], p_o[:, 0:16], AF.Identity))
                pr.flush()

                entry = pr.k
                msc, mA, mexp = {}, {}, {}
                mkv, mq, mstr, mldr, mrec, mmul, mst = ({}, {}, {}, {}, {}, {}, {})

                def kv_load(gi):
                    g = gi % 4
                    buf = gi % 2
                    w = [("TICK", entry)] if gi < 2 else [("sMM", mA[(gi - 2) * 4 * 32 + 4 * 32 - 1])]
                    pr.raw("sync", lambda e, g=g, buf=buf: e.dma_start(
                        out=s_kb[buf][:], in_=kth_d[g].rearrange("p j f -> p (j f)")),
                        waits=w, inc="sD1")
                    mkv[gi] = pr.raw("sync", lambda e, g=g, buf=buf: e.dma_start(
                        out=s_vb[buf][:], in_=vta_d[g].rearrange("p j f -> p (j f)")),
                        waits=w, inc="sD1")

                def q_load(t):
                    qb, h = t // NH, t % NH
                    buf = t % 4
                    w = ([("TICK", entry)] if t < 4
                         else [("sMM", msc[(t - 4) * 32 + 31])])
                    mq[t] = pr.raw("sync", lambda e, h=h, qb=qb, buf=buf: e.dma_start(
                        out=s_qh[buf][:],
                        in_=qfh_d[h * 128:(h + 1) * 128, qb * QW:(qb + 1) * QW]),
                        waits=w, inc="sD0")

                NT = (S // QW) * NH
                kv_load(0); kv_load(1)
                q_load(0); q_load(1); q_load(2); q_load(3)

                def pe_score(t, j):
                    J = t * 32 + j
                    gi = t // 4
                    w = []
                    if J >= 2 and J % 2 == 0:
                        w.append(("sE2", mexp[J - 2]))
                    if j == 0:
                        w += [("sD0", mq[t]), ("sD1", mkv[gi])]
                    msc[J] = pr.raw("tensor", lambda e, j=j, t=t, gi=gi: e.matmul(
                        p_s[J % 3][:], s_kb[gi % 2][:, j * 128:(j + 1) * 128],
                        s_qh[t % 4][:], start=True, stop=True), waits=w, inc="sMM")

                def pe_denom_av(t, j):
                    J = t * 32 + j
                    gi = t // 4
                    w = [("sE2", mexp[J])]
                    if j == 0 and t >= 1:
                        w.append(("sGB", mmul[t - 1]))
                    pr.raw("tensor", lambda e, j=j, t=t: e.matmul(
                        p_row[0:1, :], s_onesb[:, 0:1], s_e[J % 2][:],
                        start=(j == 0), stop=(j == 31)), waits=w)
                    mA[J] = pr.raw("tensor", lambda e, j=j, t=t, gi=gi: e.matmul(
                        p_o[:], s_vb[gi % 2][:, j * 136:j * 136 + 128],
                        s_e[J % 2][:], start=(j == 0), stop=(j == 31)), inc="sMM")

                def sc_exp(t, j):
                    J = t * 32 + j
                    h = t % NH
                    mexp[J] = pr.raw("scalar", lambda e, J=J, h=h: e.activation(
                        s_e[J % 2][:], p_s[J % 3][:], AF.Exp,
                        bias=s_nbias[:, h:h + 1], scale=SCALE0),
                        waits=[("sMM", msc[J])], inc="sE2")

                def dve_recip(t):
                    w = [("sMM", mA[t * 32 + 31])]
                    if t >= 2:
                        w.append(("sLD", mstr[t - 2]))
                    mrec[t] = pr.raw("vector", lambda e, t=t: e.reciprocal(
                        s_r[t % 2][0:1, :], p_row[0:1, :]), waits=w, inc="sGB")

                def r_store(t):
                    w = [("sGB", mrec[t])]
                    if t >= 2:
                        w.append(("sD0", mldr[t - 2]))
                    mstr[t] = pr.raw("gpsimd", lambda e, t=t: e.dma_start(
                        out=rbq_d[t % 2:t % 2 + 1, :], in_=s_r[t % 2][0:1, :]),
                        waits=w, inc="sLD")

                def r_bcast(t):
                    w = [("sLD", mstr[t])]
                    if t >= 2:
                        w.append(("sGB", mmul[t - 2]))
                    mldr[t] = pr.raw("sync", lambda e, t=t: e.dma_start(
                        out=s_rb[t % 2][:],
                        in_=rbq_d[t % 2:t % 2 + 1, :].partition_broadcast(128).squeeze(1)),
                        waits=w, inc="sD0")

                def dve_mult(t):
                    w = [("sD0", mldr[t])]
                    if t >= 2:
                        w.append(("sLD", mst[t - 2]))
                    mmul[t] = pr.raw("vector", lambda e, t=t: e.tensor_tensor(
                        s_aob[t % 2][:], p_o[:], s_rb[t % 2][:], ALU.mult),
                        waits=w, inc="sGB")

                def ao_store(t):
                    qb, h = t // NH, t % NH
                    mst[t] = pr.raw("gpsimd", lambda e, h=h, qb=qb, t=t: e.dma_start(
                        out=aof_d[h * 128:(h + 1) * 128, qb * QW:(qb + 1) * QW],
                        in_=s_aob[t % 2][:]),
                        waits=[("sGB", mmul[t])], inc="sLD")

                for t in range(NT):
                    pe_score(t, 0); sc_exp(t, 0)
                    pe_score(t, 1); sc_exp(t, 1)
                    if t >= 1:
                        dve_mult(t - 1)
                        ao_store(t - 1)
                    for j in range(30):
                        pe_denom_av(t, j)
                        pe_score(t, j + 2)
                        sc_exp(t, j + 2)
                    pe_denom_av(t, 30)
                    pe_denom_av(t, 31)
                    dve_recip(t)
                    r_store(t)
                    r_bcast(t)
                    if t + 4 < NT:
                        q_load(t + 4)
                    if t + 2 < NT and (t + 2) % 4 == 0 and (t + 2) // 4 >= 2:
                        kv_load((t + 2) // 4)
                    if t % 2 == 0:
                        bg_pump(1)
                dve_mult(NT - 1)
                ao_store(NT - 1)
                pr.barrier(("sMM", "sE2", "sGB", "sD0", "sD1", "sLD"))
                pr.flush()

        def phase_p3c():
            TW = 512
            with ExitStack() as es:
                sb = lambda n, s, dt=F32: es.enter_context(nc.sbuf_tensor(n, s, dt))
                ps = lambda n, s: es.enter_context(nc.psum_tensor(n, s, F32))
                s_ao = [sb(f"p3c_ao{c}", [128, TW], BF16) for c in range(NC_)]
                s_wst = sb("p3c_wst", [128, 16 * 128])
                s_wob = sb("p3c_wob", [128, 16 * 16 * 128], BF16)
                s_s = sb("p3c_s", [128, TW])
                p_s = ps("p3c_p", [128, TW])
                for oc in range(NC_):
                    ST("sync", lambda e, oc=oc: e.dma_start(out=s_wst[:], in_=woT_d[oc]))
                    ST("vector", lambda e, oc=oc: e.tensor_copy(
                        s_wob[:, oc * 2048:(oc + 1) * 2048], s_wst[:]))
                for tb in range(S // TW):
                    t0 = tb * TW
                    for c in range(NC_):
                        ST("sync", lambda e, c=c, t0=t0: e.dma_start(
                            out=s_ao[c][:], in_=aof_d[c * 128:(c + 1) * 128, t0:t0 + TW]))
                    for oc in range(NC_):
                        for ic in range(NC_):
                            ST("tensor", lambda e, ic=ic, oc=oc: e.matmul(
                                p_s[:], s_wob[:, (oc * 16 + ic) * 128:(oc * 16 + ic + 1) * 128],
                                s_ao[ic][:], start=(ic == 0), stop=(ic == NC_ - 1)))
                        ST("vector", lambda e: e.tensor_copy(s_s[:], p_s[:]))
                        ST("gpsimd", lambda e, oc=oc, t0=t0: e.dma_start(
                            out=sf_d[oc * 128:(oc + 1) * 128, t0:t0 + TW], in_=s_s[:]))
                        bg_pump(1)
                pr.flush()

        def phase_p4():
            TW = 1024
            NSUB = TW // 128
            with ExitStack() as es:
                sb = lambda n, s, dt=F32: es.enter_context(nc.sbuf_tensor(n, s, dt))
                ps = lambda n, s: es.enter_context(nc.psum_tensor(n, s, F32))
                s_xh = [sb(f"p4_xh{c}", [128, TW], BF16) for c in range(NC_)]
                s_cb = [sb(f"p4_cb{c}", [128, TW]) for c in range(NC_)]
                s_wh = [sb(f"p4_wh{b}", [128, 16 * 128], BF16) for b in range(2)]
                s_xld = [sb(f"p4_xld{b}", [128, TW]) for b in range(4)]
                s_sld = [sb(f"p4_sld{b}", [128, TW]) for b in range(4)]
                s_yt = [sb(f"p4_yt{b}", [128, TW]) for b in range(2)]
                s_rw = sb("p4_rw", [128, NC_ * EXP])
                s_rwh = sb("p4_rwh", [128, NC_ * EXP], BF16)
                s_rbb = sb("p4_rbb", [128, EXP])
                s_ebT = sb("p4_ebT", [128, 16 * 8])
                s_obT = sb("p4_obT", [128, 16])
                s_eb8 = sb("p4_eb8", [8, DEXP])
                s_ob1 = sb("p4_ob1", [1, D])
                s_lg = sb("p4_lg", [128, 8])
                s_m = sb("p4_m", [128, 8])
                s_gt = sb("p4_gt", [128, NSUB * 8])
                s_gf = sb("p4_gf", [8, TW])
                s_gbc = sb("p4_gbc", [128, TW])
                s_z = sb("p4_z", [128, TW])
                s_t1 = sb("p4_t1", [128, TW])
                s_t2 = sb("p4_t2", [128, TW])
                p_sm = ps("p4_sm", [128, 512])
                p_l = p_sm[:, 0:8]
                p_tr = p_sm[:, 128:256]
                p_e4 = [ps(f"p4_e4{b}", [128, TW]) for b in range(3)]
                gbn, bdn, mmn, e2n = {}, {}, {}, {}

                ST("sync", lambda e: e.dma_start(out=s_rw[:], in_=rw_d))
                ST("vector", lambda e: e.tensor_copy(s_rwh[:], s_rw[:]))
                ST("sync", lambda e: e.dma_start(out=s_lg[0:1, :], in_=rb_d.unsqueeze(0)))
                bcast_row(p_sm, s_lg[0:1, 0:8], 8)
                ST("scalar", lambda e: e.activation(s_rbb[:], p_l, AF.Identity))
                ST("sync", lambda e: e.dma_start(out=s_eb8[:], in_=eb_d))
                ST("sync", lambda e: e.dma_start(out=s_ob1[:], in_=ob_d.unsqueeze(0)))
                for dc in range(16):
                    ST("tensor", lambda e, dc=dc: e.transpose(
                        p_tr[:, 0:8], s_eb8[:, dc * 128:(dc + 1) * 128], s_ident[0:8, 0:8]))
                    ST("vector", lambda e, dc=dc: e.tensor_copy(
                        s_ebT[:, dc * 8:(dc + 1) * 8], p_tr[:, 0:8]))
                    ST("tensor", lambda e, dc=dc: e.transpose(
                        p_tr[:, 0:1], s_ob1[:, dc * 128:(dc + 1) * 128], s_ident[0:1, 0:1]))
                    ST("vector", lambda e, dc=dc: e.tensor_copy(
                        s_obT[:, dc:dc + 1], p_tr[:, 0:1]))

                for tb in range(S // TW):
                    t0 = tb * TW
                    for c in range(NC_):
                        ST("sync", lambda e, c=c, t0=t0: e.dma_start(
                            out=s_xh[c][:], in_=xnh_d[c * 128:(c + 1) * 128, t0:t0 + TW]))
                    for sub in range(NSUB):
                        for ic in range(NC_):
                            ST("tensor", lambda e, ic=ic, sub=sub: e.matmul(
                                p_l[:], s_xh[ic][:, sub * 128:(sub + 1) * 128],
                                s_rwh[:, ic * 8:(ic + 1) * 8],
                                start=(ic == 0),
                                stop=(ic == NC_ - 1)))
                        ST("vector", lambda e: e.tensor_tensor(s_lg[:], p_l[:], s_rbb[:], ALU.add))
                        ST("vector", lambda e: e.reduce_max(s_m[:, 0:1], s_lg[:], axis=AX.X))
                        ST("vector", lambda e, sub=sub: e.tensor_scalar(
                            s_gt[:, sub * 8:(sub + 1) * 8], s_lg[:], s_m[:, 0:1],
                            None, ALU.is_equal))
                        ST("vector", lambda e, sub=sub: e.scalar_tensor_tensor(
                            s_lg[:], s_gt[:, sub * 8:(sub + 1) * 8], -1e30, s_lg[:],
                            ALU.mult, ALU.add))
                        ST("vector", lambda e: e.reduce_max(s_m[:, 1:2], s_lg[:], axis=AX.X))
                        ST("vector", lambda e: e.tensor_scalar(
                            s_m[:, 4:5], s_m[:, 1:2], -1.0, None, ALU.mult))
                        ST("scalar", lambda e: e.activation(
                            s_m[:, 2:3], s_m[:, 0:1], AF.Sigmoid, bias=s_m[:, 4:5], scale=1.0))
                        ST("vector", lambda e: e.tensor_scalar(
                            s_m[:, 5:6], s_m[:, 0:1], -1.0, None, ALU.mult))
                        ST("scalar", lambda e: e.activation(
                            s_m[:, 3:4], s_m[:, 1:2], AF.Sigmoid, bias=s_m[:, 5:6], scale=1.0))
                        ST("vector", lambda e, sub=sub: e.tensor_scalar(
                            s_gt[:, sub * 8:(sub + 1) * 8], s_gt[:, sub * 8:(sub + 1) * 8],
                            s_m[:, 2:3], None, ALU.mult))
                        ST("vector", lambda e: e.tensor_scalar(
                            s_lg[:], s_lg[:], s_m[:, 1:2], None, ALU.is_equal))
                        ST("vector", lambda e: e.tensor_scalar(
                            s_lg[:], s_lg[:], s_m[:, 3:4], None, ALU.mult))
                        ST("vector", lambda e, sub=sub: e.tensor_add(
                            s_gt[:, sub * 8:(sub + 1) * 8], s_gt[:, sub * 8:(sub + 1) * 8],
                            s_lg[:]))
                        ST("tensor", lambda e, sub=sub: e.transpose(
                            p_tr[0:8, 0:128], s_gt[:, sub * 8:(sub + 1) * 8], s_ident[:]))
                        ST("vector", lambda e, sub=sub: e.tensor_copy(
                            s_gf[0:8, sub * 128:(sub + 1) * 128], p_tr[0:8, 0:128]))
                    ST("gpsimd", lambda e: e.dma_start(out=gb_d, in_=s_gf[:]))

                    entry = pr.k
                    gbn[0] = pr.raw("sync", lambda e: e.dma_start(
                        out=s_gbc[:],
                        in_=gb_d[0:1, :].partition_broadcast(128).squeeze(1)),
                        waits=[("TICK", entry)], inc="sGB")

                    groups = ([("exp", ex, dc) for ex in range(EXP) for dc in range(16)]
                              + [("out", 0, oc) for oc in range(16)])
                    NG = len(groups)
                    mxs, msty = {}, {}

                    def emit_band(li):
                        gg = tb * NG + li
                        kind, ex, dc = groups[li]
                        buf = gg % 2
                        src = ewh_d[ex, dc] if kind == "exp" else owh_d[dc]
                        w = ([("sMM", mmn[gg - 2])] if (gg - 2) in mmn
                             else [("TICK", entry)])
                        bdn[gg] = pr.raw(
                            "sync" if buf == 0 else "gpsimd",
                            lambda e, src=src, buf=buf: e.dma_start(
                                out=s_wh[buf][:], in_=src),
                            waits=w, inc="sD%d" % buf)

                    def emit_xs(oc):
                        b = oc % 4
                        w = [("TICK", entry)] if oc < 4 else [("sLD", msty[oc - 4])]
                        pr.raw("sync", lambda e, oc=oc, b=b, t0=t0: e.dma_start(
                            out=s_xld[b][:], in_=xt_d[oc * 128:(oc + 1) * 128, t0:t0 + TW]),
                            waits=w, inc="sLD")
                        mxs[oc] = pr.raw("sync", lambda e, oc=oc, b=b, t0=t0: e.dma_start(
                            out=s_sld[b][:], in_=sf_d[oc * 128:(oc + 1) * 128, t0:t0 + TW]),
                            waits=w, inc="sLD")

                    def EL(eng, fn, extra=()):
                        pr.raw(eng, fn,
                               waits=[("sE2", pr.cnt["sE2"])] + list(extra), inc="sE2")

                    emit_band(0)
                    emit_band(1)
                    for li, (kind, ex, dc) in enumerate(groups):
                        gg = tb * NG + li
                        buf = gg % 2
                        pbuf = gg % 3
                        w = [("TICK", entry), ("sD%d" % buf, bdn[gg])]
                        if (gg - 3) in e2n:
                            w.append(("sE2", e2n[gg - 3]))
                        if kind == "out" and dc == 0:
                            w.append(("sE2", e2n[gg - 1]))
                        first = True
                        for hh in range(2):
                            for ic in range(NC_):
                                inc = "sMM" if (hh == 1 and ic == NC_ - 1) else None
                                pr.raw("tensor", lambda e, hh=hh, buf=buf,
                                       pbuf=pbuf, ic=ic: e.matmul(
                                    p_e4[pbuf][:, hh * 512:(hh + 1) * 512],
                                    s_wh[buf][:, ic * 128:(ic + 1) * 128],
                                    s_xh[ic][:, hh * 512:(hh + 1) * 512],
                                    start=(ic == 0), stop=(ic == NC_ - 1)),
                                    waits=w if first else (), inc=inc)
                                first = False
                        mmn[gg] = pr.cnt["sMM"]
                        if li + 2 < NG:
                            emit_band(li + 2)
                        if kind == "exp":
                            for hh in range(2):
                                EL("scalar", lambda e, hh=hh, dc=dc, ex=ex, pbuf=pbuf:
                                   e.activation(
                                       s_z[:, hh * 512:(hh + 1) * 512],
                                       p_e4[pbuf][:, hh * 512:(hh + 1) * 512], AF.Identity,
                                       bias=s_ebT[:, dc * 8 + ex:dc * 8 + ex + 1]),
                                   extra=[("sMM", mmn[gg])] if hh == 0 else ())
                            EL("scalar", lambda e: e.activation(s_t1[:], s_z[:], AF.Square))
                            EL("vector", lambda e: e.tensor_scalar(
                                s_t1[:], s_t1[:], GELU_A, 1.0, ALU.mult, ALU.add))
                            EL("vector", lambda e: e.tensor_tensor(
                                s_t1[:], s_t1[:], s_z[:], ALU.mult))
                            EL("scalar", lambda e: e.activation(
                                s_t1[:], s_t1[:], AF.Sigmoid, scale=GELU_C2))
                            EL("vector", lambda e: e.tensor_tensor(
                                s_t1[:], s_t1[:], s_z[:], ALU.mult))
                            if ex == 0:
                                EL("vector", lambda e, dc=dc: e.tensor_tensor(
                                    s_cb[dc][:], s_t1[:], s_gbc[:], ALU.mult),
                                   extra=[("sGB", gbn[ex])] if dc == 0 else ())
                            else:
                                EL("vector", lambda e: e.tensor_tensor(
                                    s_t2[:], s_t1[:], s_gbc[:], ALU.mult),
                                   extra=[("sGB", gbn[ex])] if dc == 0 else ())
                                EL("vector", lambda e, dc=dc: e.tensor_add(
                                    s_cb[dc][:], s_cb[dc][:], s_t2[:]))
                            if dc == 15:
                                if ex + 1 < EXP:
                                    gbn[ex + 1] = pr.raw(
                                        "sync", lambda e, ex=ex: e.dma_start(
                                            out=s_gbc[:],
                                            in_=gb_d[ex + 1:ex + 2, :]
                                            .partition_broadcast(128).squeeze(1)),
                                        waits=[("sE2", pr.cnt["sE2"])], inc="sGB")
                                else:
                                    for c in range(NC_):
                                        EL("vector", lambda e, c=c: e.tensor_copy(
                                            s_xh[c][:], s_cb[c][:]))
                        else:
                            oc = dc
                            if oc == 0:
                                for o4 in range(4):
                                    emit_xs(o4)
                            for hh in range(2):
                                EL("scalar", lambda e, hh=hh, oc=oc, pbuf=pbuf: e.activation(
                                    s_z[:, hh * 512:(hh + 1) * 512],
                                    p_e4[pbuf][:, hh * 512:(hh + 1) * 512],
                                    AF.Silu, bias=s_obT[:, oc:oc + 1]),
                                   extra=[("sMM", mmn[gg])] if hh == 0 else ())
                            ex2 = [("sLD", mxs[oc])]
                            if oc >= 2:
                                ex2.append(("sLD", msty[oc - 2]))
                            EL("vector", lambda e, oc=oc: e.tensor_tensor(
                                s_yt[oc % 2][:], s_z[:], s_sld[oc % 4][:], ALU.mult),
                               extra=ex2)
                            EL("vector", lambda e, oc=oc: e.tensor_add(
                                s_yt[oc % 2][:], s_yt[oc % 2][:], s_xld[oc % 4][:]))
                            msty[oc] = pr.raw(
                                "gpsimd", lambda e, oc=oc, t0=t0: e.dma_start(
                                    out=y1f_d[oc * 128:(oc + 1) * 128, t0:t0 + TW],
                                    in_=s_yt[oc % 2][:]),
                                waits=[("sE2", pr.cnt["sE2"])], inc="sLD")
                            if oc + 4 < NC_:
                                emit_xs(oc + 4)
                        e2n[gg] = pr.cnt["sE2"]
                    pr.barrier(("sMM", "sE2", "sGB", "sD0", "sD1", "sLD"))
                pr.flush()

        def phase_p5():
            TW = 512
            with ExitStack() as es:
                sb = lambda n, s, dt=F32: es.enter_context(nc.sbuf_tensor(n, s, dt))
                ps = lambda n, s: es.enter_context(nc.psum_tensor(n, s, F32))
                s_xn = [sb(f"p5_xn{c}", [128, TW]) for c in range(NC_)]
                s_xh = [sb(f"p5_xh{c}", [128, TW], BF16) for c in range(NC_)]
                s_hh = [sb(f"p5_hh{o}", [128, TW], BF16) for o in range(NOC1)]
                s_kv = [sb(f"p5_kv{g}", [128, TW]) for g in range(4)]
                s_selv = [sb(f"p5_sv{g}", [128, TW]) for g in range(4)]
                s_wst = sb("p5_wst", [128, 16 * 128])
                s_wkvb = sb("p5_wkvb", [128, 4 * 16 * 128], BF16)
                s_q1 = [sb(f"p5_q1{b}", [128, 16 * 128], BF16) for b in range(2)]
                s_q2 = [sb(f"p5_q2{b}", [128, 16 * 128], BF16) for b in range(2)]
                s_q3 = [sb(f"p5_q3{b}", [128, NOC1 * 128], BF16) for b in range(2)]
                s_bkvT = sb("p5_bkvT", [128, 4])
                s_bkv1 = sb("p5_bkv1", [1, KVF])
                s_t1 = [sb(f"p5_t1{b}", [128, TW]) for b in range(2)]
                s_t3 = [sb(f"p5_t3{b}", [128, TW]) for b in range(2)]
                s_sq = sb("p5_sq", [128, TW])
                s_st = sb("p5_st", [1, 4 * TW + 8])
                p_a = [ps(f"p5_a{b}", [128, TW]) for b in range(2)]
                p_b2 = [ps(f"p5_b2{b}", [128, TW]) for b in range(2)]
                p_row = ps("p5_row", [1, TW])
                p_bc = ps("p5_bc", [128, TW])
                p_tr = ps("p5_tr", [128, 128])

                ST("sync", lambda e: e.dma_start(out=s_bkv1[:], in_=bkv_d.unsqueeze(0)))
                for g in range(4):
                    ST("tensor", lambda e, g=g: e.transpose(
                        p_tr[:, 0:1], s_bkv1[:, g * 128:(g + 1) * 128], s_ident[0:1, 0:1]))
                    ST("vector", lambda e, g=g: e.tensor_copy(s_bkvT[:, g:g + 1], p_tr[:, 0:1]))
                for g in range(4):
                    ST("sync", lambda e, g=g: e.dma_start(out=s_wst[:], in_=wkv_d[g]))
                    ST("vector", lambda e, g=g: e.tensor_copy(
                        s_wkvb[:, g * 2048:(g + 1) * 2048], s_wst[:]))

                for tb in range(S // TW):
                    t0 = tb * TW
                    for c in range(NC_):
                        ST("sync", lambda e, c=c, t0=t0: e.dma_start(
                            out=s_xn[c][:], in_=y1f_d[c * 128:(c + 1) * 128, t0:t0 + TW]))
                    emit_layernorm(s_xn, s_xn, p_row, p_bc, s_sq, s_st, TW)
                    for c in range(NC_):
                        ST("vector", lambda e, c=c: e.tensor_copy(s_xh[c][:], s_xn[c][:]))
                    for g in range(4):
                        for ic in range(NC_):
                            ST("tensor", lambda e, ic=ic, g=g: e.matmul(
                                p_a[0][:], s_wkvb[:, g * 2048 + ic * 128:g * 2048 + (ic + 1) * 128],
                                s_xh[ic][:], start=(ic == 0), stop=(ic == NC_ - 1)))
                        ST("scalar", lambda e, g=g: e.activation(
                            s_kv[g][:], p_a[0][:], AF.Identity, bias=s_bkvT[:, g:g + 1]))
                        ST("tensor", lambda e, g=g: e.matmul(
                            p_b2[0][:], s_selT[:], s_kv[g][:], start=True, stop=True))
                        ST("vector", lambda e, g=g: e.tensor_copy(s_selv[g][:], p_b2[0][:]))
                    for c in range(NC_):
                        ST("sync", lambda e, c=c, t0=t0: e.dma_start(
                            out=s_xn[c][:], in_=y1f_d[c * 128:(c + 1) * 128, t0:t0 + TW]))
                    pr.flush()

                    entry = pr.k
                    mld1, mld2, mld3a, mld3b = {}, {}, {}, {}
                    mmA, mmB, mm3, msig, msil, mmul5, madd, mst3 = ({}, {}, {},
                                                                    {}, {}, {}, {}, {})

                    def ld12(oc):
                        b = oc % 2
                        w1 = ([("TICK", entry)] if oc < 2 else [("sMM", mmA[oc - 2])])
                        w2 = ([("TICK", entry)] if oc < 2 else [("sMM", mmB[oc - 2])])
                        mld1[oc] = pr.raw("sync", lambda e, oc=oc, b=b: e.dma_start(
                            out=s_q1[b][:], in_=q1b_d[oc]), waits=w1, inc="sD0")
                        mld2[oc] = pr.raw("gpsimd", lambda e, oc=oc, b=b: e.dma_start(
                            out=s_q2[b][:], in_=q2b_d[oc]), waits=w2, inc="sD1")

                    def ld3(fc):
                        b = fc % 2
                        HC = (NOC1 // 2) * 128
                        w = ([("TICK", entry)] if fc < 2 else [("sMM", mm3[fc - 2])])
                        mld3a[fc] = pr.raw("sync", lambda e, fc=fc, b=b: e.dma_start(
                            out=s_q3[b][:, 0:HC], in_=q3b_d[fc, :, 0:HC]),
                            waits=w, inc="sD0")
                        mld3b[fc] = pr.raw("gpsimd", lambda e, fc=fc, b=b: e.dma_start(
                            out=s_q3[b][:, HC:2 * HC], in_=q3b_d[fc, :, HC:2 * HC]),
                            waits=w, inc="sD1")

                    def pe12(oc):
                        b = oc % 2
                        w = [("sD0", mld1[oc])]
                        if oc >= 2:
                            w += [("sE2", msig[oc - 2]), ("sGB", mmul5[oc - 2])]
                        if oc == 0:
                            w.append(("TICK", entry))
                        for ic in range(NC_):
                            mmA[oc] = pr.raw("tensor", lambda e, ic=ic, b=b: e.matmul(
                                p_a[b][:], s_q1[b][:, ic * 128:(ic + 1) * 128], s_xh[ic][:],
                                start=(ic == 0), stop=(ic == NC_ - 1)),
                                waits=w if ic == 0 else (),
                                inc="sMM" if ic == NC_ - 1 else None)
                        w2 = [("sD1", mld2[oc])]
                        for ic in range(NC_):
                            mmB[oc] = pr.raw("tensor", lambda e, ic=ic, b=b: e.matmul(
                                p_b2[b][:], s_q2[b][:, ic * 128:(ic + 1) * 128], s_xh[ic][:],
                                start=(ic == 0), stop=(ic == NC_ - 1)),
                                waits=w2 if ic == 0 else (),
                                inc="sMM" if ic == NC_ - 1 else None)

                    def tail12(oc):
                        b = oc % 2
                        w = [("sMM", mmA[oc])]
                        if oc >= 2:
                            w.append(("sGB", mmul5[oc - 2]))
                        msig[oc] = pr.raw("scalar", lambda e, b=b: e.activation(
                            s_t1[b][:], p_a[b][:], AF.Sigmoid), waits=w, inc="sE2")
                        mmul5[oc] = pr.raw("vector", lambda e, oc=oc, b=b: e.tensor_tensor(
                            s_hh[oc][:], s_t1[b][:], p_b2[b][:], ALU.mult),
                            waits=[("sE2", msig[oc]), ("sMM", mmB[oc])], inc="sGB")

                    def pe3(fc):
                        b = fc % 2
                        w = [("sD0", mld3a[fc]), ("sD1", mld3b[fc])]
                        if fc < 2:
                            w.append(("sGB", mmul5[NOC1 - 1]))
                        else:
                            w.append(("sE2", msil[fc - 2]))
                        for ic in range(NOC1):
                            mm3[fc] = pr.raw("tensor", lambda e, ic=ic, b=b: e.matmul(
                                p_a[b][:], s_q3[b][:, ic * 128:(ic + 1) * 128], s_hh[ic][:],
                                start=(ic == 0), stop=(ic == NOC1 - 1)),
                                waits=w if ic == 0 else (),
                                inc="sMM" if ic == NOC1 - 1 else None)

                    def tail3(fc):
                        b = fc % 2
                        w = [("sMM", mm3[fc])]
                        if fc >= 2:
                            w.append(("sGB", madd[fc - 2]))
                        msil[fc] = pr.raw("scalar", lambda e, b=b: e.activation(
                            s_t1[b][:], p_a[b][:], AF.Silu), waits=w, inc="sE2")
                        pr.raw("vector", lambda e, fc=fc, b=b: e.tensor_tensor(
                            s_t1[b][:], s_t1[b][:], s_selv[fc // 4][:], ALU.mult),
                            waits=[("sE2", msil[fc])], inc="sGB")
                        w3 = []
                        if fc >= 2:
                            w3.append(("sLD", mst3[fc - 2]))
                        madd[fc] = pr.raw("vector", lambda e, fc=fc, b=b: e.tensor_add(
                            s_t3[b][:], s_xn[fc][:], s_t1[b][:]), waits=w3, inc="sGB")
                        mst3[fc] = pr.raw("gpsimd", lambda e, fc=fc, b=b, t0=t0: e.dma_start(
                            out=out_d[fc * 128:(fc + 1) * 128, t0:t0 + TW], in_=s_t3[b][:]),
                            waits=[("sGB", madd[fc])], inc="sLD")

                    ld12(0); ld12(1)
                    for oc in range(NOC1):
                        pe12(oc)
                        if oc + 2 < NOC1:
                            ld12(oc + 2)
                        elif oc + 2 < NOC1 + 2:
                            ld3(oc + 2 - NOC1)
                        tail12(oc)
                    for fc in range(NC_):
                        pe3(fc)
                        if fc + 2 < NC_:
                            ld3(fc + 2)
                        tail3(fc)
                    pr.barrier(("sMM", "sE2", "sGB", "sD0", "sD1", "sLD"))
                pr.flush()

        with nc.named_scope("p2"):
            phase_p2()
        with nc.named_scope("p3a"):
            phase_p3a()
        with nc.named_scope("rg"):
            emit_rgam()
        with nc.named_scope("p3b"):
            phase_p3b()
        with nc.named_scope("p3c"):
            phase_p3c()
        with nc.named_scope("ptd"):
            bg_pump(len(BG))
            pr.barrier(("sPA", "sPB", "sPC"))
            pr.flush()
        es_pt.close()
        with nc.named_scope("p4"):
            phase_p4()
        with nc.named_scope("p5"):
            phase_p5()

        dump_srcs = {"s": sf_d, "y1": y1f_d}
        for name, ap in dump_aps.items():
            if name == "stats":
                ST("gpsimd", lambda e, ap=ap: e.dma_start(out=ap, in_=s_stats[0:1, :]))
            else:
                ST("gpsimd", lambda e, ap=ap, name=name: e.dma_start(
                    out=ap, in_=dump_srcs[name]))
        pr.flush()

        pr.barrier(list(sems))
        pr.flush()
        final = pr.k
        block.sync(lambda e: e.wait_ge(tick, 16 * final))
    return nc


def _tile_wT(w, noc, nic):
    return np.ascontiguousarray(
        w.reshape(noc, 128, nic, 128).transpose(0, 3, 2, 1).reshape(noc, 128, nic * 128))


def _tile_w(w, nic, noc):
    return np.ascontiguousarray(
        w.reshape(nic, 128, noc, 128).transpose(2, 1, 0, 3).reshape(noc, 128, nic * 128))


def kernel(L0_attn_x, L0_attn_Wq, L0_attn_Wk, L0_attn_Wv, L0_attn_Wo,
           L0_ffn_router_w, L0_ffn_router_b, L0_ffn_expert_w, L0_ffn_expert_b,
           L0_ffn_out_w, L0_ffn_out_b, L1_attn_Wq, L1_attn_bq, L1_attn_Wkv,
           L1_attn_bkv, L1_ffn_w1, L1_ffn_w2, L1_ffn_w3):
    nc = build_program()
    f32 = lambda a: np.asarray(a, dtype=np.float32)
    ew = f32(L0_ffn_expert_w)
    ew_t = np.stack([_tile_w(ew[e], 16, 16) for e in range(EXP)])
    rw = f32(L0_ffn_router_w)
    rw_t = np.ascontiguousarray(rw.reshape(16, 128, EXP).transpose(1, 0, 2).reshape(128, 16 * EXP))
    shared = {
        "wqT": _tile_wT(f32(L0_attn_Wq), 16, 16),
        "wkT": _tile_wT(f32(L0_attn_Wk), 4, 16),
        "wvT": _tile_wT(f32(L0_attn_Wv), 4, 16),
        "woT": _tile_wT(f32(L0_attn_Wo), 16, 16),
        "rw": rw_t, "rb": np.ascontiguousarray(f32(L0_ffn_router_b)),
        "ew": ew_t, "eb": np.ascontiguousarray(f32(L0_ffn_expert_b)),
        "ow": _tile_w(f32(L0_ffn_out_w), 16, 16),
        "ob": np.ascontiguousarray(f32(L0_ffn_out_b)),
        "wkv": _tile_w(f32(L1_attn_Wkv), 16, 4),
        "bkv": np.ascontiguousarray(f32(L1_attn_bkv)),
        "wq1": np.ascontiguousarray(f32(L1_attn_Wq)),
        "bq1": np.ascontiguousarray(f32(L1_attn_bq)),
        "w1T": _tile_wT(f32(L1_ffn_w1), NOC1, 16),
        "w2T": _tile_wT(f32(L1_ffn_w2), NOC1, 16),
        "w3T": _tile_wT(f32(L1_ffn_w3), 16, NOC1),
    }
    xs = f32(L0_attn_x)
    in_maps = [dict(shared, xt=np.ascontiguousarray(xs[i].T)) for i in range(8)]
    res = run_bass_kernel_spmd(nc, in_maps, core_ids=list(range(8)))
    return np.stack([np.ascontiguousarray(np.asarray(r["out"], dtype=np.float32).T)
                     for r in res.results])


# version 23 (final)
# speedup vs baseline: 1.1679x; 1.0111x over previous
import math
from contextlib import ExitStack

import numpy as np
import concourse.bass as bass
import concourse.mybir as mybir
from concourse.bass_utils import run_bass_kernel_spmd

F32 = mybir.dt.float32
BF16 = mybir.dt.bfloat16
AF = mybir.ActivationFunctionType
ALU = mybir.AluOpType
AX = mybir.AxisListType

S, D = 4096, 2048
NH, NKV, HD = 16, 4, 128
KVF = NKV * HD
EXP, DEXP = 8, 2048
H1 = 5632
NC_ = D // 128
NOC1 = H1 // 128
SCALE0 = float(HD) ** -0.5
GELU_C2 = 2.0 * math.sqrt(2.0 / math.pi)
GELU_A = 0.044715
LN_EPS = 1e-5
Q_EPS = 1e-8

DEBUG_DUMPS = ()
PHASES = ("p2", "p3a", "p3b", "p3c", "p4", "p5")


def _consts():
    ident = np.eye(128, dtype=np.float32)
    r = np.zeros((128, 128), np.float32)
    for d in range(64):
        r[d, d + 64] = -1.0
        r[64 + d, d] = 1.0
    rT = np.ascontiguousarray(r.T)
    selT = np.zeros((128, 128), np.float32)
    for p in range(128):
        selT[64 + (p % 64), p] = 1.0
    sel8 = np.zeros((8, 8 * 128), np.float32)
    for e in range(8):
        sel8[e, e * 128:(e + 1) * 128] = 1.0
    j = np.arange(64, dtype=np.float64)
    inv_freq = 10000.0 ** (-2.0 * j / 128.0)
    t = np.arange(S, dtype=np.float64)
    freqs = np.outer(t, inv_freq)
    emb = np.concatenate([freqs, freqs], 1)
    cos_f = np.ascontiguousarray(np.cos(emb).T.astype(np.float32))
    sin_f = np.ascontiguousarray(np.sin(emb).T.astype(np.float32))
    return ident, rT, selT, sel8, cos_f, sin_f


class Prog:

    ENGS = ("sync", "gpsimd", "vector", "scalar", "tensor")
    SKIP_OK = ("tensor", "sync", "gpsimd")

    def __init__(self, nc, block, tick, sems):
        self.nc, self.block, self.tick = nc, block, tick
        self.sems = sems
        self.cnt = {n: 0 for n in sems}
        self.k = 0
        self._teng = None
        self.pending = []

    def step(self, eng, fn, after=None):
        idx = self.k
        self.k += 1
        prev_eng = self._teng
        self._teng = eng
        if after is None:
            wait = None if (prev_eng == eng and eng in self.SKIP_OK) else idx
        else:
            wait = after
        waits = [] if wait is None else [("TICK", wait)]
        self.pending.append((eng, fn, waits, "TICK"))
        return idx

    def raw(self, eng, fn, waits=(), inc=None):
        if inc is not None:
            self.cnt[inc] += 1
        self.pending.append((eng, fn, list(waits), inc))
        return self.cnt[inc] if inc is not None else None

    def barrier(self, names, eng="sync"):
        waits = [("TICK", self.k)] + [(n, self.cnt[n]) for n in names]
        idx = self.k
        self.k += 1
        self._teng = eng
        self.pending.append((eng, lambda e: e.nop(), waits, "TICK"))
        return idx

    def flush(self):
        pend, self.pending = self.pending, []
        self._teng = None
        if not pend:
            return
        for name in self.ENGS:
            sel = [(fn, ws, inc) for (e, fn, ws, inc) in pend if e == name]
            if not sel:
                continue

            def body(eng, sel=sel):
                for fn, ws, inc in sel:
                    for sn, cv in ws:
                        sem = self.tick if sn == "TICK" else self.sems[sn]
                        eng.wait_ge(sem, 16 * cv)
                    inst = fn(eng)
                    if inc == "TICK":
                        inst.then_inc(self.tick, 16)
                    elif inc is not None:
                        inst.then_inc(self.sems[inc], 16)

            getattr(self.block, name)(body)


def build_program(phases=PHASES, dumps=DEBUG_DUMPS):
    nc = bass.Bass("TRN2", target_bir_lowering=False, debug=False)
    ident_np, rT_np, selT_np, sel8_np, cos_np, sin_np = _consts()

    xt_d = nc.dram_tensor("xt", [D, S], F32, kind="ExternalInput").ap()
    wqT_d = nc.dram_tensor("wqT", [16, 128, 16 * 128], F32, kind="ExternalInput").ap()
    wkT_d = nc.dram_tensor("wkT", [4, 128, 16 * 128], F32, kind="ExternalInput").ap()
    wvT_d = nc.dram_tensor("wvT", [4, 128, 16 * 128], F32, kind="ExternalInput").ap()
    woT_d = nc.dram_tensor("woT", [16, 128, 16 * 128], F32, kind="ExternalInput").ap()
    rw_d = nc.dram_tensor("rw", [128, 16 * EXP], F32, kind="ExternalInput").ap()
    rb_d = nc.dram_tensor("rb", [EXP], F32, kind="ExternalInput").ap()
    ew_d = nc.dram_tensor("ew", [EXP, 16, 128, 16 * 128], F32, kind="ExternalInput").ap()
    eb_d = nc.dram_tensor("eb", [EXP, DEXP], F32, kind="ExternalInput").ap()
    ow_d = nc.dram_tensor("ow", [16, 128, 16 * 128], F32, kind="ExternalInput").ap()
    ob_d = nc.dram_tensor("ob", [D], F32, kind="ExternalInput").ap()
    wkv_d = nc.dram_tensor("wkv", [4, 128, 16 * 128], F32, kind="ExternalInput").ap()
    bkv_d = nc.dram_tensor("bkv", [KVF], F32, kind="ExternalInput").ap()
    nc.dram_tensor("wq1", [D, D], F32, kind="ExternalInput")
    nc.dram_tensor("bq1", [D], F32, kind="ExternalInput")
    w1T_d = nc.dram_tensor("w1T", [NOC1, 128, 16 * 128], F32, kind="ExternalInput").ap()
    w2T_d = nc.dram_tensor("w2T", [NOC1, 128, 16 * 128], F32, kind="ExternalInput").ap()
    w3T_d = nc.dram_tensor("w3T", [16, 128, NOC1 * 128], F32, kind="ExternalInput").ap()
    out_d = nc.dram_tensor("out", [D, S], F32, kind="ExternalOutput").ap()

    ident_d = nc.inline_tensor(ident_np, "c_ident").ap()
    rT_d = nc.inline_tensor(rT_np, "c_rT").ap()
    selT_d = nc.inline_tensor(selT_np, "c_selT").ap()
    sel8_d = nc.inline_tensor(sel8_np, "c_sel8").ap()
    cos_d = nc.inline_tensor(cos_np, "c_cos").ap()
    sin_d = nc.inline_tensor(sin_np, "c_sin").ap()

    q1b_d = nc.dram_tensor("q1b", [NOC1, 128, 16 * 128], BF16).ap()
    q2b_d = nc.dram_tensor("q2b", [NOC1, 128, 16 * 128], BF16).ap()
    q3b_d = nc.dram_tensor("q3b", [16, 128, NOC1 * 128], BF16).ap()
    xnh_d = nc.dram_tensor("xnh", [D, S], BF16).ap()
    ewh_d = nc.dram_tensor("ewh", [EXP, 16, 128, 16 * 128], BF16).ap()
    owh_d = nc.dram_tensor("owh", [16, 128, 16 * 128], BF16).ap()
    kth_d = nc.dram_tensor("kth", [4, 128, 32, 128], BF16).ap()
    vta_d = nc.dram_tensor("vta", [4, 128, 32, 136], BF16).ap()
    qfh_d = nc.dram_tensor("qfh", [D, S], BF16).ap()
    aof_d = nc.dram_tensor("aof", [D, S], BF16).ap()
    rbq_d = nc.dram_tensor("rbq", [2, 512], F32).ap()
    sf_d = nc.dram_tensor("sf", [D, S], F32).ap()
    y1f_d = nc.dram_tensor("y1f", [D, S], F32).ap()
    gb_d = nc.dram_tensor("gbscr", [8, 1024], F32).ap()
    dump_specs = {"s": [D, S], "y1": [D, S], "stats": [1, 64]}
    dump_aps = {n: nc.dram_tensor("dbg_" + n, shp, F32, kind="ExternalOutput").ap()
                for n, shp in dump_specs.items() if n in dumps}

    with ExitStack() as top:
        sbp = lambda n, s, dt=F32: top.enter_context(nc.sbuf_tensor(n, s, dt))
        s_ident = sbp("c_ident_s", [128, 128])
        s_identb = sbp("c_identb_s", [128, 128], BF16)
        s_rT = sbp("c_rT_s", [128, 128])
        s_selT = sbp("c_selT_s", [128, 128])
        s_sel8 = sbp("c_sel8_s", [8, 8 * 128])
        s_ones = sbp("c_ones_s", [128, 128])
        s_onesb = sbp("c_onesb_s", [128, 128], BF16)
        s_stats = sbp("stats_s", [128, 64])
        s_nbias = sbp("nbias_s", [128, 16])
        s_rgam = sbp("rgam_s", [128, 4])
        tick = top.enter_context(nc.semaphore("tick"))
        sems = {n: top.enter_context(nc.semaphore(n))
                for n in ("sMM", "sE2", "sGB", "sD0", "sD1", "sLD",
                          "sPA", "sPB", "sPC")}
        block = top.enter_context(nc.Block())
        pr = Prog(nc, block, tick, sems)
        ST = pr.step

        p_pt = top.enter_context(nc.psum_tensor("pt_p", [128, 16], F32))
        es_pt = ExitStack()
        sbq = lambda n, s, dt=F32: es_pt.enter_context(nc.sbuf_tensor(n, s, dt))
        s_ptw = [sbq(f"ptw{b}", [128, 2048]) for b in range(2)]
        s_ptq = sbq("ptq", [128, 2048])
        s_ptl = sbq("ptl", [128, 2048], BF16)
        s_ptg = [sbq(f"ptg{b}", [128, 2048], BF16) for b in range(2)]
        s_pta = sbq("pta", [128, 8])
        BG = []
        _pt = {"ji": 0, "mB": {}, "lastg": [None, None], "rgam_tick": None,
               "emitted": 0}

        def bg_pump(n):
            for _ in range(min(n, len(BG))):
                BG.pop(0)()
                _pt["emitted"] += 1

        def pt_job(kind, src, dst, m, wid, qsel=None):
            ji = _pt["ji"]; _pt["ji"] += 1
            b = ji % 2
            ldeng = qsel or ("sync" if b == 0 else "gpsimd")
            steng = qsel or ("gpsimd" if b == 0 else "sync")
            wl = [] if ji < 2 else [("sPB", _pt["mB"][ji - 2])]
            la = pr.raw(ldeng, lambda e, src=src, b=b, wid=wid: e.dma_start(
                out=s_ptw[b][:, 0:wid], in_=src), waits=wl, inc="sPA")
            if kind == "g":
                pr.raw("vector", lambda e, b=b, wid=wid: e.reduce_sum(
                    s_pta[:, 4 + b:5 + b], s_ptw[b][:, 0:wid], axis=AX.X,
                    apply_absolute_value=True),
                    waits=[("sPB", pr.cnt["sPB"]), ("sPA", la)], inc="sPB")
                _pt["mB"][ji] = pr.raw("vector", lambda e, b=b, m=m: e.tensor_add(
                    s_pta[:, m:m + 1], s_pta[:, m:m + 1], s_pta[:, 4 + b:5 + b]),
                    waits=[("sPB", pr.cnt["sPB"])], inc="sPB")
                return
            wg = [("sPB", pr.cnt["sPB"]), ("sPA", la)]
            if _pt["lastg"][b] is not None:
                wg.append(("sPC", _pt["lastg"][b]))
            if kind == "q":
                wg.append(("TICK", _pt["rgam_tick"]))
                pr.raw("vector", lambda e, b=b, m=m, wid=wid: e.tensor_scalar(
                    s_ptq[:, 0:wid], s_ptw[b][:, 0:wid], s_rgam[:, m:m + 1],
                    None, ALU.mult), waits=wg, inc="sPB")
                pr.raw("vector", lambda e, wid=wid: e.tensor_scalar(
                    s_ptl[:, 0:wid], s_ptq[:, 0:wid], -0.5, None, ALU.is_lt),
                    waits=[("sPB", pr.cnt["sPB"])], inc="sPB")
                pr.raw("vector", lambda e, b=b, wid=wid: e.tensor_scalar(
                    s_ptg[b][:, 0:wid], s_ptq[:, 0:wid], 0.5, None, ALU.is_gt),
                    waits=[("sPB", pr.cnt["sPB"])], inc="sPB")
                _pt["mB"][ji] = pr.raw("vector", lambda e, b=b, wid=wid: e.tensor_tensor(
                    s_ptg[b][:, 0:wid], s_ptg[b][:, 0:wid], s_ptl[:, 0:wid],
                    ALU.subtract), waits=[("sPB", pr.cnt["sPB"])], inc="sPB")
            else:
                _pt["mB"][ji] = pr.raw("vector", lambda e, b=b, wid=wid: e.tensor_copy(
                    s_ptg[b][:, 0:wid], s_ptw[b][:, 0:wid]), waits=wg, inc="sPB")
            _pt["lastg"][b] = pr.raw(steng, lambda e, dst=dst, b=b, wid=wid: e.dma_start(
                out=dst, in_=s_ptg[b][:, 0:wid]),
                waits=[("sPB", _pt["mB"][ji])], inc="sPC")

        def _w_slices(w_ap):
            nb, _, W = w_ap.shape
            for rb in range(nb):
                for off in range(0, W, 2048):
                    yield rb, off, min(2048, W - off)

        def queue_gamma_jobs():
            for m, w_ap in enumerate((w1T_d, w2T_d, w3T_d)):
                for rb, off, wid in _w_slices(w_ap):
                    BG.append(lambda w_ap=w_ap, rb=rb, off=off, wid=wid, m=m:
                              pt_job("g", w_ap[rb, :, off:off + wid], None, m, wid))

        def queue_cast_jobs():
            casts = [(ow_d[oc], owh_d[oc]) for oc in range(16)]
            casts += [(ew_d[ex, dc], ewh_d[ex, dc])
                      for ex in range(EXP) for dc in range(16)]
            for src, dst in casts:
                BG.append(lambda src=src, dst=dst: pt_job("c", src, dst, 0, 2048))

        def queue_quant_jobs():
            for m, (w_ap, q_ap) in enumerate(((w1T_d, q1b_d), (w2T_d, q2b_d),
                                              (w3T_d, q3b_d))):
                for rb, off, wid in _w_slices(w_ap):
                    BG.append(lambda w_ap=w_ap, q_ap=q_ap, rb=rb, off=off,
                              wid=wid, m=m: pt_job(
                                  "q", w_ap[rb, :, off:off + wid],
                                  q_ap[rb, :, off:off + wid], m, wid,
                                  qsel="gpsimd"))

        N_GAMMA = 136

        def emit_rgam():
            bg_pump(max(0, N_GAMMA - _pt["emitted"]))
            pr.barrier(("sPA", "sPB"))
            for m in range(3):
                ST("tensor", lambda e, m=m: e.matmul(
                    p_pt[0:1, 0:1], s_ones[:, 0:1], s_pta[:, m:m + 1],
                    start=True, stop=True))
                ST("vector", lambda e: e.tensor_scalar(
                    s_pta[0:1, 6:7], p_pt[0:1, 0:1], 1.0 / (H1 * D), None, ALU.mult))
                ST("vector", lambda e: e.tensor_scalar(
                    s_pta[0:1, 6:7], s_pta[0:1, 6:7], Q_EPS, None, ALU.add))
                ST("vector", lambda e: e.reciprocal(s_pta[0:1, 7:8], s_pta[0:1, 6:7]))
                ST("tensor", lambda e: e.matmul(
                    p_pt[:, 0:1], s_ones[0:1, :], s_pta[0:1, 7:8],
                    start=True, stop=True))
                ST("scalar", lambda e, m=m: e.activation(
                    s_rgam[:, m:m + 1], p_pt[:, 0:1], AF.Identity))
            pr.flush()
            _pt["rgam_tick"] = pr.k
            queue_quant_jobs()

        ST("sync", lambda e: e.dma_start(out=s_ident[:], in_=ident_d))
        ST("sync", lambda e: e.dma_start(out=s_rT[:], in_=rT_d))
        ST("sync", lambda e: e.dma_start(out=s_selT[:], in_=selT_d))
        ST("sync", lambda e: e.dma_start(out=s_sel8[:], in_=sel8_d))
        ST("vector", lambda e: e.memset(s_ones[:], 1.0))
        ST("vector", lambda e: e.memset(s_onesb[:], 1.0))
        ST("vector", lambda e: e.memset(s_stats[:], 0.0))
        ST("vector", lambda e: e.tensor_copy(s_identb[:], s_ident[:]))
        ST("vector", lambda e: e.memset(s_pta[:], 0.0))
        queue_gamma_jobs()
        queue_cast_jobs()

        def bcast_row(p_bc, row_ap, width):
            ST("tensor", lambda e: e.matmul(p_bc[:, 0:width], s_ones[0:1, :],
                                            row_ap, start=True, stop=True))

        def emit_layernorm(sx, sxn, p_row, p_bc, s_sq, s_st, tw):
            mu = s_st[0:1, 0:tw]
            msq = s_st[0:1, tw:2 * tw]
            t1 = s_st[0:1, 2 * tw:3 * tw]
            t2 = s_st[0:1, 3 * tw:4 * tw]
            for c in range(NC_):
                ST("tensor", lambda e, c=c: e.matmul(
                    p_row[0:1, 0:tw], s_ones[:, 0:1], sx[c][:, 0:tw],
                    start=(c == 0), stop=(c == NC_ - 1)))
            ST("vector", lambda e: e.tensor_scalar(mu, p_row[0:1, 0:tw], 1.0 / D, None, ALU.mult))
            for c in range(NC_):
                ST("scalar", lambda e, c=c: e.activation(s_sq[:, 0:tw], sx[c][:, 0:tw], AF.Square))
                ST("tensor", lambda e, c=c: e.matmul(
                    p_row[0:1, 0:tw], s_ones[:, 0:1], s_sq[:, 0:tw],
                    start=(c == 0), stop=(c == NC_ - 1)))
            ST("vector", lambda e: e.tensor_scalar(msq, p_row[0:1, 0:tw], 1.0 / D, None, ALU.mult))
            ST("vector", lambda e: e.tensor_tensor(t1, mu, mu, ALU.mult))
            ST("vector", lambda e: e.tensor_tensor(t2, msq, t1, ALU.subtract))
            ST("vector", lambda e: e.tensor_scalar(t2, t2, LN_EPS, None, ALU.add))
            ST("scalar", lambda e: e.activation(t1, t2, AF.Sqrt))
            ST("vector", lambda e: e.reciprocal(t2, t1))
            bcast_row(p_bc, mu, tw)
            ST("scalar", lambda e: e.activation(s_sq[:, 0:tw], p_bc[:, 0:tw], AF.Identity))
            bcast_row(p_bc, t2, tw)
            for c in range(NC_):
                ST("vector", lambda e, c=c: e.tensor_tensor(
                    sxn[c][:, 0:tw], sx[c][:, 0:tw], s_sq[:, 0:tw], ALU.subtract))
                ST("vector", lambda e, c=c: e.tensor_tensor(
                    sxn[c][:, 0:tw], sxn[c][:, 0:tw], p_bc[:, 0:tw], ALU.mult))

        def phase_p2():
            TW = 512
            with ExitStack() as es:
                sb = lambda n, s, dt=F32: es.enter_context(nc.sbuf_tensor(n, s, dt))
                ps = lambda n, s, dt=F32: es.enter_context(nc.psum_tensor(n, s, dt))
                s_xf = [sb(f"p2_xf{c}", [128, TW]) for c in range(NC_)]
                s_xn = s_xf
                s_xnh = [sb(f"p2_xnh{c}", [128, TW], BF16) for c in range(NC_)]
                s_sq = sb("p2_sq", [128, TW])
                s_st = sb("p2_st", [1, 4 * TW + 8])
                s_wst = sb("p2_wst", [128, 2048])
                s_wkb = sb("p2_wkb", [128, 4 * 16 * 128], BF16)
                s_wvb = sb("p2_wvb", [128, 4 * 16 * 128], BF16)
                s_k = sb("p2_k", [128, TW])
                s_kr = sb("p2_kr", [128, TW])
                s_khb = sb("p2_khb", [128, TW], BF16)
                s_t1 = sb("p2_t1", [128, TW])
                s_vb16 = sb("p2_vb16", [128, TW], BF16)
                s_vt = sb("p2_vt", [128, 4 * 136], BF16)
                s_cos = sb("p2_cos", [128, S])
                s_sin = sb("p2_sin", [128, S])
                p_trb = ps("p2_trb", [128, 128], BF16)
                p_row = ps("p2_row", [1, TW])
                p_bc = ps("p2_bc", [128, TW])
                p_kv = ps("p2_kv", [128, TW])
                p_rot = ps("p2_rot", [128, TW])

                ST("sync", lambda e: e.dma_start(out=s_cos[:], in_=cos_d))
                ST("sync", lambda e: e.dma_start(out=s_sin[:], in_=sin_d))
                for g in range(4):
                    ST("sync", lambda e, g=g: e.dma_start(out=s_wst[:], in_=wkT_d[g]))
                    ST("vector", lambda e, g=g: e.tensor_copy(
                        s_wkb[:, g * 2048:(g + 1) * 2048], s_wst[:]))
                    ST("sync", lambda e, g=g: e.dma_start(out=s_wst[:], in_=wvT_d[g]))
                    ST("vector", lambda e, g=g: e.tensor_copy(
                        s_wvb[:, g * 2048:(g + 1) * 2048], s_wst[:]))
                ST("vector", lambda e: e.memset(s_vt[:], 0.0))
                for j in range(4):
                    ST("vector", lambda e, j=j: e.memset(
                        s_vt[:, j * 136 + 128:j * 136 + 129], 1.0))

                for tb in range(S // TW):
                    t0 = tb * TW
                    for c in range(NC_):
                        ST("sync", lambda e, c=c, t0=t0: e.dma_start(
                            out=s_xf[c][:], in_=xt_d[c * 128:(c + 1) * 128, t0:t0 + TW]))
                    emit_layernorm(s_xf, s_xn, p_row, p_bc, s_sq, s_st, TW)
                    for c in range(NC_):
                        ST("vector", lambda e, c=c: e.tensor_copy(s_xnh[c][:], s_xn[c][:]))
                        ST("gpsimd", lambda e, c=c, t0=t0: e.dma_start(
                            out=xnh_d[c * 128:(c + 1) * 128, t0:t0 + TW], in_=s_xnh[c][:]))
                    for g in range(4):
                        for ic in range(NC_):
                            ST("tensor", lambda e, ic=ic, g=g: e.matmul(
                                p_kv[:], s_wkb[:, g * 2048 + ic * 128:g * 2048 + (ic + 1) * 128],
                                s_xnh[ic][:], start=(ic == 0), stop=(ic == NC_ - 1)))
                        ST("vector", lambda e: e.tensor_copy(s_k[:], p_kv[:]))
                        ST("tensor", lambda e: e.matmul(p_rot[:], s_rT[:], s_k[:],
                                                        start=True, stop=True))
                        ST("vector", lambda e, t0=t0: e.tensor_tensor(
                            s_kr[:], s_k[:], s_cos[:, t0:t0 + TW], ALU.mult))
                        ST("vector", lambda e, t0=t0: e.tensor_tensor(
                            s_t1[:], p_rot[:], s_sin[:, t0:t0 + TW], ALU.mult))
                        ST("vector", lambda e: e.tensor_add(s_kr[:], s_kr[:], s_t1[:]))
                        ST("vector", lambda e: e.tensor_copy(s_khb[:], s_kr[:]))
                        ST("gpsimd", lambda e, g=g, tb=tb: e.dma_start(
                            out=kth_d[g, :, tb * 4:(tb + 1) * 4, :],
                            in_=s_khb[:].rearrange("p (j f) -> p j f", f=128)))
                        ST("vector", lambda e: e.tensor_tensor(s_t1[:], s_kr[:], s_kr[:], ALU.mult))
                        ST("tensor", lambda e: e.matmul(p_row[0:1, :], s_ones[:, 0:1],
                                                        s_t1[:], start=True, stop=True))
                        ST("vector", lambda e: e.reduce_max(
                            s_st[0:1, 4 * TW:4 * TW + 1], p_row[0:1, :], axis=AX.X))
                        if tb == 0:
                            ST("vector", lambda e, g=g: e.tensor_copy(
                                s_stats[0:1, g:g + 1], s_st[0:1, 4 * TW:4 * TW + 1]))
                        else:
                            ST("vector", lambda e, g=g: e.tensor_tensor(
                                s_stats[0:1, g:g + 1], s_stats[0:1, g:g + 1],
                                s_st[0:1, 4 * TW:4 * TW + 1], ALU.max))
                        for ic in range(NC_):
                            ST("tensor", lambda e, ic=ic, g=g: e.matmul(
                                p_kv[:], s_wvb[:, g * 2048 + ic * 128:g * 2048 + (ic + 1) * 128],
                                s_xnh[ic][:], start=(ic == 0), stop=(ic == NC_ - 1)))
                        ST("vector", lambda e: e.tensor_copy(s_vb16[:], p_kv[:]))
                        for j in range(TW // 128):
                            ST("tensor", lambda e, j=j: e.transpose(
                                p_trb[:], s_vb16[:, j * 128:(j + 1) * 128], s_identb[:]))
                            ST("vector", lambda e, j=j: e.tensor_copy(
                                s_vt[:, j * 136:j * 136 + 128], p_trb[:]))
                        ST("gpsimd", lambda e, g=g, tb=tb: e.dma_start(
                            out=vta_d[g, :, tb * 4:(tb + 1) * 4, :],
                            in_=s_vt[:].rearrange("p (j f) -> p j f", f=136)))
                    bg_pump(11)
                pr.flush()

        def phase_p3a():
            TW = 512
            with ExitStack() as es:
                sb = lambda n, s, dt=F32: es.enter_context(nc.sbuf_tensor(n, s, dt))
                ps = lambda n, s: es.enter_context(nc.psum_tensor(n, s, F32))
                s_xnh = [sb(f"p3a_xnh{c}", [128, TW], BF16) for c in range(NC_)]
                s_wst = sb("p3a_wst", [128, 16 * 128])
                s_wqb = sb("p3a_wqb", [128, 16 * 16 * 128], BF16)
                s_q = sb("p3a_q", [128, TW])
                s_qr = sb("p3a_qr", [128, TW])
                s_qh = sb("p3a_qh", [128, TW], BF16)
                s_t1 = sb("p3a_t1", [128, TW])
                s_m = sb("p3a_m", [1, 2])
                s_cos = sb("p3a_cos", [128, S])
                s_sin = sb("p3a_sin", [128, S])
                p_q = ps("p3a_q_p", [128, TW])
                p_rot = ps("p3a_rot", [128, TW])
                p_row = ps("p3a_row", [1, TW])
                ST("sync", lambda e: e.dma_start(out=s_cos[:], in_=cos_d))
                ST("sync", lambda e: e.dma_start(out=s_sin[:], in_=sin_d))
                for h in range(NH):
                    ST("sync", lambda e, h=h: e.dma_start(out=s_wst[:], in_=wqT_d[h]))
                    ST("vector", lambda e, h=h: e.tensor_copy(
                        s_wqb[:, h * 2048:(h + 1) * 2048], s_wst[:]))
                for tb in range(S // TW):
                    t0 = tb * TW
                    for c in range(NC_):
                        ST("sync", lambda e, c=c, t0=t0: e.dma_start(
                            out=s_xnh[c][:], in_=xnh_d[c * 128:(c + 1) * 128, t0:t0 + TW]))
                    for h in range(NH):
                        for ic in range(NC_):
                            ST("tensor", lambda e, ic=ic, h=h: e.matmul(
                                p_q[:], s_wqb[:, (h * 16 + ic) * 128:(h * 16 + ic + 1) * 128],
                                s_xnh[ic][:], start=(ic == 0), stop=(ic == NC_ - 1)))
                        ST("vector", lambda e: e.tensor_copy(s_q[:], p_q[:]))
                        ST("tensor", lambda e: e.matmul(p_rot[:], s_rT[:], s_q[:],
                                                        start=True, stop=True))
                        ST("vector", lambda e, t0=t0: e.tensor_tensor(
                            s_qr[:], s_q[:], s_cos[:, t0:t0 + TW], ALU.mult))
                        ST("vector", lambda e, t0=t0: e.tensor_tensor(
                            s_t1[:], p_rot[:], s_sin[:, t0:t0 + TW], ALU.mult))
                        ST("vector", lambda e: e.tensor_add(s_qr[:], s_qr[:], s_t1[:]))
                        ST("vector", lambda e: e.tensor_copy(s_qh[:], s_qr[:]))
                        ST("gpsimd", lambda e, h=h, t0=t0: e.dma_start(
                            out=qfh_d[h * 128:(h + 1) * 128, t0:t0 + TW], in_=s_qh[:]))
                        ST("vector", lambda e: e.tensor_tensor(s_t1[:], s_qr[:], s_qr[:], ALU.mult))
                        ST("tensor", lambda e: e.matmul(p_row[0:1, :], s_ones[:, 0:1],
                                                        s_t1[:], start=True, stop=True))
                        ST("vector", lambda e: e.reduce_max(s_m[0:1, 0:1], p_row[0:1, :], axis=AX.X))
                        if tb == 0:
                            ST("vector", lambda e, h=h: e.tensor_copy(
                                s_stats[0:1, 4 + h:5 + h], s_m[0:1, 0:1]))
                        else:
                            ST("vector", lambda e, h=h: e.tensor_tensor(
                                s_stats[0:1, 4 + h:5 + h], s_stats[0:1, 4 + h:5 + h],
                                s_m[0:1, 0:1], ALU.max))
                        bg_pump(2)
                pr.flush()

        def phase_p3b():
            QW = 512
            with ExitStack() as es:
                sb = lambda n, s, dt=F32: es.enter_context(nc.sbuf_tensor(n, s, dt))
                ps = lambda n, s, dt=F32: es.enter_context(nc.psum_tensor(n, s, dt))
                s_kb = [sb(f"p3b_kb{b}", [128, 32 * 128], BF16) for b in range(2)]
                s_vb = [sb(f"p3b_vb{b}", [128, 32 * 136], BF16) for b in range(2)]
                s_qh = [sb(f"p3b_qh{b}", [128, QW], BF16) for b in range(4)]
                s_e = [sb(f"p3b_e{b}", [128, QW], BF16) for b in range(2)]
                s_r = [sb(f"p3b_r{b}", [1, QW]) for b in range(2)]
                s_rb = [sb(f"p3b_rb{b}", [128, QW]) for b in range(2)]
                s_aob = [sb(f"p3b_aob{b}", [128, QW], BF16) for b in range(2)]
                s_sc = sb("p3b_sc", [1, 16])
                p_s = [ps(f"p3b_s{b}", [128, QW]) for b in range(3)]
                p_o = ps("p3b_o", [128, QW])
                p_row = ps("p3b_row", [1, QW])

                ST("scalar", lambda e: e.activation(s_sc[0:1, 0:4], s_stats[0:1, 0:4], AF.Sqrt))
                ST("scalar", lambda e: e.activation(s_stats[0:1, 20:36],
                                                    s_stats[0:1, 4:20], AF.Sqrt))
                for h in range(NH):
                    ST("vector", lambda e, h=h: e.tensor_tensor(
                        s_stats[0:1, 40 + h:41 + h], s_stats[0:1, 20 + h:21 + h],
                        s_sc[0:1, h // 4:h // 4 + 1], ALU.mult))
                ST("vector", lambda e: e.tensor_scalar(
                    s_stats[0:1, 40:56], s_stats[0:1, 40:56], -SCALE0, None, ALU.mult))
                ST("tensor", lambda e: e.matmul(p_o[:, 0:16], s_ones[0:1, :],
                                                s_stats[0:1, 40:56], start=True, stop=True))
                ST("scalar", lambda e: e.activation(s_nbias[:], p_o[:, 0:16], AF.Identity))
                pr.flush()

                entry = pr.k
                msc, mA, mexp = {}, {}, {}
                mkv, mq, mstr, mldr, mrec, mmul, mst = ({}, {}, {}, {}, {}, {}, {})

                def kv_load(gi):
                    g = gi % 4
                    buf = gi % 2
                    w = [("TICK", entry)] if gi < 2 else [("sMM", mA[(gi - 2) * 4 * 32 + 4 * 32 - 1])]
                    pr.raw("sync", lambda e, g=g, buf=buf: e.dma_start(
                        out=s_kb[buf][:], in_=kth_d[g].rearrange("p j f -> p (j f)")),
                        waits=w, inc="sD1")
                    mkv[gi] = pr.raw("sync", lambda e, g=g, buf=buf: e.dma_start(
                        out=s_vb[buf][:], in_=vta_d[g].rearrange("p j f -> p (j f)")),
                        waits=w, inc="sD1")

                def q_load(t):
                    qb, h = t // NH, t % NH
                    buf = t % 4
                    w = ([("TICK", entry)] if t < 4
                         else [("sMM", msc[(t - 4) * 32 + 31])])
                    mq[t] = pr.raw("sync", lambda e, h=h, qb=qb, buf=buf: e.dma_start(
                        out=s_qh[buf][:],
                        in_=qfh_d[h * 128:(h + 1) * 128, qb * QW:(qb + 1) * QW]),
                        waits=w, inc="sD0")

                NT = (S // QW) * NH
                kv_load(0); kv_load(1)
                q_load(0); q_load(1); q_load(2); q_load(3)

                def pe_score(t, j):
                    J = t * 32 + j
                    gi = t // 4
                    w = []
                    if J >= 2 and J % 2 == 0:
                        w.append(("sE2", mexp[J - 2]))
                    if j == 0:
                        w += [("sD0", mq[t]), ("sD1", mkv[gi])]
                    msc[J] = pr.raw("tensor", lambda e, j=j, t=t, gi=gi: e.matmul(
                        p_s[J % 3][:], s_kb[gi % 2][:, j * 128:(j + 1) * 128],
                        s_qh[t % 4][:], start=True, stop=True), waits=w, inc="sMM")

                def pe_denom_av(t, j):
                    J = t * 32 + j
                    gi = t // 4
                    w = [("sE2", mexp[J])]
                    if j == 0 and t >= 1:
                        w.append(("sGB", mmul[t - 1]))
                    pr.raw("tensor", lambda e, j=j, t=t: e.matmul(
                        p_row[0:1, :], s_onesb[:, 0:1], s_e[J % 2][:],
                        start=(j == 0), stop=(j == 31)), waits=w)
                    mA[J] = pr.raw("tensor", lambda e, j=j, t=t, gi=gi: e.matmul(
                        p_o[:], s_vb[gi % 2][:, j * 136:j * 136 + 128],
                        s_e[J % 2][:], start=(j == 0), stop=(j == 31)), inc="sMM")

                def sc_exp(t, j):
                    J = t * 32 + j
                    h = t % NH
                    mexp[J] = pr.raw("scalar", lambda e, J=J, h=h: e.activation(
                        s_e[J % 2][:], p_s[J % 3][:], AF.Exp,
                        bias=s_nbias[:, h:h + 1], scale=SCALE0),
                        waits=[("sMM", msc[J])], inc="sE2")

                def dve_recip(t):
                    w = [("sMM", mA[t * 32 + 31])]
                    if t >= 2:
                        w.append(("sLD", mstr[t - 2]))
                    mrec[t] = pr.raw("vector", lambda e, t=t: e.reciprocal(
                        s_r[t % 2][0:1, :], p_row[0:1, :]), waits=w, inc="sGB")

                def r_store(t):
                    w = [("sGB", mrec[t])]
                    if t >= 2:
                        w.append(("sD0", mldr[t - 2]))
                    mstr[t] = pr.raw("gpsimd", lambda e, t=t: e.dma_start(
                        out=rbq_d[t % 2:t % 2 + 1, :], in_=s_r[t % 2][0:1, :]),
                        waits=w, inc="sLD")

                def r_bcast(t):
                    w = [("sLD", mstr[t])]
                    if t >= 2:
                        w.append(("sGB", mmul[t - 2]))
                    mldr[t] = pr.raw("sync", lambda e, t=t: e.dma_start(
                        out=s_rb[t % 2][:],
                        in_=rbq_d[t % 2:t % 2 + 1, :].partition_broadcast(128).squeeze(1)),
                        waits=w, inc="sD0")

                def dve_mult(t):
                    w = [("sD0", mldr[t])]
                    if t >= 2:
                        w.append(("sLD", mst[t - 2]))
                    mmul[t] = pr.raw("vector", lambda e, t=t: e.tensor_tensor(
                        s_aob[t % 2][:], p_o[:], s_rb[t % 2][:], ALU.mult),
                        waits=w, inc="sGB")

                def ao_store(t):
                    qb, h = t // NH, t % NH
                    mst[t] = pr.raw("gpsimd", lambda e, h=h, qb=qb, t=t: e.dma_start(
                        out=aof_d[h * 128:(h + 1) * 128, qb * QW:(qb + 1) * QW],
                        in_=s_aob[t % 2][:]),
                        waits=[("sGB", mmul[t])], inc="sLD")

                for t in range(NT):
                    pe_score(t, 0); sc_exp(t, 0)
                    pe_score(t, 1); sc_exp(t, 1)
                    if t >= 1:
                        dve_mult(t - 1)
                        ao_store(t - 1)
                    for j in range(30):
                        pe_denom_av(t, j)
                        pe_score(t, j + 2)
                        sc_exp(t, j + 2)
                    pe_denom_av(t, 30)
                    pe_denom_av(t, 31)
                    dve_recip(t)
                    r_store(t)
                    r_bcast(t)
                    if t + 4 < NT:
                        q_load(t + 4)
                    if t + 2 < NT and (t + 2) % 4 == 0 and (t + 2) // 4 >= 2:
                        kv_load((t + 2) // 4)
                    if t % 2 == 0:
                        bg_pump(1)
                dve_mult(NT - 1)
                ao_store(NT - 1)
                pr.barrier(("sMM", "sE2", "sGB", "sD0", "sD1", "sLD"))
                pr.flush()

        def phase_p3c():
            TW = 512
            with ExitStack() as es:
                sb = lambda n, s, dt=F32: es.enter_context(nc.sbuf_tensor(n, s, dt))
                ps = lambda n, s: es.enter_context(nc.psum_tensor(n, s, F32))
                s_ao = [sb(f"p3c_ao{c}", [128, TW], BF16) for c in range(NC_)]
                s_wst = sb("p3c_wst", [128, 16 * 128])
                s_wob = sb("p3c_wob", [128, 16 * 16 * 128], BF16)
                s_s = sb("p3c_s", [128, TW])
                p_s = ps("p3c_p", [128, TW])
                for oc in range(NC_):
                    ST("sync", lambda e, oc=oc: e.dma_start(out=s_wst[:], in_=woT_d[oc]))
                    ST("vector", lambda e, oc=oc: e.tensor_copy(
                        s_wob[:, oc * 2048:(oc + 1) * 2048], s_wst[:]))
                for tb in range(S // TW):
                    t0 = tb * TW
                    for c in range(NC_):
                        ST("sync", lambda e, c=c, t0=t0: e.dma_start(
                            out=s_ao[c][:], in_=aof_d[c * 128:(c + 1) * 128, t0:t0 + TW]))
                    for oc in range(NC_):
                        for ic in range(NC_):
                            ST("tensor", lambda e, ic=ic, oc=oc: e.matmul(
                                p_s[:], s_wob[:, (oc * 16 + ic) * 128:(oc * 16 + ic + 1) * 128],
                                s_ao[ic][:], start=(ic == 0), stop=(ic == NC_ - 1)))
                        ST("vector", lambda e: e.tensor_copy(s_s[:], p_s[:]))
                        ST("gpsimd", lambda e, oc=oc, t0=t0: e.dma_start(
                            out=sf_d[oc * 128:(oc + 1) * 128, t0:t0 + TW], in_=s_s[:]))
                        bg_pump(1)
                pr.flush()

        def phase_p4():
            TW = 1024
            NSUB = TW // 128
            with ExitStack() as es:
                sb = lambda n, s, dt=F32: es.enter_context(nc.sbuf_tensor(n, s, dt))
                ps = lambda n, s: es.enter_context(nc.psum_tensor(n, s, F32))
                s_xh = [sb(f"p4_xh{c}", [128, TW], BF16) for c in range(NC_)]
                s_cb = [sb(f"p4_cb{c}", [128, TW]) for c in range(NC_)]
                s_wh = [sb(f"p4_wh{b}", [128, 16 * 128], BF16) for b in range(2)]
                s_xld = [sb(f"p4_xld{b}", [128, TW]) for b in range(4)]
                s_sld = [sb(f"p4_sld{b}", [128, TW]) for b in range(4)]
                s_yt = [sb(f"p4_yt{b}", [128, TW]) for b in range(2)]
                s_rw = sb("p4_rw", [128, NC_ * EXP])
                s_rwh = sb("p4_rwh", [128, NC_ * EXP], BF16)
                s_rbb = sb("p4_rbb", [128, EXP])
                s_ebT = sb("p4_ebT", [128, 16 * 8])
                s_obT = sb("p4_obT", [128, 16])
                s_eb8 = sb("p4_eb8", [8, DEXP])
                s_ob1 = sb("p4_ob1", [1, D])
                s_lg = sb("p4_lg", [128, 8])
                s_m = sb("p4_m", [128, 8])
                s_gt = sb("p4_gt", [128, NSUB * 8])
                s_gf = sb("p4_gf", [8, TW])
                s_gbc = sb("p4_gbc", [128, TW])
                s_z = [sb(f"p4_z{b}", [128, TW]) for b in range(2)]
                s_t1 = [sb(f"p4_t1{b}", [128, TW]) for b in range(2)]
                s_t2 = [sb(f"p4_t2{b}", [128, TW]) for b in range(2)]
                p_sm = ps("p4_sm", [128, 512])
                p_l = p_sm[:, 0:8]
                p_tr = p_sm[:, 128:256]
                p_e4 = [ps(f"p4_e4{b}", [128, TW]) for b in range(3)]
                gbn, bdn, mmn, e2n = {}, {}, {}, {}

                ST("sync", lambda e: e.dma_start(out=s_rw[:], in_=rw_d))
                ST("vector", lambda e: e.tensor_copy(s_rwh[:], s_rw[:]))
                ST("sync", lambda e: e.dma_start(out=s_lg[0:1, :], in_=rb_d.unsqueeze(0)))
                bcast_row(p_sm, s_lg[0:1, 0:8], 8)
                ST("scalar", lambda e: e.activation(s_rbb[:], p_l, AF.Identity))
                ST("sync", lambda e: e.dma_start(out=s_eb8[:], in_=eb_d))
                ST("sync", lambda e: e.dma_start(out=s_ob1[:], in_=ob_d.unsqueeze(0)))
                for dc in range(16):
                    ST("tensor", lambda e, dc=dc: e.transpose(
                        p_tr[:, 0:8], s_eb8[:, dc * 128:(dc + 1) * 128], s_ident[0:8, 0:8]))
                    ST("vector", lambda e, dc=dc: e.tensor_copy(
                        s_ebT[:, dc * 8:(dc + 1) * 8], p_tr[:, 0:8]))
                    ST("tensor", lambda e, dc=dc: e.transpose(
                        p_tr[:, 0:1], s_ob1[:, dc * 128:(dc + 1) * 128], s_ident[0:1, 0:1]))
                    ST("vector", lambda e, dc=dc: e.tensor_copy(
                        s_obT[:, dc:dc + 1], p_tr[:, 0:1]))

                for tb in range(S // TW):
                    t0 = tb * TW
                    for c in range(NC_):
                        ST("sync", lambda e, c=c, t0=t0: e.dma_start(
                            out=s_xh[c][:], in_=xnh_d[c * 128:(c + 1) * 128, t0:t0 + TW]))
                    for sub in range(NSUB):
                        for ic in range(NC_):
                            ST("tensor", lambda e, ic=ic, sub=sub: e.matmul(
                                p_l[:], s_xh[ic][:, sub * 128:(sub + 1) * 128],
                                s_rwh[:, ic * 8:(ic + 1) * 8],
                                start=(ic == 0),
                                stop=(ic == NC_ - 1)))
                        ST("vector", lambda e: e.tensor_tensor(s_lg[:], p_l[:], s_rbb[:], ALU.add))
                        ST("vector", lambda e: e.reduce_max(s_m[:, 0:1], s_lg[:], axis=AX.X))
                        ST("vector", lambda e, sub=sub: e.tensor_scalar(
                            s_gt[:, sub * 8:(sub + 1) * 8], s_lg[:], s_m[:, 0:1],
                            None, ALU.is_equal))
                        ST("vector", lambda e, sub=sub: e.scalar_tensor_tensor(
                            s_lg[:], s_gt[:, sub * 8:(sub + 1) * 8], -1e30, s_lg[:],
                            ALU.mult, ALU.add))
                        ST("vector", lambda e: e.reduce_max(s_m[:, 1:2], s_lg[:], axis=AX.X))
                        ST("vector", lambda e: e.tensor_scalar(
                            s_m[:, 4:5], s_m[:, 1:2], -1.0, None, ALU.mult))
                        ST("scalar", lambda e: e.activation(
                            s_m[:, 2:3], s_m[:, 0:1], AF.Sigmoid, bias=s_m[:, 4:5], scale=1.0))
                        ST("vector", lambda e: e.tensor_scalar(
                            s_m[:, 5:6], s_m[:, 0:1], -1.0, None, ALU.mult))
                        ST("scalar", lambda e: e.activation(
                            s_m[:, 3:4], s_m[:, 1:2], AF.Sigmoid, bias=s_m[:, 5:6], scale=1.0))
                        ST("vector", lambda e, sub=sub: e.tensor_scalar(
                            s_gt[:, sub * 8:(sub + 1) * 8], s_gt[:, sub * 8:(sub + 1) * 8],
                            s_m[:, 2:3], None, ALU.mult))
                        ST("vector", lambda e: e.tensor_scalar(
                            s_lg[:], s_lg[:], s_m[:, 1:2], None, ALU.is_equal))
                        ST("vector", lambda e: e.tensor_scalar(
                            s_lg[:], s_lg[:], s_m[:, 3:4], None, ALU.mult))
                        ST("vector", lambda e, sub=sub: e.tensor_add(
                            s_gt[:, sub * 8:(sub + 1) * 8], s_gt[:, sub * 8:(sub + 1) * 8],
                            s_lg[:]))
                        ST("tensor", lambda e, sub=sub: e.transpose(
                            p_tr[0:8, 0:128], s_gt[:, sub * 8:(sub + 1) * 8], s_ident[:]))
                        ST("vector", lambda e, sub=sub: e.tensor_copy(
                            s_gf[0:8, sub * 128:(sub + 1) * 128], p_tr[0:8, 0:128]))
                    ST("gpsimd", lambda e: e.dma_start(out=gb_d, in_=s_gf[:]))

                    entry = pr.k
                    gbn[0] = pr.raw("sync", lambda e: e.dma_start(
                        out=s_gbc[:],
                        in_=gb_d[0:1, :].partition_broadcast(128).squeeze(1)),
                        waits=[("TICK", entry)], inc="sGB")

                    groups = ([("exp", ex, dc) for ex in range(EXP) for dc in range(16)]
                              + [("out", 0, oc) for oc in range(16)])
                    NG = len(groups)
                    mxs, msty = {}, {}

                    def emit_band(li):
                        gg = tb * NG + li
                        kind, ex, dc = groups[li]
                        buf = gg % 2
                        src = ewh_d[ex, dc] if kind == "exp" else owh_d[dc]
                        w = ([("sMM", mmn[gg - 2])] if (gg - 2) in mmn
                             else [("TICK", entry)])
                        bdn[gg] = pr.raw(
                            "sync" if buf == 0 else "gpsimd",
                            lambda e, src=src, buf=buf: e.dma_start(
                                out=s_wh[buf][:], in_=src),
                            waits=w, inc="sD%d" % buf)

                    def emit_xs(oc):
                        b = oc % 4
                        w = [("TICK", entry)] if oc < 4 else [("sLD", msty[oc - 4])]
                        pr.raw("sync", lambda e, oc=oc, b=b, t0=t0: e.dma_start(
                            out=s_xld[b][:], in_=xt_d[oc * 128:(oc + 1) * 128, t0:t0 + TW]),
                            waits=w, inc="sLD")
                        mxs[oc] = pr.raw("sync", lambda e, oc=oc, b=b, t0=t0: e.dma_start(
                            out=s_sld[b][:], in_=sf_d[oc * 128:(oc + 1) * 128, t0:t0 + TW]),
                            waits=w, inc="sLD")

                    def EL(ch, eng, fn, extra=()):
                        sm = "sE2" if ch == 0 else "sPB"
                        pr.raw(eng, fn,
                               waits=[(sm, pr.cnt[sm])] + list(extra), inc=sm)

                    emit_band(0)
                    emit_band(1)
                    for li, (kind, ex, dc) in enumerate(groups):
                        gg = tb * NG + li
                        buf = gg % 2
                        pbuf = gg % 3
                        w = [("TICK", entry), ("sD%d" % buf, bdn[gg])]
                        ch = li % 2
                        sm = "sE2" if ch == 0 else "sPB"
                        if (gg - 3) in e2n:
                            w.append(e2n[gg - 3])
                        if kind == "out" and dc == 0:
                            w.append(e2n[gg - 1])
                        if kind == "out" and dc == 1:
                            w.append(e2n[gg - 2])
                        first = True
                        for hh in range(2):
                            for ic in range(NC_):
                                inc = "sMM" if (hh == 1 and ic == NC_ - 1) else None
                                pr.raw("tensor", lambda e, hh=hh, buf=buf,
                                       pbuf=pbuf, ic=ic: e.matmul(
                                    p_e4[pbuf][:, hh * 512:(hh + 1) * 512],
                                    s_wh[buf][:, ic * 128:(ic + 1) * 128],
                                    s_xh[ic][:, hh * 512:(hh + 1) * 512],
                                    start=(ic == 0), stop=(ic == NC_ - 1)),
                                    waits=w if first else (), inc=inc)
                                first = False
                        mmn[gg] = pr.cnt["sMM"]
                        if li + 2 < NG:
                            emit_band(li + 2)
                        if kind == "exp":
                            for hh in range(2):
                                EL(ch, "scalar", lambda e, hh=hh, dc=dc, ex=ex,
                                   pbuf=pbuf, ch=ch: e.activation(
                                       s_z[ch][:, hh * 512:(hh + 1) * 512],
                                       p_e4[pbuf][:, hh * 512:(hh + 1) * 512], AF.Identity,
                                       bias=s_ebT[:, dc * 8 + ex:dc * 8 + ex + 1]),
                                   extra=[("sMM", mmn[gg])] if hh == 0 else ())
                            EL(ch, "scalar", lambda e, ch=ch: e.activation(
                                s_t1[ch][:], s_z[ch][:], AF.Square))
                            EL(ch, "vector", lambda e, ch=ch: e.tensor_scalar(
                                s_t1[ch][:], s_t1[ch][:], GELU_A, 1.0, ALU.mult, ALU.add))
                            EL(ch, "vector", lambda e, ch=ch: e.tensor_tensor(
                                s_t1[ch][:], s_t1[ch][:], s_z[ch][:], ALU.mult))
                            EL(ch, "scalar", lambda e, ch=ch: e.activation(
                                s_t1[ch][:], s_t1[ch][:], AF.Sigmoid, scale=GELU_C2))
                            EL(ch, "vector", lambda e, ch=ch: e.tensor_tensor(
                                s_t1[ch][:], s_t1[ch][:], s_z[ch][:], ALU.mult))
                            if ex == 0:
                                EL(ch, "vector", lambda e, dc=dc, ch=ch: e.tensor_tensor(
                                    s_cb[dc][:], s_t1[ch][:], s_gbc[:], ALU.mult),
                                   extra=[("sGB", gbn[ex])] if dc < 2 else ())
                            else:
                                EL(ch, "vector", lambda e, ch=ch: e.tensor_tensor(
                                    s_t2[ch][:], s_t1[ch][:], s_gbc[:], ALU.mult),
                                   extra=[("sGB", gbn[ex])] if dc < 2 else ())
                                EL(ch, "vector", lambda e, dc=dc, ch=ch: e.tensor_add(
                                    s_cb[dc][:], s_cb[dc][:], s_t2[ch][:]))
                            if dc == 15:
                                if ex + 1 < EXP:
                                    gbn[ex + 1] = pr.raw(
                                        "sync", lambda e, ex=ex: e.dma_start(
                                            out=s_gbc[:],
                                            in_=gb_d[ex + 1:ex + 2, :]
                                            .partition_broadcast(128).squeeze(1)),
                                        waits=[("sE2", pr.cnt["sE2"]),
                                               ("sPB", pr.cnt["sPB"])], inc="sGB")
                                else:
                                    for c in range(NC_):
                                        EL(ch, "vector", lambda e, c=c: e.tensor_copy(
                                            s_xh[c][:], s_cb[c][:]))
                        else:
                            oc = dc
                            if oc == 0:
                                for o4 in range(4):
                                    emit_xs(o4)
                            for hh in range(2):
                                EL(ch, "scalar", lambda e, hh=hh, oc=oc, pbuf=pbuf,
                                   ch=ch: e.activation(
                                    s_z[ch][:, hh * 512:(hh + 1) * 512],
                                    p_e4[pbuf][:, hh * 512:(hh + 1) * 512],
                                    AF.Silu, bias=s_obT[:, oc:oc + 1]),
                                   extra=[("sMM", mmn[gg])] if hh == 0 else ())
                            ex2 = [("sLD", mxs[oc])]
                            if oc >= 2:
                                ex2.append(("sLD", msty[oc - 2]))
                            EL(ch, "vector", lambda e, oc=oc, ch=ch: e.tensor_tensor(
                                s_yt[oc % 2][:], s_z[ch][:], s_sld[oc % 4][:], ALU.mult),
                               extra=ex2)
                            EL(ch, "vector", lambda e, oc=oc, ch=ch: e.tensor_add(
                                s_yt[oc % 2][:], s_yt[oc % 2][:], s_xld[oc % 4][:]))
                            msty[oc] = pr.raw(
                                "gpsimd", lambda e, oc=oc, t0=t0: e.dma_start(
                                    out=y1f_d[oc * 128:(oc + 1) * 128, t0:t0 + TW],
                                    in_=s_yt[oc % 2][:]),
                                waits=[(sm, pr.cnt[sm])], inc="sLD")
                            if oc + 4 < NC_:
                                emit_xs(oc + 4)
                        e2n[gg] = (sm, pr.cnt[sm])
                    pr.barrier(("sMM", "sE2", "sGB", "sD0", "sD1", "sLD",
                                "sPB"))
                pr.flush()

        def phase_p5():
            TW = 512
            with ExitStack() as es:
                sb = lambda n, s, dt=F32: es.enter_context(nc.sbuf_tensor(n, s, dt))
                ps = lambda n, s: es.enter_context(nc.psum_tensor(n, s, F32))
                s_xn = [sb(f"p5_xn{c}", [128, TW]) for c in range(NC_)]
                s_xh = [sb(f"p5_xh{c}", [128, TW], BF16) for c in range(NC_)]
                s_hh = [sb(f"p5_hh{o}", [128, TW], BF16) for o in range(NOC1)]
                s_kv = [sb(f"p5_kv{g}", [128, TW]) for g in range(4)]
                s_selv = [sb(f"p5_sv{g}", [128, TW]) for g in range(4)]
                s_wst = sb("p5_wst", [128, 16 * 128])
                s_wkvb = sb("p5_wkvb", [128, 4 * 16 * 128], BF16)
                s_q1 = [sb(f"p5_q1{b}", [128, 16 * 128], BF16) for b in range(2)]
                s_q2 = [sb(f"p5_q2{b}", [128, 16 * 128], BF16) for b in range(2)]
                s_q3 = [sb(f"p5_q3{b}", [128, NOC1 * 128], BF16) for b in range(2)]
                s_bkvT = sb("p5_bkvT", [128, 4])
                s_bkv1 = sb("p5_bkv1", [1, KVF])
                s_t1 = [sb(f"p5_t1{b}", [128, TW]) for b in range(2)]
                s_t3 = [sb(f"p5_t3{b}", [128, TW]) for b in range(2)]
                s_sq = sb("p5_sq", [128, TW])
                s_st = sb("p5_st", [1, 4 * TW + 8])
                p_a = [ps(f"p5_a{b}", [128, TW]) for b in range(2)]
                p_b2 = [ps(f"p5_b2{b}", [128, TW]) for b in range(2)]
                p_row = ps("p5_row", [1, TW])
                p_bc = ps("p5_bc", [128, TW])
                p_tr = ps("p5_tr", [128, 128])

                ST("sync", lambda e: e.dma_start(out=s_bkv1[:], in_=bkv_d.unsqueeze(0)))
                for g in range(4):
                    ST("tensor", lambda e, g=g: e.transpose(
                        p_tr[:, 0:1], s_bkv1[:, g * 128:(g + 1) * 128], s_ident[0:1, 0:1]))
                    ST("vector", lambda e, g=g: e.tensor_copy(s_bkvT[:, g:g + 1], p_tr[:, 0:1]))
                for g in range(4):
                    ST("sync", lambda e, g=g: e.dma_start(out=s_wst[:], in_=wkv_d[g]))
                    ST("vector", lambda e, g=g: e.tensor_copy(
                        s_wkvb[:, g * 2048:(g + 1) * 2048], s_wst[:]))

                for tb in range(S // TW):
                    t0 = tb * TW
                    for c in range(NC_):
                        ST("sync", lambda e, c=c, t0=t0: e.dma_start(
                            out=s_xn[c][:], in_=y1f_d[c * 128:(c + 1) * 128, t0:t0 + TW]))
                    emit_layernorm(s_xn, s_xn, p_row, p_bc, s_sq, s_st, TW)
                    for c in range(NC_):
                        ST("vector", lambda e, c=c: e.tensor_copy(s_xh[c][:], s_xn[c][:]))
                    for g in range(4):
                        for ic in range(NC_):
                            ST("tensor", lambda e, ic=ic, g=g: e.matmul(
                                p_a[0][:], s_wkvb[:, g * 2048 + ic * 128:g * 2048 + (ic + 1) * 128],
                                s_xh[ic][:], start=(ic == 0), stop=(ic == NC_ - 1)))
                        ST("scalar", lambda e, g=g: e.activation(
                            s_kv[g][:], p_a[0][:], AF.Identity, bias=s_bkvT[:, g:g + 1]))
                        ST("tensor", lambda e, g=g: e.matmul(
                            p_b2[0][:], s_selT[:], s_kv[g][:], start=True, stop=True))
                        ST("vector", lambda e, g=g: e.tensor_copy(s_selv[g][:], p_b2[0][:]))
                    for c in range(NC_):
                        ST("sync", lambda e, c=c, t0=t0: e.dma_start(
                            out=s_xn[c][:], in_=y1f_d[c * 128:(c + 1) * 128, t0:t0 + TW]))
                    pr.flush()

                    entry = pr.k
                    mld1, mld2, mld3a, mld3b = {}, {}, {}, {}
                    mmA, mmB, mm3, msig, msil, mmul5, madd, mst3 = ({}, {}, {},
                                                                    {}, {}, {}, {}, {})

                    def ld12(oc):
                        b = oc % 2
                        w1 = ([("TICK", entry)] if oc < 2 else [("sMM", mmA[oc - 2])])
                        w2 = ([("TICK", entry)] if oc < 2 else [("sMM", mmB[oc - 2])])
                        mld1[oc] = pr.raw("sync", lambda e, oc=oc, b=b: e.dma_start(
                            out=s_q1[b][:], in_=q1b_d[oc]), waits=w1, inc="sD0")
                        mld2[oc] = pr.raw("gpsimd", lambda e, oc=oc, b=b: e.dma_start(
                            out=s_q2[b][:], in_=q2b_d[oc]), waits=w2, inc="sD1")

                    def ld3(fc):
                        b = fc % 2
                        HC = (NOC1 // 2) * 128
                        w = ([("TICK", entry)] if fc < 2 else [("sMM", mm3[fc - 2])])
                        mld3a[fc] = pr.raw("sync", lambda e, fc=fc, b=b: e.dma_start(
                            out=s_q3[b][:, 0:HC], in_=q3b_d[fc, :, 0:HC]),
                            waits=w, inc="sD0")
                        mld3b[fc] = pr.raw("gpsimd", lambda e, fc=fc, b=b: e.dma_start(
                            out=s_q3[b][:, HC:2 * HC], in_=q3b_d[fc, :, HC:2 * HC]),
                            waits=w, inc="sD1")

                    def pe12(oc):
                        b = oc % 2
                        w = [("sD0", mld1[oc])]
                        if oc >= 2:
                            w += [("sE2", msig[oc - 2]), ("sGB", mmul5[oc - 2])]
                        if oc == 0:
                            w.append(("TICK", entry))
                        for ic in range(NC_):
                            mmA[oc] = pr.raw("tensor", lambda e, ic=ic, b=b: e.matmul(
                                p_a[b][:], s_q1[b][:, ic * 128:(ic + 1) * 128], s_xh[ic][:],
                                start=(ic == 0), stop=(ic == NC_ - 1)),
                                waits=w if ic == 0 else (),
                                inc="sMM" if ic == NC_ - 1 else None)
                        w2 = [("sD1", mld2[oc])]
                        for ic in range(NC_):
                            mmB[oc] = pr.raw("tensor", lambda e, ic=ic, b=b: e.matmul(
                                p_b2[b][:], s_q2[b][:, ic * 128:(ic + 1) * 128], s_xh[ic][:],
                                start=(ic == 0), stop=(ic == NC_ - 1)),
                                waits=w2 if ic == 0 else (),
                                inc="sMM" if ic == NC_ - 1 else None)

                    def tail12(oc):
                        b = oc % 2
                        w = [("sMM", mmA[oc])]
                        if oc >= 2:
                            w.append(("sGB", mmul5[oc - 2]))
                        msig[oc] = pr.raw("scalar", lambda e, b=b: e.activation(
                            s_t1[b][:], p_a[b][:], AF.Sigmoid), waits=w, inc="sE2")
                        mmul5[oc] = pr.raw("vector", lambda e, oc=oc, b=b: e.tensor_tensor(
                            s_hh[oc][:], s_t1[b][:], p_b2[b][:], ALU.mult),
                            waits=[("sE2", msig[oc]), ("sMM", mmB[oc])], inc="sGB")

                    def pe3(fc):
                        b = fc % 2
                        w = [("sD0", mld3a[fc]), ("sD1", mld3b[fc])]
                        if fc < 2:
                            w.append(("sGB", mmul5[NOC1 - 1]))
                        else:
                            w.append(("sE2", msil[fc - 2]))
                        for ic in range(NOC1):
                            mm3[fc] = pr.raw("tensor", lambda e, ic=ic, b=b: e.matmul(
                                p_a[b][:], s_q3[b][:, ic * 128:(ic + 1) * 128], s_hh[ic][:],
                                start=(ic == 0), stop=(ic == NOC1 - 1)),
                                waits=w if ic == 0 else (),
                                inc="sMM" if ic == NOC1 - 1 else None)

                    def tail3(fc):
                        b = fc % 2
                        w = [("sMM", mm3[fc])]
                        if fc >= 2:
                            w.append(("sGB", madd[fc - 2]))
                        msil[fc] = pr.raw("scalar", lambda e, b=b: e.activation(
                            s_t1[b][:], p_a[b][:], AF.Silu), waits=w, inc="sE2")
                        pr.raw("vector", lambda e, fc=fc, b=b: e.tensor_tensor(
                            s_t1[b][:], s_t1[b][:], s_selv[fc // 4][:], ALU.mult),
                            waits=[("sE2", msil[fc])], inc="sGB")
                        w3 = []
                        if fc >= 2:
                            w3.append(("sLD", mst3[fc - 2]))
                        madd[fc] = pr.raw("vector", lambda e, fc=fc, b=b: e.tensor_add(
                            s_t3[b][:], s_xn[fc][:], s_t1[b][:]), waits=w3, inc="sGB")
                        mst3[fc] = pr.raw("gpsimd", lambda e, fc=fc, b=b, t0=t0: e.dma_start(
                            out=out_d[fc * 128:(fc + 1) * 128, t0:t0 + TW], in_=s_t3[b][:]),
                            waits=[("sGB", madd[fc])], inc="sLD")

                    ld12(0); ld12(1)
                    for oc in range(NOC1):
                        pe12(oc)
                        if oc + 2 < NOC1:
                            ld12(oc + 2)
                        elif oc + 2 < NOC1 + 2:
                            ld3(oc + 2 - NOC1)
                        tail12(oc)
                    for fc in range(NC_):
                        pe3(fc)
                        if fc + 2 < NC_:
                            ld3(fc + 2)
                        tail3(fc)
                    pr.barrier(("sMM", "sE2", "sGB", "sD0", "sD1", "sLD"))
                pr.flush()

        with nc.named_scope("p2"):
            phase_p2()
        with nc.named_scope("p3a"):
            phase_p3a()
        with nc.named_scope("rg"):
            emit_rgam()
        with nc.named_scope("p3b"):
            phase_p3b()
        with nc.named_scope("p3c"):
            phase_p3c()
        with nc.named_scope("ptd"):
            bg_pump(len(BG))
            pr.barrier(("sPA", "sPB", "sPC"))
            pr.flush()
        es_pt.close()
        with nc.named_scope("p4"):
            phase_p4()
        with nc.named_scope("p5"):
            phase_p5()

        dump_srcs = {"s": sf_d, "y1": y1f_d}
        for name, ap in dump_aps.items():
            if name == "stats":
                ST("gpsimd", lambda e, ap=ap: e.dma_start(out=ap, in_=s_stats[0:1, :]))
            else:
                ST("gpsimd", lambda e, ap=ap, name=name: e.dma_start(
                    out=ap, in_=dump_srcs[name]))
        pr.flush()

        pr.barrier(list(sems))
        pr.flush()
        final = pr.k
        block.sync(lambda e: e.wait_ge(tick, 16 * final))
    return nc


def _tile_wT(w, noc, nic):
    return np.ascontiguousarray(
        w.reshape(noc, 128, nic, 128).transpose(0, 3, 2, 1).reshape(noc, 128, nic * 128))


def _tile_w(w, nic, noc):
    return np.ascontiguousarray(
        w.reshape(nic, 128, noc, 128).transpose(2, 1, 0, 3).reshape(noc, 128, nic * 128))


def kernel(L0_attn_x, L0_attn_Wq, L0_attn_Wk, L0_attn_Wv, L0_attn_Wo,
           L0_ffn_router_w, L0_ffn_router_b, L0_ffn_expert_w, L0_ffn_expert_b,
           L0_ffn_out_w, L0_ffn_out_b, L1_attn_Wq, L1_attn_bq, L1_attn_Wkv,
           L1_attn_bkv, L1_ffn_w1, L1_ffn_w2, L1_ffn_w3):
    nc = build_program()
    f32 = lambda a: np.asarray(a, dtype=np.float32)
    ew = f32(L0_ffn_expert_w)
    ew_t = np.stack([_tile_w(ew[e], 16, 16) for e in range(EXP)])
    rw = f32(L0_ffn_router_w)
    rw_t = np.ascontiguousarray(rw.reshape(16, 128, EXP).transpose(1, 0, 2).reshape(128, 16 * EXP))
    shared = {
        "wqT": _tile_wT(f32(L0_attn_Wq), 16, 16),
        "wkT": _tile_wT(f32(L0_attn_Wk), 4, 16),
        "wvT": _tile_wT(f32(L0_attn_Wv), 4, 16),
        "woT": _tile_wT(f32(L0_attn_Wo), 16, 16),
        "rw": rw_t, "rb": np.ascontiguousarray(f32(L0_ffn_router_b)),
        "ew": ew_t, "eb": np.ascontiguousarray(f32(L0_ffn_expert_b)),
        "ow": _tile_w(f32(L0_ffn_out_w), 16, 16),
        "ob": np.ascontiguousarray(f32(L0_ffn_out_b)),
        "wkv": _tile_w(f32(L1_attn_Wkv), 16, 4),
        "bkv": np.ascontiguousarray(f32(L1_attn_bkv)),
        "wq1": np.ascontiguousarray(f32(L1_attn_Wq)),
        "bq1": np.ascontiguousarray(f32(L1_attn_bq)),
        "w1T": _tile_wT(f32(L1_ffn_w1), NOC1, 16),
        "w2T": _tile_wT(f32(L1_ffn_w2), NOC1, 16),
        "w3T": _tile_wT(f32(L1_ffn_w3), 16, NOC1),
    }
    xs = f32(L0_attn_x)
    in_maps = [dict(shared, xt=np.ascontiguousarray(xs[i].T)) for i in range(8)]
    res = run_bass_kernel_spmd(nc, in_maps, core_ids=list(range(8)))
    return np.stack([np.ascontiguousarray(np.asarray(r["out"], dtype=np.float32).T)
                     for r in res.results])
